# version 01
# baseline (speedup 1.0000x reference)
import numpy as np
import ml_dtypes
import contextlib
import concourse.bass as bass
import concourse.mybir as mybir

F32 = mybir.dt.float32
BF16 = mybir.dt.bfloat16
AF = mybir.ActivationFunctionType
ALU = mybir.AluOpType
AX = mybir.AxisListType.X

D = 1024
CH = 512
S = 2048
NCH = S // CH
NSEQ = 2
H = 16
DH = 64
W = 256
NL = 6
HID = 2816
HT = HID // 128
NEG = -1.0e30

bf = ml_dtypes.bfloat16


def make_consts():
    c = {}
    c["ident_b"] = np.eye(128, dtype=bf)
    c["ident_f"] = np.eye(128, dtype=np.float32)
    R = np.zeros((128, 128), dtype=np.float64)
    for h0 in (0, 64):
        for m in range(32):
            R[h0 + m + 32, h0 + m] = -1.0
            R[h0 + m, h0 + m + 32] = 1.0
    c["rmat"] = R.astype(bf)
    lo = np.zeros((128, 8, 16), dtype=np.float64)
    for f in range(8):
        lo[0:64, f, 2 * f] = 1.0
        lo[64:128, f, 2 * f + 1] = 1.0
    c["l2ones"] = lo.reshape(128, 128).astype(bf)
    pat = np.zeros((16, 8, 128), dtype=np.float64)
    for f in range(8):
        pat[2 * f, f, 0:64] = 1.0
        pat[2 * f + 1, f, 64:128] = 1.0
    c["l2pat"] = pat.reshape(16, 1024).astype(bf)
    c["ones"] = np.ones((128, 128), dtype=bf)
    half = np.arange(0, DH, 2, dtype=np.float64)
    inv_freq = 1.0 / 10000.0 ** (half / DH)
    t = np.arange(S, dtype=np.float64)
    freqs = t[:, None] * inv_freq[None, :]
    freqs = np.concatenate([freqs, freqs], -1)
    base_scale = (half + 0.4 * DH) / (1.4 * DH)
    power = (t - S // 2) / (W // 2)
    scale = base_scale[None, :] ** power[:, None]
    scale = np.concatenate([scale, scale], -1)
    cos, sin = np.cos(freqs), np.sin(freqs)
    qk_scale = 8.0
    cs8 = (cos * scale * qk_scale).T
    ss8 = (sin * scale * qk_scale).T
    cik = (cos / scale).T
    sik = (sin / scale).T
    for name, tab in (("cs8", cs8), ("ss8", ss8), ("cik", cik), ("sik", sik)):
        c[name] = np.concatenate([tab, tab], 0).astype(np.float32)
    def mask_tile(i_base, j_base, first):
        m = np.zeros((128, 384), dtype=np.float64)
        for ii in range(128):
            i = i_base + ii
            for jj in range(384):
                j = j_base + jj
                valid = (i <= j <= i + W) and not (first and j < 256)
                if not valid:
                    m[ii, jj] = NEG
        return m.astype(bf)
    c["mask_n0"] = mask_tile(0, 0, False)
    c["mask_n1"] = mask_tile(128, 128, False)
    c["mask_f0"] = mask_tile(0, 0, True)
    c["mask_f1"] = mask_tile(128, 128, True)
    return c


def make_weights(iw):
    w = {}
    w["wqT"] = np.ascontiguousarray(iw["L0_attn_W_q"].T)
    w["wkT"] = np.ascontiguousarray(iw["L0_attn_W_k"].T)
    w["wvT"] = np.ascontiguousarray(iw["L0_attn_W_v"].T)
    w["woT"] = np.ascontiguousarray(iw["L0_attn_W_o"].T)
    w["wqkv"] = np.ascontiguousarray(iw["L0_ffn_Wqkv"])
    w["wo_i"] = np.ascontiguousarray(iw["L0_ffn_Wo"])
    W1 = iw["L0_ffn_W1"]
    w1p = np.zeros((NL, D, 2 * HID), dtype=np.float32)
    w1p[:, :, 0:2730] = W1[:, :, 0:2730]
    w1p[:, :, HID:HID + 2730] = W1[:, :, 2730:5460]
    w["w1"] = w1p
    b1 = iw["L0_ffn_b1"]
    b1p = np.zeros((NL, 2 * HID), dtype=np.float32)
    b1p[:, 0:2730] = b1[:, 0:2730]
    b1p[:, HID:HID + 2730] = b1[:, 2730:5460]
    w["b1"] = np.ascontiguousarray(b1p.reshape(NL, 2 * HT, 128).transpose(2, 0, 1).reshape(128, NL * 2 * HT))
    W2 = iw["L0_ffn_W2"]
    w2p = np.zeros((NL, HID, D), dtype=np.float32)
    w2p[:, 0:2730, :] = W2
    w["w2"] = w2p
    def pp(v):
        return np.ascontiguousarray(v.reshape(NL, 8, 128).transpose(2, 0, 1).reshape(128, NL * 8))
    w["b2"] = pp(iw["L0_ffn_b2"])
    w["lng"] = pp(iw["L0_ffn_ln_g"])
    w["lnb"] = pp(iw["L0_ffn_ln_b"])
    w["qs"] = np.ascontiguousarray(np.tile(iw["L0_ffn_q_scale"], (1, 2)).T)
    w["ks"] = np.ascontiguousarray(np.tile(iw["L0_ffn_k_scale"], (1, 2)).T)
    return {k: np.asarray(v, dtype=(bf if v.dtype == bf else np.float32)) for k, v in w.items()}


class Prog:
    ENGINES = ("sync", "tensor", "vector", "scalar", "gpsimd")

    def __init__(self, nc, es):
        self.nc = nc
        self.es = es
        self.q = {e: [] for e in self.ENGINES}
        self.semc = {}
        self.sems = {}

    def sem(self, name):
        h = self.es.enter_context(self.nc.semaphore(name))
        self.semc[name] = 0
        self.sems[name] = h
        return name

    def tick(self, name):
        return self.semc[name]

    DEFER = ("vector", "scalar")

    def set_spacer(self, eng, fn):
        self.spacer_fn = getattr(self, "spacer_fn", {})
        self.spacer_fn[eng] = fn

    def _emit_spacer(self, eng, carry):
        fn = self.spacer_fn[eng]

        def cb(e, fn=fn, carry=carry):
            ins = fn(e)
            for h, amt in carry:
                ins = ins.then_inc(h, amt)
        self.q[eng].append(cb)

    def op(self, eng, fn, waits=(), inc=None, amt=1):
        for s, v in waits:
            assert v <= self.semc[s], f"wait-ahead-of-inc on {s}: {v} > {self.semc[s]}"
        self.pending = getattr(self, "pending", {e: [] for e in self.ENGINES})
        waits_h = [(self.sems[s], v) for s, v in waits if v > 0]
        defer = eng in self.DEFER
        if defer and self.pending[eng] and waits_h:
            self._emit_spacer(eng, self.pending[eng])
            self.pending[eng] = []
        carry = list(self.pending[eng]) if defer else []
        if defer:
            self.pending[eng] = []
        cap_inc = None
        if inc is not None:
            self.semc[inc] += amt
            if not defer:
                cap_inc = (self.sems[inc], amt)

        def cb(e, fn=fn, waits_h=waits_h, cap_inc=cap_inc, carry=carry):
            for h, v in waits_h:
                e.wait_ge(h, v)
            ins = fn(e)
            for h, amt2 in carry:
                assert ins is not None, "carry needs a real instruction"
                ins = ins.then_inc(h, amt2)
            if cap_inc is not None:
                assert ins is not None, "inc requested but fn returned None"
                ins.then_inc(*cap_inc)
        self.q[eng].append(cb)
        if inc is not None and defer:
            self.pending[eng].append((self.sems[inc], amt))
        return self.semc[inc] if inc is not None else None

    def finalize(self, block):
        for eng in self.DEFER:
            if getattr(self, "pending", {}).get(eng):
                self._emit_spacer(eng, self.pending[eng])
                self.pending[eng] = []
        for name in self.ENGINES:
            cbs = self.q[name]
            if not cbs:
                continue
            def section(e, cbs=cbs):
                for cb in cbs:
                    cb(e)
            getattr(block, name)(section)


class Core:

    def __init__(self, cfg):
        self.cfg = cfg
        self.nc = bass.Bass("TRN2", target_bir_lowering=False)
        self.es = contextlib.ExitStack()
        self.taps = {}

    def declare_dram(self):
        nc = self.nc
        dr = {}
        dr["xin"] = nc.dram_tensor("xin", [NSEQ * S, D], F32, kind="ExternalInput").ap()
        dr["yout"] = nc.dram_tensor("yout", [NSEQ * S, D], F32, kind="ExternalOutput").ap()
        for n, shp in (("wqT", [D, D]), ("wkT", [D, D]), ("wvT", [D, D]), ("woT", [D, D])):
            dr[n] = nc.dram_tensor(n, shp, F32, kind="ExternalInput").ap()
        dr["wqkv"] = nc.dram_tensor("wqkv", [NL, D, 3072], F32, kind="ExternalInput").ap()
        dr["wo_i"] = nc.dram_tensor("wo_i", [NL, D, D], F32, kind="ExternalInput").ap()
        dr["w1"] = nc.dram_tensor("w1", [NL, D, 2 * HID], F32, kind="ExternalInput").ap()
        dr["w2"] = nc.dram_tensor("w2", [NL, HID, D], F32, kind="ExternalInput").ap()
        for n, cc in (("b1", NL * 2 * HT), ("b2", NL * 8), ("lng", NL * 8), ("lnb", NL * 8),
                      ("qs", NL), ("ks", NL)):
            dr[n] = nc.dram_tensor(n, [128, cc], F32, kind="ExternalInput").ap()
        for n in ("cs8", "ss8", "cik", "sik"):
            dr[n] = nc.dram_tensor(n, [128, S], F32, kind="ExternalInput").ap()
        dr["masks"] = nc.dram_tensor("masks", [128, 4 * 384], BF16, kind="ExternalInput").ap()
        dr["identb"] = nc.dram_tensor("identb", [128, 128], BF16, kind="ExternalInput").ap()
        dr["identf"] = nc.dram_tensor("identf", [128, 128], F32, kind="ExternalInput").ap()
        dr["rmat"] = nc.dram_tensor("rmat", [128, 128], BF16, kind="ExternalInput").ap()
        dr["l2ones"] = nc.dram_tensor("l2ones", [128, 128], BF16, kind="ExternalInput").ap()
        dr["l2pat"] = nc.dram_tensor("l2pat", [16, 1024], BF16, kind="ExternalInput").ap()
        dr["onesb"] = nc.dram_tensor("onesb", [128, 128], BF16, kind="ExternalInput").ap()
        dr["xnsp"] = nc.dram_tensor("xnsp", [NSEQ * NCH * 8 * 128, CH], BF16, kind="Internal").ap()
        dr["ysp"] = nc.dram_tensor("ysp", [NSEQ * NCH * 8 * 128, CH], F32, kind="Internal").ap()
        self.dr = dr

    def alloc(self):
        nc, es = self.nc, self.es
        sb = {}

        def t(name, shape, dt):
            sb[name] = es.enter_context(nc.sbuf_tensor("s_" + name, shape, dt))
            return sb[name]

        t("y", [128, 8 * CH], F32)
        t("kcar", [128, NL * 8 * 256], BF16)
        t("vcar", [128, NL * 2 * 1040], BF16)
        t("acar", [128, NL * 4], F32)
        t("fcar", [128, NL * 4], F32)
        for n in ("cs8", "ss8", "cik", "sik"):
            t(n + "b", [128, S], BF16)
        t("masks", [128, 4 * 384], BF16)
        t("identb", [128, 128], BF16)
        t("identf", [128, 128], F32)
        t("rmat", [128, 128], BF16)
        t("l2ones", [128, 128], BF16)
        t("l2pat", [16, 1024], BF16)
        t("onesb", [128, 128], BF16)
        t("consts", [128, 8], F32)
        t("b1s", [128, NL * 2 * HT], F32)
        t("b2s", [128, NL * 8], F32)
        t("lngs", [128, NL * 8], F32)
        t("lnbs", [128, NL * 8], F32)
        t("qss", [128, NL], F32)
        t("kss", [128, NL], F32)
        t("kv", [64, 1024], F32)
        t("kvb2", [128, 512], BF16)
        t("xt", [128, 2 * 1024], F32)
        t("regB", [128, 22 * CH // 2], F32)
        t("xs", [128, 8 * CH], BF16)
        t("xnb", [128, 8 * CH], BF16)
        t("qp", [128, 8 * CH], BF16)
        t("kp", [128, 8 * CH], BF16)
        t("sqbf", [128, 2 * CH], BF16)
        t("rot", [128, 4 * CH], BF16)
        t("vaug", [128, 4 * 1040], BF16)
        t("pbuf", [128, 4 * CH], BF16)
        t("ptsb", [128, 2 * 1024], BF16)
        t("wsg", [128, 2 * 1408], F32)
        t("wbf", [128, 2 * 1408], BF16)
        t("stats", [128, 32], F32)
        t("nmax", [128, 8], F32)
        t("small", [64, 2048], F32)
        sb["invt"] = sb["small"][0:16, 0:1024]
        sb["lnt"] = sb["small"][0:1, 0:2048]
        sb["rs"] = sb["small"][0:1, 0:512]
        sb["rb"] = sb["small"][0:64, 512:1024]
        sb["invtb"] = sb["small"].bitcast(BF16)[0:16, 2048:2560]
        t("otm", [128, 1024], F32)
        sb["gtm"] = sb["regB"]
        sb["gel"] = sb["pbuf"][:, 0:1024]
        sb["afm"] = sb["regB"].bitcast(BF16)
        sb["vfs"] = sb["xt"].bitcast(BF16)
        self.sb = sb

        ps = {}
        for n in ("A0", "A1", "B0", "B1", "C0", "C1"):
            ps[n] = es.enter_context(nc.psum_tensor(n, [128, 512], F32))
        ps["PT0"] = es.enter_context(nc.psum_tensor("PT0", [128, 1024], BF16))
        ps["PT1"] = es.enter_context(nc.psum_tensor("PT1", [128, 1024], BF16))
        self.ps = ps

        t("spc", [1, 8], F32)
        t("onesf", [1, 128], F32)
        t("rqs", [128, 128], BF16)
        self.P = Prog(nc, es)
        self.P.set_spacer("vector", lambda e: e.tensor_copy(sb["spc"][0:1, 0:1], sb["spc"][0:1, 1:2]))
        self.P.set_spacer("scalar", lambda e: e.copy(sb["spc"][0:1, 2:3], sb["spc"][0:1, 3:4]))
        for s in ("dwA", "dwB", "dxA", "dxB", "dco", "dspy", "dspx", "dgx", "dgy",
                  "dgo", "t", "v", "s"):
            self.P.sem(s)
        self.bankfree = {n: [] for n in ps}
        self.free = {n: [] for n in ("xnb", "y", "qp", "kp", "vaug", "afm", "xs", "regB",
                                     "qbf", "sqbf", "rot", "pbuf", "ptsb", "gel", "vfs",
                                     "xt", "gtm", "otm")}

    def T(self, fn, waits=(), inc=False):
        return self.P.op("tensor", fn, waits, "t" if inc else None)

    def V(self, fn, waits=(), inc=False):
        return self.P.op("vector", fn, waits, "v" if inc else None)

    def S(self, fn, waits=(), inc=False):
        return self.P.op("scalar", fn, waits, "s" if inc else None)

    def Sy(self, fn, waits=(), sem="dwA"):
        return self.P.op("sync", fn, waits, sem, amt=16)

    def G(self, fn, waits=(), sem="dgo"):
        return self.P.op("gpsimd", fn, waits, sem, amt=16)

    def tick(self, s):
        return self.P.tick(s)

    def Vsp(self):
        self.P.op("vector", lambda e: e.tensor_copy(
            self.sb["spc"][0:1, 4:5], self.sb["spc"][0:1, 5:6]))

    def Ssp(self):
        self.P.op("scalar", lambda e: e.copy(
            self.sb["spc"][0:1, 6:7], self.sb["spc"][0:1, 7:8]))

    def init_stream(self):
        self.wslot = 0
        self.wfree = [[], []]

    def stream_w(self, dram_ap3, ncols):
        sl = self.wslot
        self.wslot ^= 1
        sb = self.sb
        kt = ncols // 128
        wsem = ("dwA", "dwB")[sl]
        dw = self.Sy(lambda e, a=dram_ap3, sl=sl, kt=kt: e.dma_start(
            out=sb["wsg"][:, sl * 1408: sl * 1408 + kt * 128].rearrange("p (k m) -> p k m", k=kt),
            in_=a), waits=self.wfree[sl], sem=wsem)
        ct = self.S(lambda e, sl=sl, n=kt * 128: e.copy(
            sb["wbf"][:, sl * 1408: sl * 1408 + n], sb["wsg"][:, sl * 1408: sl * 1408 + n]),
            waits=[(wsem, dw)] + self.wfree[sl], inc=True)
        self.cur_wslot = sl
        return sl * 1408, ct

    def wdone(self, sl_base, t_tick):
        self.wfree[sl_base // 1408] = [("t", t_tick)]

    def prologue(self):
        sb, dr = self.sb, self.dr
        self.V(lambda e: e.memset(sb["onesf"][0:1, :], 1.0))
        self.V(lambda e: e.memset(sb["spc"][0:1, 0:2], 0.0))
        self.V(lambda e: e.memset(sb["spc"][0:1, 4:6], 0.0))
        self.S(lambda e: e.memzero(sb["spc"][0:1, 2:4]))
        self.S(lambda e: e.memzero(sb["spc"][0:1, 6:8]))
        for n, d in (("masks", "masks"), ("identb", "identb"), ("identf", "identf"),
                     ("rmat", "rmat"), ("l2ones", "l2ones"), ("l2pat", "l2pat"),
                     ("onesb", "onesb"), ("b1s", "b1"), ("b2s", "b2"), ("lngs", "lng"),
                     ("lnbs", "lnb"), ("qss", "qs"), ("kss", "ks")):
            self.Sy(lambda e, n=n, d=d: e.dma_start(out=sb[n][:], in_=dr[d][:, :]), sem="dco")
        for rnd, (na, nb) in enumerate((("cs8", "ss8"), ("cik", "sik"))):
            wv = [("v", self.tick("v"))] if rnd else []
            ta = self.Sy(lambda e, na=na, r=rnd: e.dma_start(out=sb["gtm"][:, 0:2048], in_=dr[na][:, :]),
                         waits=wv, sem="dxA")
            tb = self.Sy(lambda e, nb=nb: e.dma_start(out=sb["gtm"][:, 2048:4096], in_=dr[nb][:, :]),
                         waits=wv, sem="dxB")
            self.V(lambda e, na=na: e.tensor_copy(sb[na + "b"][:], sb["gtm"][:, 0:2048]),
                   waits=[("dxA", ta)], inc=True)
            self.V(lambda e, nb=nb: e.tensor_copy(sb[nb + "b"][:], sb["gtm"][:, 2048:4096]),
                   waits=[("dxB", tb)], inc=True)
        cvals = [(-1.0, 0), (1e-5, 1), (1.0 / 1024.0, 2), (1e-24, 3), (0.125, 4), (1.0, 5)]
        for v, i in cvals:
            self.V(lambda e, v=v, i=i: e.memset(sb["consts"][:, i:i + 1], v))
            self.nc.const_aps.register(F32, v, sb["consts"][:, i:i + 1])
        self.cap = {v: sb["consts"][:, i:i + 1] for v, i in cvals}
        self.V(lambda e: e.memset(sb["small"][:], 0.0))
        self.V(lambda e: e.memset(sb["kcar"][:], 0.0))
        self.V(lambda e: e.memset(sb["vcar"][:], 0.0))
        self.V(lambda e: e.memset(
            sb["vaug"][:].rearrange("p (t h c) -> p t h c", t=4, c=65)[:, :, :, 64:65], 1.0), inc=True)
        self.prol_v = self.tick("v")
        self.free["regB"] = [("v", self.prol_v)]
        pw = [("dco", self.tick("dco"))]
        for eng in ("tensor", "vector", "scalar", "gpsimd"):
            self.P.op(eng, lambda e: None, waits=pw)

    def seq_init(self, s):
        sb = self.sb
        self.V(lambda e: e.memset(sb["acar"][:], 0.0))
        self.V(lambda e: e.memset(sb["fcar"][:], 0.0))
        self.V(lambda e: e.memset(sb["kv"][:], 0.0), inc=True)
        self.seq_v = self.tick("v")

    def phase1(self, s, c):
        sb, dr, ps = self.sb, self.dr, self.ps
        parts = 9
        ntiles = 4
        base = s * S + c * CH
        st = sb["stats"]
        xn_done_s = None
        self.xt_free = getattr(self, "xt_free", [[], []])
        self.xnt_free = getattr(self, "xnt_free", [[], []])
        for t in range(ntiles):
            sl = t % 2
            xsem = ("dxA", "dxB")[sl]
            dx = self.Sy(lambda e, t=t, sl=sl: e.dma_start(
                out=sb["xt"][:, sl * 1024:(sl + 1) * 1024],
                in_=dr["xin"][base + t * 128: base + (t + 1) * 128, :]),
                waits=self.xt_free[sl] + ([("v", self.prol_v)] if (s == 0 and c == 0 and t == 0) else []),
                sem=xsem)
            xsl = sb["xt"][:, sl * 1024:(sl + 1) * 1024]
            NOSTATS = False
            if NOSTATS:
                ss_t = self.S(lambda e, xsl=xsl, t=t: e.copy(
                    sb["regB"][:, 2048:2176], xsl[:, 0:128]),
                    waits=[(xsem, dx)] + (self.free["regB"] if t == 0 else []), inc=True)
            else:
                ss_t = self.S(lambda e, xsl=xsl, t=t: e.activation(
                    sb["regB"][:, 2048:3072], xsl, AF.Square, accum_out=st[:, t:t + 1]),
                    waits=[(xsem, dx)] + (self.free["regB"] if t == 0 else []), inc=True)
            sm_t = self.V(lambda e, xsl=xsl, t=t: e.reduce_sum(st[:, 4 + t:5 + t], xsl, axis=AX),
                          waits=[(xsem, dx)], inc=True)
            mu_t = self.V(lambda e, t=t: e.tensor_scalar_mul(st[:, 8 + t:9 + t], st[:, 4 + t:5 + t], 1.0 / 1024.0),
                          waits=[("v", sm_t)], inc=True)
            m2_t = self.V(lambda e, t=t: e.tensor_scalar_mul(st[:, 12 + t:13 + t], st[:, t:t + 1], 1.0 / 1024.0),
                          waits=[("s", ss_t)], inc=True)
            mu2_t = self.V(lambda e, t=t: e.tensor_mul(st[:, 16 + t:17 + t], st[:, 8 + t:9 + t], st[:, 8 + t:9 + t]),
                           waits=[("v", mu_t)], inc=True)
            var_t = self.V(lambda e, t=t: e.tensor_sub(st[:, 20 + t:21 + t], st[:, 12 + t:13 + t],
                                                       st[:, 16 + t:17 + t]),
                           waits=[("v", mu2_t), ("v", m2_t)], inc=True)
            if NOSTATS:
                ln_t = self.S(lambda e, t=t: e.copy(st[:, 12 + t:13 + t], st[:, 20 + t:21 + t]),
                              waits=[("v", var_t)], inc=True)
                rstd_t = self.S(lambda e, t=t: e.copy(st[:, 24 + t:25 + t], st[:, 12 + t:13 + t]),
                                waits=[("s", ln_t)], inc=True)
            else:
                ln_t = self.S(lambda e, t=t: e.activation(st[:, 12 + t:13 + t], st[:, 20 + t:21 + t], AF.Ln,
                                                          bias=self.cap[1e-5], scale=1.0),
                              waits=[("v", var_t)], inc=True)
                rstd_t = self.S(lambda e, t=t: e.activation(st[:, 24 + t:25 + t], st[:, 12 + t:13 + t],
                                                            AF.Exp, scale=-0.5),
                                waits=[("s", ln_t)], inc=True)
            nmr_t = self.V(lambda e, t=t: e.scalar_tensor_tensor(
                st[:, 28 + t:29 + t], st[:, 8 + t:9 + t], -1.0, st[:, 24 + t:25 + t],
                op0=ALU.mult, op1=ALU.mult), waits=[("s", rstd_t)], inc=True)
            xsl2 = sb["regB"][:, (sl * 1024):(sl + 1) * 1024]
            xn_t = self.V(lambda e, xsl=xsl, xsl2=xsl2, t=t: e.tensor_scalar(
                xsl2, xsl, st[:, 24 + t:25 + t], st[:, 28 + t:29 + t],
                op0=ALU.mult, op1=ALU.add),
                waits=[("v", nmr_t)] + self.xnt_free[sl], inc=True)
            self.xt_free[sl] = [("v", xn_t)]
            EV = "vs"
            if parts < 1:
                self.xnt_free[sl] = []
                last_evs = []
                continue
            evs = []
            for bi, bank in enumerate(("A0", "A1")):
                tt = None
                for d in range(4 * bi, 4 * bi + 4):
                    tt = self.T(lambda e, bank=bank, d=d, xsl2=xsl2: e.transpose(
                        ps[bank][:, (d % 4) * 128:((d % 4) + 1) * 128],
                        xsl2[:, d * 128:(d + 1) * 128], sb["identf"][:]),
                        waits=([("v", xn_t)] + self.bankfree[bank]) if d % 4 == 0 else (),
                        inc=(d % 4 == 3))
                tt = self.tick("t")
                import os as _os
                if EV == "none":
                    yv = self.V(lambda e: e.tensor_copy(sb["stats"][:, 5:6], sb["stats"][:, 6:7]),
                                waits=[("t", tt)], inc=True)
                    xv = self.S(lambda e: e.copy(sb["stats"][:, 6:7], sb["stats"][:, 7:8]),
                                waits=[("t", tt)], inc=True)
                elif EV == "v":
                    yv = self.V(lambda e, bank=bank, bi=bi, t=t: e.tensor_copy(
                        sb["y"][:].rearrange("p (d m) -> p d m", d=8)[:, 4 * bi:4 * bi + 4, t * 128:(t + 1) * 128],
                        ps[bank][:].rearrange("p (d m) -> p d m", d=4)),
                        waits=[("t", tt)] + (self.free["y"] if (t == 0 and bi == 0) else []), inc=True)
                    xv = self.S(lambda e: e.copy(sb["stats"][:, 6:7], sb["stats"][:, 7:8]),
                                waits=[("t", tt)], inc=True)
                else:
                    yv = self.V(lambda e, bank=bank, bi=bi, t=t: e.tensor_copy(
                        sb["y"][:].rearrange("p (d m) -> p d m", d=8)[:, 4 * bi:4 * bi + 4, t * 128:(t + 1) * 128],
                        ps[bank][:].rearrange("p (d m) -> p d m", d=4)),
                        waits=[("t", tt)] + (self.free["y"] if (t == 0 and bi == 0) else []), inc=True)
                    xv = None
                    for dd in range(4):
                        d2 = 4 * bi + dd
                        xv = self.S(lambda e, d2=d2, t=t: e.copy(
                            sb["xnb"][:, d2 * CH + t * 128: d2 * CH + (t + 1) * 128],
                            sb["y"][:, d2 * CH + t * 128: d2 * CH + (t + 1) * 128]),
                            waits=([("v", yv)] + (self.free["xnb"] if (t == 0 and bi == 0) else []))
                            if dd == 0 else (),
                            inc=(dd == 3))
                self.bankfree[bank] = [("v", yv), ("s", xv)]
                evs += [("v", yv), ("s", xv)]
            self.xnt_free[sl] = [("t", self.tick("t"))]
            last_evs = evs
        self.xn_ready = last_evs
        del last_evs
        if parts < 2:
            self.y_ready = self.xn_ready
            self.kv_tick = 0
            return
        self.y_ready = self.xn_ready
        if parts < 1:
            self.kv_tick = 0
            return
        gsp = self.G(lambda e, s=s, c=c: e.dma_start(
            out=dr["xnsp"].rearrange("(k d p) m -> k p d m", d=8, p=128)[s * NCH + c],
            in_=sb["xnb"][:].rearrange("p (d m) -> p d m", d=8)),
            waits=self.xn_ready, sem="dspx")
        self.xnb_spilled = gsp
        if parts < 3:
            self.kv_tick = 0
            return
        wk = dr["wkT"].rearrange("(t p) m -> p t m", p=128)
        wv = dr["wvT"].rearrange("(t p) m -> p t m", p=128)
        for wi, (wsrc, dst) in enumerate(((wk, "qp"), (wv, "kp"))):
            for blk in range(8):
                cb, ct = self.stream_w(wsrc[:, :, blk * 128:(blk + 1) * 128], 1024)
                bank = ("B0", "B1")[blk % 2]
                mmt = None
                for t in range(4):
                    for d in range(8):
                        mmt = self.T(lambda e, bank=bank, t=t, d=d, cb=cb: e.matmul(
                            ps[bank][:, t * 128:(t + 1) * 128],
                            sb["xnb"][:, d * 512 + t * 128: d * 512 + (t + 1) * 128],
                            sb["wbf"][:, cb + d * 128: cb + (d + 1) * 128],
                            start=(d == 0), stop=(d == 7)),
                            waits=([("s", ct)] + self.xn_ready + self.bankfree[bank]) if (t == 0 and d == 0) else (),
                            inc=(t == 3 and d == 7))
                mmt = self.tick("t")
                self.wdone(cb, mmt)
                ev = self.V(lambda e, dst=dst, blk=blk, bank=bank: e.tensor_copy(
                    sb[dst][:].rearrange("p (t m) -> p t m", t=4)[:, :, blk * 128:(blk + 1) * 128],
                    ps[bank][:].rearrange("p (t m) -> p t m", t=4)),
                    waits=[("t", mmt)] + (self.free["qp" if dst == "qp" else "kp"] if blk == 0 else []),
                    inc=True)
                self.bankfree[bank] = [("v", ev)]
        kvready = [("v", self.tick("v"))]
        self.free["xnb"] = [("t", self.tick("t"))]
        if parts < 4:
            self.kv_tick = 0
            return
        mmt = None
        for t in range(4):
            for h in range(H):
                bank = "C0" if h < 8 else "C1"
                mmt = self.T(lambda e, t=t, h=h, bank=bank: e.matmul(
                    ps[bank][0:64, (h % 8) * 64: (h % 8 + 1) * 64],
                    sb["qp"][:, t * 1024 + h * 64: t * 1024 + (h + 1) * 64],
                    sb["kp"][:, t * 1024 + h * 64: t * 1024 + (h + 1) * 64],
                    start=(t == 0 and h % 8 == 0), stop=(t == 3 and h % 8 == 7),
                    skip_group_check=True),
                    waits=(kvready + self.bankfree["C0"] + self.bankfree["C1"])
                          if (t == 0 and h == 0) else (),
                    inc=(t == 3 and h == H - 1))
        self.free["qp"] = [("t", mmt)]
        self.free["kp"] = [("t", mmt)]
        kvv0 = self.V(lambda e: e.tensor_tensor(sb["kv"][:, 0:512], sb["kv"][:, 0:512],
                                                ps["C0"][0:64, :], op=ALU.add),
                      waits=[("t", mmt), ("v", self.seq_v)], inc=True)
        kvv = self.V(lambda e: e.tensor_tensor(sb["kv"][:, 512:1024], sb["kv"][:, 512:1024],
                                               ps["C1"][0:64, :], op=ALU.add), inc=True)
        self.bankfree["C0"] = [("v", kvv0)]
        self.bankfree["C1"] = [("v", kvv)]
        self.kv_tick = kvv

    def attn(self, s, c, l):
        sb, dr, ps = self.sb, self.dr, self.ps
        P = self.P
        ccol = c * CH
        x0 = self.V(lambda e: e.tensor_copy(sb["xs"][:, 0:4 * CH], sb["y"][:, 0:4 * CH]),
                    waits=self.free["xs"] + self.y_state, inc=True)
        for d in range(4, 8):
            self.V(lambda e, d=d: e.tensor_copy(sb["xs"][:, d * CH + 1:(d + 1) * CH],
                                                sb["y"][:, d * CH:d * CH + CH - 1]))
        self.V(lambda e, l=l: e.tensor_copy(
            sb["xs"][:].rearrange("p (d m) -> p d m", d=8)[:, 4:8, 0:1],
            sb["acar"][:].rearrange("p (l d) -> p l d", l=NL)[:, l, :, None]))
        acv = self.V(lambda e, l=l: e.tensor_copy(
            sb["acar"][:].rearrange("p (l d) -> p l d", l=NL)[:, l, :],
            sb["y"][:].rearrange("p (d m) -> p d m", d=8)[:, 4:8, CH - 1]), inc=True)
        xs_done = [("v", acv)]
        self.free["xs"] = []

        wq3 = dr["wqkv"].rearrange("l (t p) m -> l p t m", p=128)[l]
        for which, (dst, ssbank, tabs, scl) in enumerate(
                (("qp", "C0", ("cs8b", "ss8b"), "qss"), ("kp", "C1", ("cik b", "sikb"), "kss"))):
            tabs = (tabs[0].replace(" ", ""), tabs[1])
            rqs_t = self.V(lambda e, scl=scl, l=l: e.tensor_scalar_mul(
                sb["rqs"][:], sb["rmat"][:], sb[scl][:, l:l + 1]),
                waits=[("t", self.tick("t"))], inc=True)
            asq_rd = [[], []]
            for dt in range(8):
                col0 = (which * 8 + dt) * 128
                cb, ct = self.stream_w(wq3[:, :, col0:col0 + 128], 1024)
                bank = ("A0", "A1")[dt % 2]
                for d in range(8):
                    mmt = self.T(lambda e, bank=bank, d=d, cb=cb: e.matmul(
                        ps[bank][:], sb["wbf"][:, cb + d * 128: cb + (d + 1) * 128],
                        sb["xs"][:, d * CH:(d + 1) * CH],
                        start=(d == 0), stop=(d == 7), skip_group_check=True),
                        waits=([("s", ct)] + xs_done + self.bankfree[bank] +
                               (self.free[dst] if dt == 0 else [])) if d == 0 else (),
                        inc=(d == 7))
                self.wdone(cb, mmt)
                ev = self.V(lambda e, dst=dst, dt=dt, bank=bank: e.tensor_copy(
                    sb[dst][:, dt * CH:(dt + 1) * CH], ps[bank][:]),
                    waits=[("t", mmt)], inc=True)
                self.bankfree[bank] = [("v", ev)]
                sq = self.S(lambda e, dst=dst, dt=dt: e.activation(
                    sb["sqbf"][:, (dt % 2) * CH:(dt % 2 + 1) * CH], sb[dst][:, dt * CH:(dt + 1) * CH],
                    AF.Square), waits=[("v", ev)] + asq_rd[dt % 2], inc=True)
                cst = self.T(lambda e, dt=dt, ssbank=ssbank: e.matmul(
                    ps[ssbank][0:16, :], sb["l2ones"][:, dt * 16:(dt + 1) * 16],
                    sb["sqbf"][:, (dt % 2) * CH:(dt % 2 + 1) * CH],
                    start=(dt == 0), stop=(dt == 7), skip_group_check=True),
                    waits=[("s", sq)] + (self.bankfree[ssbank] if dt == 0 else []), inc=True)
                asq_rd[dt % 2] = [("t", cst)]
            mx = self.V(lambda e, ssbank=ssbank: e.tensor_single_scalar(
                sb["invt"][:, 0:512], ps[ssbank][0:16, :], 1e-24, op=ALU.max),
                waits=[("t", cst)], inc=True)
            self.bankfree[ssbank] = [("v", mx)]
            lnt_ = self.S(lambda e: e.activation(sb["invt"][:, 512:1024], sb["invt"][:, 0:512], AF.Ln),
                          waits=[("v", mx)], inc=True)
            ivt = self.S(lambda e: e.activation(sb["invtb"][:, :], sb["invt"][:, 512:1024],
                                                AF.Exp, scale=-0.5), waits=[("s", lnt_)], inc=True)
            rq_rd = [list(asq_rd[0]), list(asq_rd[1])]
            for dt in range(8):
                rqb = ("B0", "B1")[dt % 2]
                ivb = ("C0", "C1")[dt % 2]
                rq = self.T(lambda e, rqb=rqb, dst=dst, dt=dt: e.matmul(
                    ps[rqb][:], sb["rqs"][:], sb[dst][:, dt * CH:(dt + 1) * CH],
                    start=True, stop=True),
                    waits=([("v", rqs_t)] if dt == 0 else []) + self.bankfree[rqb], inc=True)
                iv = self.T(lambda e, ivb=ivb, dt=dt: e.matmul(
                    ps[ivb][:], sb["l2pat"][:, dt * 128:(dt + 1) * 128], sb["invtb"][:, :],
                    start=True, stop=True),
                    waits=[("s", ivt)] + self.bankfree[ivb], inc=True)
                rqc = self.S(lambda e, rqb=rqb, dt=dt: e.copy(
                    sb["sqbf"][:, (dt % 2) * CH:(dt % 2 + 1) * CH], ps[rqb][:]),
                    waits=[("t", rq)] + rq_rd[dt % 2], inc=True)
                self.bankfree[rqb] = [("s", rqc)]
                self.V(lambda e, dst=dst, dt=dt, tabs=tabs, ccol=ccol: e.tensor_tensor(
                    sb["rot"][:, 0:CH], sb[dst][:, dt * CH:(dt + 1) * CH],
                    sb[tabs[0]][:, ccol:ccol + CH], op=ALU.mult))
                t1s = self.V(lambda e, scl=scl, l=l: e.tensor_scalar_mul(
                    sb["rot"][:, 3 * CH:4 * CH], sb["rot"][:, 0:CH], sb[scl][:, l:l + 1]),
                    inc=True)
                t2r = self.V(lambda e, dt=dt, tabs=tabs, ccol=ccol: e.tensor_tensor(
                    sb["rot"][:, CH:2 * CH], sb["sqbf"][:, (dt % 2) * CH:(dt % 2 + 1) * CH],
                    sb[tabs[1]][:, ccol:ccol + CH], op=ALU.mult),
                    waits=[("s", rqc)], inc=True)
                rq_rd[dt % 2] = [("v", t2r)]
                t3 = self.V(lambda e: e.tensor_add(sb["rot"][:, 2 * CH:3 * CH],
                                                   sb["rot"][:, 3 * CH:4 * CH],
                                                   sb["rot"][:, CH:2 * CH]),
                            waits=[("v", t1s)], inc=True)
                qf = self.V(lambda e, dst=dst, dt=dt, ivb=ivb: e.tensor_tensor(
                    sb[dst][:, dt * CH:(dt + 1) * CH], sb["rot"][:, 2 * CH:3 * CH],
                    ps[ivb][:], op=ALU.mult), waits=[("t", iv), ("v", t3)], inc=True)
                self.bankfree[ivb] = [("v", qf)]
            if which == 0:
                qdone = [("v", self.tick("v"))]
            else:
                kdone = [("v", self.tick("v"))]
        for blk in range(8):
            cb, ct = self.stream_w(wq3[:, :, 2048 + blk * 128:2048 + (blk + 1) * 128], 1024)
            bank = ("A0", "A1")[blk % 2]
            for d in range(8):
                mmt = self.T(lambda e, bank=bank, d=d, cb=cb: e.matmul(
                    ps[bank][:], sb["wbf"][:, cb + d * 128: cb + (d + 1) * 128],
                    sb["xs"][:, d * CH:(d + 1) * CH],
                    start=(d == 0), stop=(d == 7), skip_group_check=True),
                    waits=([("s", ct)] + self.bankfree[bank] +
                           (self.free["vfs"] + self.xt_free[0] + self.xt_free[1]
                            if blk == 0 else [])) if d == 0 else (),
                    inc=(d == 7))
            self.wdone(cb, mmt)
            ev = self.V(lambda e, blk=blk, bank=bank: e.tensor_copy(
                sb["vfs"][:, blk * CH:(blk + 1) * CH], ps[bank][:]),
                waits=[("t", mmt)], inc=True)
            self.bankfree[bank] = [("v", ev)]
        vfs_done = [("v", self.tick("v"))]
        for st in range(4):
            ptb = ("PT0", "PT1")[st % 2]
            for blk in range(8):
                tt = self.T(lambda e, ptb=ptb, st=st, blk=blk: e.transpose(
                    ps[ptb][:, blk * 128:(blk + 1) * 128],
                    sb["vfs"][:, blk * CH + st * 128: blk * CH + (st + 1) * 128],
                    sb["identb"][:]),
                    waits=(vfs_done + self.bankfree[ptb] + (self.free["vaug"] if (st == 0 and blk == 0) else [])) if blk == 0 else (),
                    inc=(blk == 7))
            ev = self.V(lambda e, ptb=ptb, st=st: e.tensor_copy(
                sb["vaug"][:, st * 1040:(st + 1) * 1040].rearrange("p (h q) -> p h q", q=65)[:, :, 0:64],
                ps[ptb][:].rearrange("p (h q) -> p h q", q=64)),
                waits=[("t", tt)], inc=True)
            self.bankfree[ptb] = [("v", ev)]
        vaug_done = [("v", self.tick("v"))]
        self.xt_free[0] = [("t", self.tick("t"))]
        self.xt_free[1] = [("t", self.tick("t"))]
        self.free["vfs"] = [("t", self.tick("t"))]
        return qdone, kdone, vaug_done

    def attn_core(self, s, c, l, qdone, kdone, vaug_done):
        sb, dr, ps = self.sb, self.dr, self.ps
        first = (c == 0)
        nmx_i = [0]
        pbuf_i = [0]
        pbuf_free = [[] for _ in range(4)]
        nmax_free = [[] for _ in range(8)]
        ptsb_free = [[], []]
        for w in (0, 1):
            for p in range(8):
                simb = {0: ("A0", "A1"), 1: ("B0", "B1")}[p % 2]
                ptb = ("PT0", "PT1")[p % 2]
                ptev = [None, None]
                for par in (0, 1):
                    bank = simb[par]
                    r0 = par * 64
                    for i in (0, 1):
                        segs = []
                        if w == 0:
                            kc = sb["kcar"][:].rearrange("p (L q m) -> p L q m", L=NL, q=8)
                            if i == 0:
                                segs.append((kc[r0:r0 + 64, l, p, 0:256], 0, 256))
                                segs.append((sb["kp"][r0:r0 + 64, p * CH: p * CH + 128], 256, 128))
                            else:
                                segs.append((kc[r0:r0 + 64, l, p, 128:256], 0, 128))
                                segs.append((sb["kp"][r0:r0 + 64, p * CH: p * CH + 256], 128, 256))
                        else:
                            j0 = i * 128
                            segs.append((sb["kp"][r0:r0 + 64, p * CH + j0: p * CH + j0 + 384], 0, 384))
                        lq = sb["qp"][r0:r0 + 64, p * CH + w * 256 + i * 128: p * CH + w * 256 + (i + 1) * 128]
                        w0waits = qdone + kdone + list(self.bankfree[bank])
                        mmt = None
                        for si, (rhs, o, n) in enumerate(segs):
                            mmt = self.T(lambda e, bank=bank, lq=lq, rhs=rhs, o=o, n=n, si=si: e.matmul(
                                ps[bank][:, o:o + n], lq, rhs, start=(si == 0), stop=False,
                                skip_group_check=True),
                                waits=w0waits if si == 0 else (), inc=False)
                        mi = (2 if (first and w == 0) else 0) + i
                        mmt = self.T(lambda e, bank=bank, mi=mi: e.matmul(
                            ps[bank][:, 0:384], sb["identb"][:], sb["masks"][:, mi * 384:(mi + 1) * 384],
                            start=False, stop=True, skip_group_check=True), inc=True)
                        nc_ = nmx_i[0] % 8
                        nmx_i[0] += 1
                        rmx = self.V(lambda e, bank=bank, nc_=nc_: e.reduce_max(
                            sb["nmax"][:, nc_:nc_ + 1], ps[bank][:, 0:384], axis=AX, negate=True),
                            waits=[("t", mmt)] + nmax_free[nc_], inc=True)
                        pb = pbuf_i[0] % 4
                        pbuf_i[0] += 1
                        ex = self.S(lambda e, bank=bank, nc_=nc_, pb=pb: e.activation(
                            sb["pbuf"][:, pb * CH: pb * CH + 384], ps[bank][:, 0:384], AF.Exp,
                            bias=sb["nmax"][:, nc_:nc_ + 1]),
                            waits=[("v", rmx)] + pbuf_free[pb], inc=True)
                        nmax_free[nc_] = [("s", ex)]
                        self.bankfree[bank] = [("s", ex)]
                        for jj in range(3):
                            jt = jj + i
                            tt = self.T(lambda e, ptb=ptb, pb=pb, jj=jj, jt=jt, i=i: e.transpose(
                                ps[ptb][:, jt * 256 + i * 128: jt * 256 + i * 128 + 128],
                                sb["pbuf"][:, pb * CH + jj * 128: pb * CH + (jj + 1) * 128],
                                sb["identb"][:]),
                                waits=([("s", ex)] + (self.bankfree[ptb] if (i == 0 and jj == 0) else [])),
                                inc=(jj == 2))
                        pbuf_free[pb] = [("t", tt)]
                    pev = self.V(lambda e, ptb=ptb, par=par: e.tensor_copy(
                        sb["ptsb"][:, par * 1024:(par + 1) * 1024], ps[ptb][:]),
                        waits=[("t", tt)] + ptsb_free[par], inc=True)
                    self.bankfree[ptb] = [("v", pev)]
                    ptev[par] = pev
                for par in (0, 1):
                    h = 2 * p + par
                    pvb = ("C0", "C1")[par]
                    vsrcs = []
                    for jt in range(4):
                        if w == 0 and jt < 2:
                            vc = sb["vcar"][:].rearrange("p (L q m) -> p L q m", L=NL, q=2)
                            vsrcs.append(vc[:, l, jt, h * 65:(h + 1) * 65])
                        else:
                            st = (jt - 2) if w == 0 else jt
                            vsrcs.append(sb["vaug"][:, st * 1040 + h * 65: st * 1040 + (h + 1) * 65])
                    mmt = None
                    order = (1, 2, 0, 3)
                    for oi, jt in enumerate(order):
                        ilo = 128 if jt == 3 else 0
                        ihi = 128 if jt == 0 else 256
                        mmt = self.T(lambda e, pvb=pvb, jt=jt, par=par, vs=vsrcs[jt], ilo=ilo, ihi=ihi,
                                     oi=oi: e.matmul(
                            ps[pvb][0:65, ilo:ihi], vs,
                            sb["ptsb"][:, par * 1024 + jt * 256 + ilo: par * 1024 + jt * 256 + ihi],
                            start=(oi == 0), stop=(oi == 3), skip_group_check=True),
                            waits=([("v", ptev[par])] + vaug_done + self.bankfree[pvb]) if oi == 0 else (),
                            inc=(oi == 3))
                    ptsb_free[par] = [("t", mmt)]
                    rssl = par
                    rc = self.V(lambda e, pvb=pvb, rssl=rssl: e.reciprocal(
                        sb["rs"][0:1, rssl * 256:(rssl + 1) * 256], ps[pvb][64:65, 0:256]),
                        waits=[("t", mmt)], inc=True)
                    rbt = self.T(lambda e, pvb=pvb, rssl=rssl: e.matmul(
                        ps[pvb][0:64, 256:512], sb["onesf"][0:1, 0:64],
                        sb["rs"][0:1, rssl * 256:(rssl + 1) * 256], start=True, stop=True,
                        skip_group_check=True),
                        waits=[("v", rc)], inc=True)
                    rbe = self.V(lambda e, pvb=pvb, rssl=rssl: e.tensor_copy(
                        sb["rb"][:, rssl * 256:(rssl + 1) * 256], ps[pvb][0:64, 256:512]),
                        waits=[("t", rbt)], inc=True)
                    at = self.V(lambda e, pvb=pvb, rssl=rssl, r0=par * 64, p=p, w=w: e.tensor_tensor(
                        sb["afm"][r0:r0 + 64, p * CH + w * 256: p * CH + (w + 1) * 256],
                        ps[pvb][0:64, 0:256], sb["rb"][:, rssl * 256:(rssl + 1) * 256], op=ALU.mult),
                        waits=[("v", rbe)] + (self.free["afm"] if (w == 0 and p == 0 and par == 0) else []),
                        inc=True)
                    self.bankfree[pvb] = [("v", at)]
            if w == 0:
                tnow = self.tick("t")
                kcv = self.V(lambda e, l=l: e.tensor_copy(
                    sb["kcar"][:].rearrange("p (L q m) -> p L q m", L=NL, q=8)[:, l],
                    sb["kp"][:].rearrange("p (q m) -> p q m", q=8)[:, :, 256:512]),
                    waits=[("t", tnow)], inc=True)
                vcv = self.V(lambda e, l=l: e.tensor_copy(
                    sb["vcar"][:].rearrange("p (L q m) -> p L q m", L=NL, q=2)[:, l],
                    sb["vaug"][:].rearrange("p (q m) -> p q m", q=4)[:, 2:4]),
                    waits=[("t", tnow)], inc=True)
        afm_done = [("v", self.tick("v"))]
        self.free["qp"] = [("t", self.tick("t"))]
        self.free["kp"] = [("t", self.tick("t"))]
        self.free["vaug"] = [("t", self.tick("t"))]
        self.free["xs"] = [("t", self.tick("t"))]
        wo3 = dr["wo_i"].rearrange("l (t p) m -> l p t m", p=128)[l]
        for dt in range(8):
            cb, ct = self.stream_w(wo3[:, :, dt * 128:(dt + 1) * 128], 1024)
            bank = ("A0", "A1")[dt % 2]
            for d in range(8):
                mmt = self.T(lambda e, bank=bank, d=d, cb=cb: e.matmul(
                    ps[bank][:], sb["wbf"][:, cb + d * 128: cb + (d + 1) * 128],
                    sb["afm"][:, d * CH:(d + 1) * CH],
                    start=(d == 0), stop=(d == 7), skip_group_check=True),
                    waits=([("s", ct)] + afm_done + self.bankfree[bank]) if d == 0 else (),
                    inc=(d == 7))
            self.wdone(cb, mmt)
            yv = self.V(lambda e, dt=dt, bank=bank: e.scalar_tensor_tensor(
                sb["y"][:, dt * CH:(dt + 1) * CH], ps[bank][:], 1.0,
                sb["y"][:, dt * CH:(dt + 1) * CH], op0=ALU.mult, op1=ALU.add),
                waits=[("t", mmt)], inc=True)
            self.bankfree[bank] = [("v", yv)]
        self.y_state = [("v", self.tick("v"))]
        self.free["afm"] = [("t", self.tick("t"))]
        self.free["regB"] = [("t", self.tick("t"))]

    def ffn(self, s, c, l):
        sb, dr, ps = self.sb, self.dr, self.ps
        x0 = self.V(lambda e: e.tensor_copy(sb["xs"][:, 0:4 * CH], sb["y"][:, 0:4 * CH]),
                    waits=self.free["xs"] + self.y_state, inc=True)
        for d in range(4, 8):
            self.V(lambda e, d=d: e.tensor_copy(sb["xs"][:, d * CH + 1:(d + 1) * CH],
                                                sb["y"][:, d * CH:d * CH + CH - 1]))
        xcp = self.V(lambda e, l=l: e.tensor_copy(
            sb["xs"][:].rearrange("p (d m) -> p d m", d=8)[:, 4:8, 0:1],
            sb["fcar"][:].rearrange("p (l d) -> p l d", l=NL)[:, l, :, None]), inc=True)
        xs_done = [("v", xcp)]
        cst = sst = None
        sq_rd = [[], []]
        SQV = False
        for d in range(8):
            if SQV:
                sq = self.V(lambda e, d=d: e.tensor_mul(
                    sb["sqbf"][:, (d % 2) * CH:(d % 2 + 1) * CH], sb["xs"][:, d * CH:(d + 1) * CH],
                    sb["xs"][:, d * CH:(d + 1) * CH]),
                    waits=(list(xs_done) if d == 0 else []) + sq_rd[d % 2], inc=True)
                sqw = [("v", sq)]
            else:
                sq = self.S(lambda e, d=d: e.activation(
                    sb["sqbf"][:, (d % 2) * CH:(d % 2 + 1) * CH], sb["xs"][:, d * CH:(d + 1) * CH],
                    AF.Square), waits=(list(xs_done) if d == 0 else []) + sq_rd[d % 2], inc=True)
                sqw = [("s", sq)]
            cst = self.T(lambda e, d=d: e.matmul(
                ps["C0"][0:1, :], sb["onesb"][:, 0:1], sb["xs"][:, d * CH:(d + 1) * CH],
                start=(d == 0), stop=(d == 7), skip_group_check=True),
                waits=(xs_done + self.bankfree["C0"]) if d == 0 else (), inc=True)
            sst = self.T(lambda e, d=d: e.matmul(
                ps["C1"][0:1, :], sb["onesb"][:, 0:1],
                sb["sqbf"][:, (d % 2) * CH:(d % 2 + 1) * CH],
                start=(d == 0), stop=(d == 7), skip_group_check=True),
                waits=sqw + (self.bankfree["C1"] if d == 0 else []), inc=True)
            sq_rd[d % 2] = [("t", sst)]
        mu = self.V(lambda e: e.tensor_scalar_mul(sb["lnt"][0:1, 0:512], ps["C0"][0:1, :], 1.0 / 1024.0),
                    waits=[("t", cst)], inc=True)
        m2 = self.V(lambda e: e.tensor_scalar_mul(sb["lnt"][0:1, 512:1024], ps["C1"][0:1, :], 1.0 / 1024.0),
                    waits=[("t", sst)], inc=True)
        mu2 = self.V(lambda e: e.tensor_mul(sb["lnt"][0:1, 1024:1536], sb["lnt"][0:1, 0:512],
                                            sb["lnt"][0:1, 0:512]), waits=[("v", mu)], inc=True)
        var = self.V(lambda e: e.tensor_sub(sb["lnt"][0:1, 1024:1536], sb["lnt"][0:1, 512:1024],
                                            sb["lnt"][0:1, 1024:1536]),
                     waits=[("v", mu2), ("v", m2)], inc=True)
        lnv = self.S(lambda e: e.activation(sb["lnt"][0:1, 512:1024], sb["lnt"][0:1, 1024:1536], AF.Ln,
                                            bias=self.cap[1e-5][0:1, 0:1]), waits=[("v", var)], inc=True)
        rstd = self.S(lambda e: e.activation(sb["lnt"][0:1, 1024:1536], sb["lnt"][0:1, 512:1024],
                                             AF.Exp, scale=-0.5), waits=[("s", lnv)], inc=True)
        nmr = self.V(lambda e: e.scalar_tensor_tensor(
            sb["lnt"][0:1, 1536:2048], sb["lnt"][0:1, 0:512], -1.0, sb["lnt"][0:1, 1024:1536],
            op0=ALU.mult, op1=ALU.mult), waits=[("s", rstd)], inc=True)
        rb_t = self.T(lambda e: e.matmul(ps["C0"][:], sb["onesf"][0:1, :], sb["lnt"][0:1, 1024:1536],
                                         start=True, stop=True, skip_group_check=True),
                      waits=[("v", nmr)], inc=True)
        nb_t = self.T(lambda e: e.matmul(ps["C1"][:], sb["onesf"][0:1, :], sb["lnt"][0:1, 1536:2048],
                                         start=True, stop=True, skip_group_check=True),
                      waits=[("v", nmr)], inc=True)
        for d in range(8):
            if d < 4:
                t1 = self.V(lambda e, d=d: e.tensor_tensor(
                    sb["otm"][:, 0:CH], sb["y"][:, d * CH:(d + 1) * CH], ps["C0"][:], op=ALU.mult),
                    waits=([("t", rb_t), ("t", nb_t)] + self.free["otm"]) if d == 0 else (), inc=True)
            else:
                t1c = self.V(lambda e, d=d, l=l: e.tensor_tensor(
                    sb["otm"][:, 0:1],
                    sb["fcar"][:].rearrange("p (l d) -> p l d", l=NL)[:, l, d - 4, None], ps["C0"][:, 0:1],
                    op=ALU.mult),
                    waits=(([("t", rb_t), ("t", nb_t)] + self.free["otm"]) if d == 4 else []), inc=True)
                t1 = self.V(lambda e, d=d: e.tensor_tensor(
                    sb["otm"][:, 1:CH], sb["y"][:, d * CH:d * CH + CH - 1], ps["C0"][:, 1:CH],
                    op=ALU.mult), inc=True)
                t2w = [("v", t1), ("v", t1c)]
            if d < 4:
                t2w = [("v", t1)]
            t2 = self.V(lambda e: e.tensor_tensor(sb["otm"][:, CH:2 * CH], sb["otm"][:, 0:CH],
                                                  ps["C1"][:], op=ALU.add), waits=t2w, inc=True)
            xv = self.V(lambda e, d=d, l=l: e.tensor_scalar(
                sb["xnb"][:, d * CH:(d + 1) * CH], sb["otm"][:, CH:2 * CH],
                sb["lngs"][:, l * 8 + d:l * 8 + d + 1], sb["lnbs"][:, l * 8 + d:l * 8 + d + 1],
                op0=ALU.mult, op1=ALU.add),
                waits=[("v", t2)] + (self.free["xnb"] if d == 0 else []), inc=True)
        self.V(lambda e, l=l: e.tensor_copy(
            sb["fcar"][:].rearrange("p (l d) -> p l d", l=NL)[:, l, :],
            sb["y"][:].rearrange("p (d m) -> p d m", d=8)[:, 4:8, CH - 1]), inc=True)
        xn_done = [("v", self.tick("v"))]
        self.bankfree["C0"] = [("v", self.tick("v")), ("s", self.tick("s"))]
        self.bankfree["C1"] = [("v", self.tick("v")), ("s", self.tick("s"))]
        self.free["otm"] = [("s", self.tick("s"))]
        w13 = dr["w1"].rearrange("l (t p) m -> l p t m", p=128)[l]
        gb = sb["regB"].bitcast(BF16)
        gel_free = [list(self.free["gel"]), list(self.free["gel"])]
        for ht in range(HT):
            cba, cta = self.stream_w(w13[:, :, ht * 128:(ht + 1) * 128], 1024)
            banka = ("A0", "A1")[ht % 2]
            for d in range(8):
                mma = self.T(lambda e, bank=banka, d=d, cb=cba: e.matmul(
                    ps[bank][:], sb["wbf"][:, cb + d * 128: cb + (d + 1) * 128],
                    sb["xnb"][:, d * CH:(d + 1) * CH],
                    start=(d == 0), stop=(d == 7), skip_group_check=True),
                    waits=([("s", cta)] + xn_done + self.bankfree[banka] +
                           (self.free["regB"] if ht == 0 else [])) if d == 0 else (),
                    inc=(d == 7))
            self.wdone(cba, mma)
            cbg, ctg = self.stream_w(w13[:, :, HID + ht * 128:HID + (ht + 1) * 128], 1024)
            bankg = ("B0", "B1")[ht % 2]
            for d in range(8):
                mmg = self.T(lambda e, bank=bankg, d=d, cb=cbg: e.matmul(
                    ps[bank][:], sb["wbf"][:, cb + d * 128: cb + (d + 1) * 128],
                    sb["xnb"][:, d * CH:(d + 1) * CH],
                    start=(d == 0), stop=(d == 7), skip_group_check=True),
                    waits=([("s", ctg)] + self.bankfree[bankg]) if d == 0 else (),
                    inc=(d == 7))
            self.wdone(cbg, mmg)
            gl = self.S(lambda e, bank=bankg, ht=ht, l=l: e.activation(
                sb["gel"][:, (ht % 2) * CH:(ht % 2 + 1) * CH], ps[bank][:], AF.Gelu,
                bias=sb["b1s"][:, l * 2 * HT + HT + ht:l * 2 * HT + HT + ht + 1]),
                waits=[("t", mmg)] + gel_free[ht % 2], inc=True)
            self.bankfree[bankg] = [("s", gl)]
            gv = self.V(lambda e, bank=banka, ht=ht, l=l: e.scalar_tensor_tensor(
                gb[:, ht * CH:(ht + 1) * CH], ps[bank][:],
                sb["b1s"][:, l * 2 * HT + ht:l * 2 * HT + ht + 1],
                sb["gel"][:, (ht % 2) * CH:(ht % 2 + 1) * CH], op0=ALU.add, op1=ALU.mult),
                waits=[("t", mma), ("s", gl)], inc=True)
            gel_free[ht % 2] = [("v", gv)]
            self.bankfree[banka] = [("v", gv)]
        gated_done = [("v", self.tick("v"))]
        w23 = dr["w2"].rearrange("l (k p) m -> l p k m", p=128)[l]
        for dt in range(8):
            mmt = None
            for half in range(2):
                cb, ct = self.stream_w(w23[:, :, dt * 128:(dt + 1) * 128][:, half * 11:(half + 1) * 11],
                                       11 * 128)
                for k in range(11):
                    kk = half * 11 + k
                    mmt = self.T(lambda e, dt=dt, k=k, kk=kk, cb=cb: e.matmul(
                        ps[("A0", "A1")[dt % 2]][:], sb["wbf"][:, cb + k * 128: cb + (k + 1) * 128],
                        gb[:, kk * CH:(kk + 1) * CH],
                        start=(kk == 0), stop=(kk == 21), skip_group_check=True),
                        waits=([("s", ct)] + gated_done +
                               (self.bankfree[("A0", "A1")[dt % 2]] if kk == 0 else [])) if k == 0 else (),
                        inc=(k == 10))
                self.wdone(cb, mmt)
            yv = self.V(lambda e, dt=dt: e.scalar_tensor_tensor(
                sb["y"][:, dt * CH:(dt + 1) * CH], ps[("A0", "A1")[dt % 2]][:],
                sb["b2s"][:, l * 8 + dt:l * 8 + dt + 1],
                sb["y"][:, dt * CH:(dt + 1) * CH], op0=ALU.add, op1=ALU.add),
                waits=[("t", mmt)], inc=True)
            self.bankfree[("A0", "A1")[dt % 2]] = [("v", yv)]
        self.y_state = [("v", self.tick("v"))]
        self.free["xs"] = [("t", self.tick("t"))]
        self.free["xnb"] = [("t", self.tick("t"))]
        self.free["regB"] = [("t", self.tick("t"))]
        self.free["afm"] = [("t", self.tick("t"))]
        self.free["gel"] = [("t", self.tick("t"))]

    def chunk_end(self, s, c):

        sb, dr = self.sb, self.dr
        gt = self.G(lambda e, s=s, c=c: e.dma_start(
            out=dr["ysp"].rearrange("(k d p) m -> k p d m", d=8, p=128)[s * NCH + c],
            in_=sb["y"][:].rearrange("p (d m) -> p d m", d=8)),
            waits=self.y_state, sem="dspy")
        self.free["y"] = [("dspy", gt)]

    def output(self, s):
        OSTAGE = 9
        sb, dr, ps = self.sb, self.dr, self.ps
        k1 = None
        if False:
            pass
        else:
          for h in range(H):
            par, p = h % 2, h // 2
            k1 = self.S(lambda e, par=par, p=p, h=h: e.activation(
                sb["kvb2"][par * 64:(par + 1) * 64, p * 64:(p + 1) * 64],
                sb["kv"][:, h * 64:(h + 1) * 64], AF.Copy, scale=0.125),
                waits=[("v", self.kv_tick)] if h == 0 else (), inc=(h == H - 1))
          kvb_done = [("s", k1)]
        woc = dr["woT"].rearrange("(t p) m -> p t m", p=128)
        wqc = dr["wqT"].rearrange("(t p) m -> p t m", p=128)
        for c in range(self.cfg.get("nchunks", NCH)):
            if True:
                gx = self.G(lambda e, s=s, c=c: e.dma_start(
                    out=sb["xnb"][:].rearrange("p (d m) -> p d m", d=8),
                    in_=dr["xnsp"].rearrange("(k d p) m -> k p d m", d=8, p=128)[s * NCH + c]),
                    waits=self.free["xnb"] + [("dspx", self.tick("dspx"))], sem="dgx")
            else:
                gx = 0
            if True:
                gy = self.G(lambda e, s=s, c=c: e.dma_start(
                    out=sb["y"][:].rearrange("p (d m) -> p d m", d=8),
                    in_=dr["ysp"].rearrange("(k d p) m -> k p d m", d=8, p=128)[s * NCH + c]),
                    waits=self.free["y"] + [("dspy", self.tick("dspy"))], sem="dgy")
            else:
                gy = 0
            if OSTAGE < 2:
                continue
            for dt in range(8):
                cb, ct = self.stream_w(wqc[:, :, dt * 128:(dt + 1) * 128], 1024)
                bank = ("A0", "A1")[dt % 2]
                for d in range(8):
                    mmt = self.T(lambda e, bank=bank, d=d, cb=cb: e.matmul(
                        ps[bank][:], sb["wbf"][:, cb + d * 128: cb + (d + 1) * 128],
                        sb["xnb"][:, d * CH:(d + 1) * CH],
                        start=(d == 0), stop=(d == 7), skip_group_check=True),
                        waits=([("s", ct), ("dgx", gx)] + self.bankfree[bank] +
                               (self.free["qp"] if dt == 0 else [])) if d == 0 else (),
                        inc=(d == 7))
                self.wdone(cb, mmt)
                ev = self.V(lambda e, dt=dt, bank=bank: e.tensor_copy(
                    sb["qp"][:, dt * CH:(dt + 1) * CH], ps[bank][:]),
                    waits=[("t", mmt)], inc=True)
                self.bankfree[bank] = [("v", ev)]
            qo_done = [("v", self.tick("v"))]
            if OSTAGE < 3:
                continue
            for p in range(8):
                for par in (0, 1):
                    bank = ("B0", "B1")[par]
                    r0 = par * 64
                    mmt = self.T(lambda e, bank=bank, p=p, r0=r0: e.matmul(
                        ps[bank][0:64, :], sb["kvb2"][r0:r0 + 64, p * 64:(p + 1) * 64],
                        sb["qp"][r0:r0 + 64, p * CH:(p + 1) * CH], start=True, stop=True,
                        skip_group_check=True),
                        waits=kvb_done + qo_done + self.bankfree[bank] +
                              (self.free["xs"] if (p == 0 and par == 0) else []),
                        inc=True)
                    ev = self.V(lambda e, bank=bank, p=p, r0=r0: e.tensor_copy(
                        sb["xs"][r0:r0 + 64, p * CH:(p + 1) * CH], ps[bank][0:64, :]),
                        waits=[("t", mmt)], inc=True)
                    self.bankfree[bank] = [("v", ev)]
            sp_done = [("v", self.tick("v"))]
            if OSTAGE < 4:
                continue
            for t in range(4):
                for bi, bank in enumerate(("A0", "A1")):
                    tt = None
                    for d in range(4 * bi, 4 * bi + 4):
                        tt = self.T(lambda e, bank=bank, d=d, t=t: e.transpose(
                            ps[bank][:, (d % 4) * 128:((d % 4) + 1) * 128],
                            sb["y"][:, d * CH + t * 128: d * CH + (t + 1) * 128], sb["identf"][:]),
                            waits=([("dgy", gy)] + self.bankfree[bank]) if d % 4 == 0 else (),
                            inc=(d % 4 == 3))
                    gv = self.S(lambda e, bank=bank, bi=bi, t=t: e.activation(
                        sb["gtm"][:, t * 1024 + bi * 512: t * 1024 + (bi + 1) * 512],
                        ps[bank][:], AF.Silu),
                        waits=[("t", tt)] + (self.free["gtm"] + self.free["regB"]
                                             if (t == 0 and bi == 0) else []), inc=True)
                    self.bankfree[bank] = [("s", gv)]
            g_done = [("s", self.tick("s"))]
            base = s * S + c * CH
            st_dg = []
            for t in range(4):
                dx = self.Sy(lambda e, t=t, base=base: e.dma_start(
                    out=sb["xt"][:, (t % 2) * 1024:(t % 2 + 1) * 1024],
                    in_=dr["xin"][base + t * 128: base + (t + 1) * 128, :]),
                    waits=self.xt_free[t % 2], sem=("dxA", "dxB")[t % 2])
                for half in range(2):
                    bank = ("B0", "B1")[half]
                    for blk in range(4 * half, 4 * half + 4):
                        cb, ct = self.stream_w(woc[:, :, blk * 128:(blk + 1) * 128], 1024)
                        for p in range(8):
                            mmt = self.T(lambda e, bank=bank, p=p, t=t, blk=blk, cb=cb: e.matmul(
                                ps[bank][:, (blk % 4) * 128:(blk % 4 + 1) * 128],
                                sb["xs"][:, p * CH + t * 128: p * CH + (t + 1) * 128],
                                sb["wbf"][:, cb + p * 128: cb + (p + 1) * 128],
                                start=(p == 0), stop=(p == 7), skip_group_check=True),
                                waits=([("s", ct)] + sp_done +
                                       (self.bankfree[bank] if blk % 4 == 0 else [])) if p == 0 else (),
                                inc=(p == 7))
                        self.wdone(cb, mmt)
                    c0 = half * 512
                    t1 = self.V(lambda e, bank=bank, t=t, c0=c0: e.tensor_tensor(
                        sb["otm"][:, c0:c0 + 512], ps[bank][:],
                        sb["gtm"][:, t * 1024 + c0: t * 1024 + c0 + 512], op=ALU.mult),
                        waits=[("t", mmt)] + g_done + (self.free["otm"] if half == 0 else []),
                        inc=True)
                    t2 = self.V(lambda e, t=t, c0=c0: e.tensor_tensor(
                        sb["otm"][:, c0:c0 + 512], sb["otm"][:, c0:c0 + 512],
                        sb["xt"][:, (t % 2) * 1024 + c0: (t % 2) * 1024 + c0 + 512], op=ALU.add),
                        waits=[(("dxA", "dxB")[t % 2], dx), ("v", t1)], inc=True)
                    self.bankfree[bank] = [("v", t2)]
                gs = self.G(lambda e, t=t, base=base: e.dma_start(
                    out=dr["yout"][base + t * 128: base + (t + 1) * 128, :], in_=sb["otm"][:, 0:1024]),
                    waits=[("v", self.tick("v"))], sem="dgo")
                st_dg.append(gs)
                self.xt_free[t % 2] = [("v", self.tick("v"))]
                self.free["otm"] = [("dgo", gs)]
            self.free["xs"] = [("t", self.tick("t"))]
            self.free["gtm"] = [("v", self.tick("v"))]
            self.free["regB"] = [("v", self.tick("v"))]
            self.free["xnb"] = [("t", self.tick("t"))]
            self.free["y"] = [("t", self.tick("t"))]
            self.free["qp"] = [("t", self.tick("t"))]
        self.out_done = [("dgo", self.tick("dgo"))]

    TAPS = {
        "y": ("y", F32), "xnb": ("xnb", BF16), "qp": ("qp", BF16), "kp": ("kp", BF16),
        "xs": ("xs", BF16), "vaug": ("vaug", BF16), "afm": ("afm", BF16),
        "kv": ("kv", F32), "ptsb": ("ptsb", BF16), "gtm": ("gtm", F32),
        "stats": ("stats", F32), "xt": ("xt", F32), "gated": ("afm", BF16),
        "small": ("small", F32), "otm": ("otm", F32), "kvb2": ("kvb2", BF16),
    }

    def emit_taps(self):
        names = self.cfg.get("taps", ())
        if not names:
            return
        sb = self.sb
        fw = []
        vq = self.V(lambda e: e.tensor_copy(sb["stats"][:, 0:1], sb["stats"][:, 1:2]), inc=True)
        sq = self.S(lambda e: e.copy(sb["stats"][:, 2:3], sb["stats"][:, 3:4]), inc=True)
        fw = [("v", vq), ("s", sq), ("t", self.tick("t"))]
        caps = {"afm": 8 * CH, "gtm": 4096, "gel": 1024, "gated": HT * CH}
        for n in names:
            src, dt = self.TAPS[n]
            hh = self.sb[src]
            shape = list(hh.shape)
            if n in caps:
                shape = [shape[0], caps[n]]
            tap = self.nc.dram_tensor("tap_" + n, shape, dt, kind="ExternalOutput").ap()
            self.G(lambda e, tap=tap, hh=hh, shape=shape: e.dma_start(
                out=tap[:, :], in_=hh[0:shape[0], 0:shape[1]]),
                   waits=fw, sem="dgo")

    def build(self):
        self.declare_dram()
        with self.es:
            self.alloc()
            self.init_stream()
            self.prologue()
            self.y_state = []
            self.kv_tick = 0
            nl = self.cfg.get("nlayers", NL)
            for s in range(self.cfg.get("nseq", NSEQ)):
                self.seq_init(s)
                for c in range(self.cfg.get("nchunks", NCH)):
                    self.phase1(s, c)
                    self.y_state = list(self.y_ready)
                    for l in range(nl):
                        qd, kd, vd = self.attn(s, c, l)
                        self.attn_core(s, c, l, qd, kd, vd)
                        if not self.cfg.get("attn_only", False):
                            self.ffn(s, c, l)
                    if self.cfg.get("output", True):
                        self.chunk_end(s, c)
                if self.cfg.get("output", True):
                    self.output(s)
            self.emit_taps()
            final = self.tick("dgo")
            if final:
                self.P.op("sync", lambda e: None, waits=[("dgo", final)])
            block = self.es.enter_context(self.nc.Block())
            self.P.finalize(block)
        return self.nc


def build_core(cfg):
    core = Core(cfg)
    nc = core.build()
    return nc


_NCORES = 8


def kernel(**inputs):
    from concourse.bass_utils import run_bass_kernel_spmd
    x = np.ascontiguousarray(np.asarray(inputs["L0_attn_x"], dtype=np.float32))
    B = x.shape[0]
    consts = make_consts()
    w = make_weights({k: np.asarray(v) for k, v in inputs.items()})
    base = {}
    for k in ("wqT", "wkT", "wvT", "woT", "wqkv", "wo_i", "w1", "w2",
              "b1", "b2", "lng", "lnb", "qs", "ks"):
        base[k] = w[k]
    for k in ("cs8", "ss8", "cik", "sik"):
        base[k] = consts[k]
    base["masks"] = np.ascontiguousarray(
        np.concatenate([consts["mask_n0"], consts["mask_n1"], consts["mask_f0"], consts["mask_f1"]], 1))
    base["identb"] = consts["ident_b"]
    base["identf"] = consts["ident_f"]
    base["rmat"] = consts["rmat"]
    base["l2ones"] = consts["l2ones"]
    base["l2pat"] = consts["l2pat"]
    base["onesb"] = consts["ones"]
    cfg = {"nseq": NSEQ, "nchunks": NCH, "nlayers": NL, "output": True, "taps": ()}
    nc = build_core(cfg)
    in_maps = []
    per = B // _NCORES
    for ci in range(_NCORES):
        m = dict(base)
        m["xin"] = np.ascontiguousarray(x[ci * per:(ci + 1) * per].reshape(per * S, D))
        in_maps.append(m)
    res = run_bass_kernel_spmd(nc, in_maps, core_ids=list(range(_NCORES)))
    out = np.empty((B, S, D), dtype=np.float32)
    for ci in range(_NCORES):
        out[ci * per:(ci + 1) * per] = np.asarray(
            res.results[ci]["yout"], dtype=np.float32).reshape(per, S, D)
    return out


# version 02
# speedup vs baseline: 1.1492x; 1.1492x over previous
import numpy as np
import ml_dtypes
import contextlib
import concourse.bass as bass
import concourse.mybir as mybir

F32 = mybir.dt.float32
BF16 = mybir.dt.bfloat16
AF = mybir.ActivationFunctionType
ALU = mybir.AluOpType
AX = mybir.AxisListType.X

D = 1024
CH = 512
S = 2048
NCH = S // CH
NSEQ = 2
H = 16
DH = 64
W = 256
NL = 6
HID = 2816
HT = HID // 128
NEG = -1.0e30

bf = ml_dtypes.bfloat16


def make_consts():
    c = {}
    c["ident_b"] = np.eye(128, dtype=bf)
    c["ident_f"] = np.eye(128, dtype=np.float32)
    R = np.zeros((128, 128), dtype=np.float64)
    for h0 in (0, 64):
        for m in range(32):
            R[h0 + m + 32, h0 + m] = -1.0
            R[h0 + m, h0 + m + 32] = 1.0
    c["rmat"] = R.astype(bf)
    lo = np.zeros((128, 8, 16), dtype=np.float64)
    for f in range(8):
        lo[0:64, f, 2 * f] = 1.0
        lo[64:128, f, 2 * f + 1] = 1.0
    c["l2ones"] = lo.reshape(128, 128).astype(bf)
    pat = np.zeros((16, 8, 128), dtype=np.float64)
    for f in range(8):
        pat[2 * f, f, 0:64] = 1.0
        pat[2 * f + 1, f, 64:128] = 1.0
    c["l2pat"] = pat.reshape(16, 1024).astype(bf)
    c["ones"] = np.ones((128, 128), dtype=bf)
    half = np.arange(0, DH, 2, dtype=np.float64)
    inv_freq = 1.0 / 10000.0 ** (half / DH)
    t = np.arange(S, dtype=np.float64)
    freqs = t[:, None] * inv_freq[None, :]
    freqs = np.concatenate([freqs, freqs], -1)
    base_scale = (half + 0.4 * DH) / (1.4 * DH)
    power = (t - S // 2) / (W // 2)
    scale = base_scale[None, :] ** power[:, None]
    scale = np.concatenate([scale, scale], -1)
    cos, sin = np.cos(freqs), np.sin(freqs)
    qk_scale = 8.0
    cs8 = (cos * scale * qk_scale).T
    ss8 = (sin * scale * qk_scale).T
    cik = (cos / scale).T
    sik = (sin / scale).T
    for name, tab in (("cs8", cs8), ("ss8", ss8), ("cik", cik), ("sik", sik)):
        c[name] = np.concatenate([tab, tab], 0).astype(np.float32)
    def mask_tile(i_base, j_base, first):
        m = np.zeros((128, 384), dtype=np.float64)
        for ii in range(128):
            i = i_base + ii
            for jj in range(384):
                j = j_base + jj
                valid = (i <= j <= i + W) and not (first and j < 256)
                if not valid:
                    m[ii, jj] = NEG
        return m.astype(bf)
    c["mask_n0"] = mask_tile(0, 0, False)
    c["mask_n1"] = mask_tile(128, 128, False)
    c["mask_f0"] = mask_tile(0, 0, True)
    c["mask_f1"] = mask_tile(128, 128, True)
    return c


def make_weights(iw):
    w = {}
    w["wqT"] = np.ascontiguousarray(iw["L0_attn_W_q"].T)
    w["wkT"] = np.ascontiguousarray(iw["L0_attn_W_k"].T)
    w["wvT"] = np.ascontiguousarray(iw["L0_attn_W_v"].T)
    w["woT"] = np.ascontiguousarray(iw["L0_attn_W_o"].T)
    w["wqkv"] = np.ascontiguousarray(iw["L0_ffn_Wqkv"])
    w["wo_i"] = np.ascontiguousarray(iw["L0_ffn_Wo"])
    W1 = iw["L0_ffn_W1"]
    w1p = np.zeros((NL, D, 2 * HID), dtype=np.float32)
    w1p[:, :, 0:2730] = W1[:, :, 0:2730]
    w1p[:, :, HID:HID + 2730] = W1[:, :, 2730:5460]
    w["w1"] = w1p
    b1 = iw["L0_ffn_b1"]
    b1p = np.zeros((NL, 2 * HID), dtype=np.float32)
    b1p[:, 0:2730] = b1[:, 0:2730]
    b1p[:, HID:HID + 2730] = b1[:, 2730:5460]
    w["b1"] = np.ascontiguousarray(b1p.reshape(NL, 2 * HT, 128).transpose(2, 0, 1).reshape(128, NL * 2 * HT))
    W2 = iw["L0_ffn_W2"]
    w2p = np.zeros((NL, HID, D), dtype=np.float32)
    w2p[:, 0:2730, :] = W2
    w["w2"] = w2p
    def pp(v):
        return np.ascontiguousarray(v.reshape(NL, 8, 128).transpose(2, 0, 1).reshape(128, NL * 8))
    w["b2"] = pp(iw["L0_ffn_b2"])
    w["lng"] = pp(iw["L0_ffn_ln_g"])
    w["lnb"] = pp(iw["L0_ffn_ln_b"])
    w["qs"] = np.ascontiguousarray(np.tile(iw["L0_ffn_q_scale"], (1, 2)).T)
    w["ks"] = np.ascontiguousarray(np.tile(iw["L0_ffn_k_scale"], (1, 2)).T)
    return {k: np.asarray(v, dtype=(bf if v.dtype == bf else np.float32)) for k, v in w.items()}


class Prog:
    ENGINES = ("sync", "tensor", "vector", "scalar", "gpsimd")

    def __init__(self, nc, es):
        self.nc = nc
        self.es = es
        self.q = {e: [] for e in self.ENGINES}
        self.semc = {}
        self.sems = {}

    def sem(self, name):
        h = self.es.enter_context(self.nc.semaphore(name))
        self.semc[name] = 0
        self.sems[name] = h
        return name

    def tick(self, name):
        return self.semc[name]

    DEFER = ("vector", "scalar")

    def set_spacer(self, eng, fn):
        self.spacer_fn = getattr(self, "spacer_fn", {})
        self.spacer_fn[eng] = fn

    def _emit_spacer(self, eng, carry):
        fn = self.spacer_fn[eng]

        def cb(e, fn=fn, carry=carry):
            ins = fn(e)
            for h, amt in carry:
                ins = ins.then_inc(h, amt)
        self.q[eng].append(cb)

    def op(self, eng, fn, waits=(), inc=None, amt=1):
        for s, v in waits:
            assert v <= self.semc[s], f"wait-ahead-of-inc on {s}: {v} > {self.semc[s]}"
        self.pending = getattr(self, "pending", {e: [] for e in self.ENGINES})
        waits_h = [(self.sems[s], v) for s, v in waits if v > 0]
        defer = eng in self.DEFER
        if defer and self.pending[eng] and waits_h:
            self._emit_spacer(eng, self.pending[eng])
            self.pending[eng] = []
        carry = list(self.pending[eng]) if defer else []
        if defer:
            self.pending[eng] = []
        cap_inc = None
        if inc is not None:
            self.semc[inc] += amt
            if not defer:
                cap_inc = (self.sems[inc], amt)

        def cb(e, fn=fn, waits_h=waits_h, cap_inc=cap_inc, carry=carry):
            for h, v in waits_h:
                e.wait_ge(h, v)
            ins = fn(e)
            for h, amt2 in carry:
                assert ins is not None, "carry needs a real instruction"
                ins = ins.then_inc(h, amt2)
            if cap_inc is not None:
                assert ins is not None, "inc requested but fn returned None"
                ins.then_inc(*cap_inc)
        self.q[eng].append(cb)
        if inc is not None and defer:
            self.pending[eng].append((self.sems[inc], amt))
        return self.semc[inc] if inc is not None else None

    def finalize(self, block):
        for eng in self.DEFER:
            if getattr(self, "pending", {}).get(eng):
                self._emit_spacer(eng, self.pending[eng])
                self.pending[eng] = []
        for name in self.ENGINES:
            cbs = self.q[name]
            if not cbs:
                continue
            def section(e, cbs=cbs):
                for cb in cbs:
                    cb(e)
            getattr(block, name)(section)


class Core:

    def __init__(self, cfg):
        self.cfg = cfg
        self.nc = bass.Bass("TRN2", target_bir_lowering=False)
        self.es = contextlib.ExitStack()
        self.taps = {}

    def declare_dram(self):
        nc = self.nc
        dr = {}
        dr["xin"] = nc.dram_tensor("xin", [NSEQ * S, D], F32, kind="ExternalInput").ap()
        dr["yout"] = nc.dram_tensor("yout", [NSEQ * S, D], F32, kind="ExternalOutput").ap()
        for n, shp in (("wqT", [D, D]), ("wkT", [D, D]), ("wvT", [D, D]), ("woT", [D, D])):
            dr[n] = nc.dram_tensor(n, shp, F32, kind="ExternalInput").ap()
        dr["wqkv"] = nc.dram_tensor("wqkv", [NL, D, 3072], F32, kind="ExternalInput").ap()
        dr["wo_i"] = nc.dram_tensor("wo_i", [NL, D, D], F32, kind="ExternalInput").ap()
        dr["w1"] = nc.dram_tensor("w1", [NL, D, 2 * HID], F32, kind="ExternalInput").ap()
        dr["w2"] = nc.dram_tensor("w2", [NL, HID, D], F32, kind="ExternalInput").ap()
        for n, cc in (("b1", NL * 2 * HT), ("b2", NL * 8), ("lng", NL * 8), ("lnb", NL * 8),
                      ("qs", NL), ("ks", NL)):
            dr[n] = nc.dram_tensor(n, [128, cc], F32, kind="ExternalInput").ap()
        for n in ("cs8", "ss8", "cik", "sik"):
            dr[n] = nc.dram_tensor(n, [128, S], F32, kind="ExternalInput").ap()
        dr["masks"] = nc.dram_tensor("masks", [128, 4 * 384], BF16, kind="ExternalInput").ap()
        dr["identb"] = nc.dram_tensor("identb", [128, 128], BF16, kind="ExternalInput").ap()
        dr["identf"] = nc.dram_tensor("identf", [128, 128], F32, kind="ExternalInput").ap()
        dr["rmat"] = nc.dram_tensor("rmat", [128, 128], BF16, kind="ExternalInput").ap()
        dr["l2ones"] = nc.dram_tensor("l2ones", [128, 128], BF16, kind="ExternalInput").ap()
        dr["l2pat"] = nc.dram_tensor("l2pat", [16, 1024], BF16, kind="ExternalInput").ap()
        dr["onesb"] = nc.dram_tensor("onesb", [128, 128], BF16, kind="ExternalInput").ap()
        dr["wbfd"] = nc.dram_tensor("wbfd", [128, self.cfg.get("wcols", 700000)], BF16, kind="Internal").ap()
        dr["xnsp"] = nc.dram_tensor("xnsp", [NSEQ * NCH * 8 * 128, CH], BF16, kind="Internal").ap()
        dr["ysp"] = nc.dram_tensor("ysp", [NSEQ * NCH * 8 * 128, CH], F32, kind="Internal").ap()
        self.dr = dr

    def alloc(self):
        nc, es = self.nc, self.es
        sb = {}

        def t(name, shape, dt):
            sb[name] = es.enter_context(nc.sbuf_tensor("s_" + name, shape, dt))
            return sb[name]

        t("y", [128, 8 * CH], F32)
        t("kcar", [128, NL * 8 * 256], BF16)
        t("vcar", [128, NL * 2 * 1040], BF16)
        t("acar", [128, NL * 4], F32)
        t("fcar", [128, NL * 4], F32)
        for n in ("cs8", "ss8", "cik", "sik"):
            t(n + "b", [128, S], BF16)
        t("masks", [128, 4 * 384], BF16)
        t("identb", [128, 128], BF16)
        t("identf", [128, 128], F32)
        t("rmat", [128, 128], BF16)
        t("l2ones", [128, 128], BF16)
        t("l2pat", [16, 1024], BF16)
        t("onesb", [128, 128], BF16)
        t("consts", [128, 8], F32)
        t("b1s", [128, NL * 2 * HT], F32)
        t("b2s", [128, NL * 8], F32)
        t("lngs", [128, NL * 8], F32)
        t("lnbs", [128, NL * 8], F32)
        t("qss", [128, NL], F32)
        t("kss", [128, NL], F32)
        t("kv", [64, 1024], F32)
        t("kvb2", [128, 512], BF16)
        t("xt", [128, 2 * 1024], F32)
        t("regB", [128, 22 * CH // 2], F32)
        t("xs", [128, 8 * CH], BF16)
        t("xnb", [128, 8 * CH], BF16)
        t("qp", [128, 8 * CH], BF16)
        t("kp", [128, 8 * CH], BF16)
        t("sqbf", [128, 2 * CH], BF16)
        t("rot", [128, 4 * CH], BF16)
        t("vaug", [128, 4 * 1040], BF16)
        t("pbuf", [128, 4 * CH], BF16)
        t("ptsb", [128, 2 * 1024], BF16)
        t("wsg", [128, 2 * 1408], F32)
        t("wbf", [128, 2 * 1408], BF16)
        t("stats", [128, 32], F32)
        t("nmax", [128, 8], F32)
        t("small", [64, 2048], F32)
        sb["invt"] = sb["small"][0:16, 0:1024]
        sb["lnt"] = sb["small"][0:1, 0:2048]
        sb["rs"] = sb["small"][0:1, 0:512]
        sb["rb"] = sb["small"][0:64, 512:1024]
        sb["invtb"] = sb["small"].bitcast(BF16)[0:16, 2048:2560]
        t("otm", [128, 1024], F32)
        sb["gtm"] = sb["regB"]
        sb["gel"] = sb["pbuf"][:, 0:1024]
        sb["afm"] = sb["regB"].bitcast(BF16)
        sb["vfs"] = sb["xt"].bitcast(BF16)
        self.sb = sb

        ps = {}
        for n in ("A0", "A1", "B0", "B1", "C0", "C1"):
            ps[n] = es.enter_context(nc.psum_tensor(n, [128, 512], F32))
        ps["PT0"] = es.enter_context(nc.psum_tensor("PT0", [128, 1024], BF16))
        ps["PT1"] = es.enter_context(nc.psum_tensor("PT1", [128, 1024], BF16))
        self.ps = ps

        t("spc", [1, 8], F32)
        t("onesf", [1, 128], F32)
        t("rqs", [128, 128], BF16)
        self.P = Prog(nc, es)
        self.P.set_spacer("vector", lambda e: e.tensor_copy(sb["spc"][0:1, 0:1], sb["spc"][0:1, 1:2]))
        self.P.set_spacer("scalar", lambda e: e.copy(sb["spc"][0:1, 2:3], sb["spc"][0:1, 3:4]))
        for s in ("dwA", "dwB", "dxA", "dxB", "dco", "dspy", "dspx", "dgx", "dgy",
                  "dgo", "dgw", "t", "v", "s"):
            self.P.sem(s)
        self.bankfree = {n: [] for n in ps}
        self.free = {n: [] for n in ("xnb", "y", "qp", "kp", "vaug", "afm", "xs", "regB",
                                     "qbf", "sqbf", "rot", "pbuf", "ptsb", "gel", "vfs",
                                     "xt", "gtm", "otm")}

    def T(self, fn, waits=(), inc=False):
        return self.P.op("tensor", fn, waits, "t" if inc else None)

    def V(self, fn, waits=(), inc=False):
        return self.P.op("vector", fn, waits, "v" if inc else None)

    def S(self, fn, waits=(), inc=False):
        return self.P.op("scalar", fn, waits, "s" if inc else None)

    def Sy(self, fn, waits=(), sem="dwA"):
        return self.P.op("sync", fn, waits, sem, amt=16)

    def G(self, fn, waits=(), sem="dgo"):
        return self.P.op("gpsimd", fn, waits, sem, amt=16)

    def tick(self, s):
        return self.P.tick(s)

    def Vsp(self):
        self.P.op("vector", lambda e: e.tensor_copy(
            self.sb["spc"][0:1, 4:5], self.sb["spc"][0:1, 5:6]))

    def Ssp(self):
        self.P.op("scalar", lambda e: e.copy(
            self.sb["spc"][0:1, 6:7], self.sb["spc"][0:1, 7:8]))

    def init_stream(self):
        self.wslot = 0
        self.wfree = [[], []]
        self.collect = False
        self.wreg = {}
        self.wreg_order = []
        self.wreg_cols = 0
        self.wconv_waited = [False, False]

    def stream_w(self, dram_ap3, ncols, key=None):
        assert key is not None
        if self.collect:
            if key not in self.wreg:
                self.wreg[key] = (self.wreg_cols, ncols)
                self.wreg_order.append((key, dram_ap3, ncols))
                self.wreg_cols += ncols
            self.wsem_used = "dwA"
            return 0, 0
        off, nc2 = self.wreg[key][0], self.wreg[key][1]
        assert nc2 == ncols
        sl = self.wslot
        self.wslot ^= 1
        sb = self.sb
        wsem = ("dwA", "dwB")[sl]
        extra = [] if self.wconv_waited[sl] else [("dgw", self.tick("dgw"))]
        self.wconv_waited[sl] = True
        dw = self.Sy(lambda e, off=off, sl=sl, ncols=ncols: e.dma_start(
            out=sb["wbf"][:, sl * 1408: sl * 1408 + ncols],
            in_=self.dr["wbfd"][:, off:off + ncols]),
            waits=self.wfree[sl] + extra, sem=wsem)
        self.cur_wslot = sl
        self.wsem_used = wsem
        return sl * 1408, dw

    def wdone(self, sl_base, t_tick):
        self.wfree[sl_base // 1408] = [("t", t_tick)]

    def emit_weight_cache(self):
        sb = self.sb
        wsg_free = [[], []]
        wbf_free = [[], []]
        for i, (key, ap3, ncols) in enumerate(self.wreg_order):
            sl = i % 2
            kt = ncols // 128
            wsem = ("dxA", "dxB")[sl]
            dw = self.Sy(lambda e, a=ap3, sl=sl, kt=kt: e.dma_start(
                out=sb["wsg"][:, sl * 1408: sl * 1408 + kt * 128].rearrange("p (k m) -> p k m", k=kt),
                in_=a), waits=wsg_free[sl], sem=wsem)
            ct = self.S(lambda e, sl=sl, n=ncols: e.copy(
                sb["wbf"][:, sl * 1408: sl * 1408 + n], sb["wsg"][:, sl * 1408: sl * 1408 + n]),
                waits=[(wsem, dw)] + wbf_free[sl], inc=True)
            wsg_free[sl] = [("s", ct)]
            off = self.wreg[key][0]
            gw = self.G(lambda e, sl=sl, off=off, n=ncols: e.dma_start(
                out=self.dr["wbfd"][:, off:off + n],
                in_=sb["wbf"][:, sl * 1408: sl * 1408 + n]),
                waits=[("s", ct)], sem="dgw")
            wbf_free[sl] = [("dgw", gw)]
        self.wfree = [[], []]

    def prologue(self):
        sb, dr = self.sb, self.dr
        self.V(lambda e: e.memset(sb["onesf"][0:1, :], 1.0))
        self.V(lambda e: e.memset(sb["spc"][0:1, 0:2], 0.0))
        self.V(lambda e: e.memset(sb["spc"][0:1, 4:6], 0.0))
        self.S(lambda e: e.memzero(sb["spc"][0:1, 2:4]))
        self.S(lambda e: e.memzero(sb["spc"][0:1, 6:8]))
        for n, d in (("masks", "masks"), ("identb", "identb"), ("identf", "identf"),
                     ("rmat", "rmat"), ("l2ones", "l2ones"), ("l2pat", "l2pat"),
                     ("onesb", "onesb"), ("b1s", "b1"), ("b2s", "b2"), ("lngs", "lng"),
                     ("lnbs", "lnb"), ("qss", "qs"), ("kss", "ks")):
            self.Sy(lambda e, n=n, d=d: e.dma_start(out=sb[n][:], in_=dr[d][:, :]), sem="dco")
        for rnd, (na, nb) in enumerate((("cs8", "ss8"), ("cik", "sik"))):
            wv = [("v", self.tick("v"))] if rnd else []
            ta = self.Sy(lambda e, na=na, r=rnd: e.dma_start(out=sb["gtm"][:, 0:2048], in_=dr[na][:, :]),
                         waits=wv, sem="dxA")
            tb = self.Sy(lambda e, nb=nb: e.dma_start(out=sb["gtm"][:, 2048:4096], in_=dr[nb][:, :]),
                         waits=wv, sem="dxB")
            self.V(lambda e, na=na: e.tensor_copy(sb[na + "b"][:], sb["gtm"][:, 0:2048]),
                   waits=[("dxA", ta)], inc=True)
            self.V(lambda e, nb=nb: e.tensor_copy(sb[nb + "b"][:], sb["gtm"][:, 2048:4096]),
                   waits=[("dxB", tb)], inc=True)
        cvals = [(-1.0, 0), (1e-5, 1), (1.0 / 1024.0, 2), (1e-24, 3), (0.125, 4), (1.0, 5)]
        for v, i in cvals:
            self.V(lambda e, v=v, i=i: e.memset(sb["consts"][:, i:i + 1], v))
            self.nc.const_aps.register(F32, v, sb["consts"][:, i:i + 1])
        self.cap = {v: sb["consts"][:, i:i + 1] for v, i in cvals}
        self.V(lambda e: e.memset(sb["small"][:], 0.0))
        self.V(lambda e: e.memset(sb["kcar"][:], 0.0))
        self.V(lambda e: e.memset(sb["vcar"][:], 0.0))
        self.V(lambda e: e.memset(
            sb["vaug"][:].rearrange("p (t h c) -> p t h c", t=4, c=65)[:, :, :, 64:65], 1.0), inc=True)
        self.prol_v = self.tick("v")
        self.free["regB"] = [("v", self.prol_v)]
        pw = [("dco", self.tick("dco"))]
        for eng in ("tensor", "vector", "scalar", "gpsimd"):
            self.P.op(eng, lambda e: None, waits=pw)

    def seq_init(self, s):
        sb = self.sb
        self.V(lambda e: e.memset(sb["acar"][:], 0.0))
        self.V(lambda e: e.memset(sb["fcar"][:], 0.0))
        self.V(lambda e: e.memset(sb["kv"][:], 0.0), inc=True)
        self.seq_v = self.tick("v")

    def phase1(self, s, c):
        sb, dr, ps = self.sb, self.dr, self.ps
        parts = 9
        ntiles = 4
        base = s * S + c * CH
        st = sb["stats"]
        xn_done_s = None
        self.xt_free = getattr(self, "xt_free", [[], []])
        self.xnt_free = getattr(self, "xnt_free", [[], []])
        for t in range(ntiles):
            sl = t % 2
            xsem = ("dxA", "dxB")[sl]
            dx = self.Sy(lambda e, t=t, sl=sl: e.dma_start(
                out=sb["xt"][:, sl * 1024:(sl + 1) * 1024],
                in_=dr["xin"][base + t * 128: base + (t + 1) * 128, :]),
                waits=self.xt_free[sl] + ([("v", self.prol_v)] if (s == 0 and c == 0 and t == 0) else []),
                sem=xsem)
            xsl = sb["xt"][:, sl * 1024:(sl + 1) * 1024]
            NOSTATS = False
            if NOSTATS:
                ss_t = self.S(lambda e, xsl=xsl, t=t: e.copy(
                    sb["regB"][:, 2048:2176], xsl[:, 0:128]),
                    waits=[(xsem, dx)] + (self.free["regB"] if t == 0 else []), inc=True)
            else:
                ss_t = self.S(lambda e, xsl=xsl, t=t: e.activation(
                    sb["regB"][:, 2048:3072], xsl, AF.Square, accum_out=st[:, t:t + 1]),
                    waits=[(xsem, dx)] + (self.free["regB"] if t == 0 else []), inc=True)
            sm_t = self.V(lambda e, xsl=xsl, t=t: e.reduce_sum(st[:, 4 + t:5 + t], xsl, axis=AX),
                          waits=[(xsem, dx)], inc=True)
            mu_t = self.V(lambda e, t=t: e.tensor_scalar_mul(st[:, 8 + t:9 + t], st[:, 4 + t:5 + t], 1.0 / 1024.0),
                          waits=[("v", sm_t)], inc=True)
            m2_t = self.V(lambda e, t=t: e.tensor_scalar_mul(st[:, 12 + t:13 + t], st[:, t:t + 1], 1.0 / 1024.0),
                          waits=[("s", ss_t)], inc=True)
            mu2_t = self.V(lambda e, t=t: e.tensor_mul(st[:, 16 + t:17 + t], st[:, 8 + t:9 + t], st[:, 8 + t:9 + t]),
                           waits=[("v", mu_t)], inc=True)
            var_t = self.V(lambda e, t=t: e.tensor_sub(st[:, 20 + t:21 + t], st[:, 12 + t:13 + t],
                                                       st[:, 16 + t:17 + t]),
                           waits=[("v", mu2_t), ("v", m2_t)], inc=True)
            if NOSTATS:
                ln_t = self.S(lambda e, t=t: e.copy(st[:, 12 + t:13 + t], st[:, 20 + t:21 + t]),
                              waits=[("v", var_t)], inc=True)
                rstd_t = self.S(lambda e, t=t: e.copy(st[:, 24 + t:25 + t], st[:, 12 + t:13 + t]),
                                waits=[("s", ln_t)], inc=True)
            else:
                ln_t = self.S(lambda e, t=t: e.activation(st[:, 12 + t:13 + t], st[:, 20 + t:21 + t], AF.Ln,
                                                          bias=self.cap[1e-5], scale=1.0),
                              waits=[("v", var_t)], inc=True)
                rstd_t = self.S(lambda e, t=t: e.activation(st[:, 24 + t:25 + t], st[:, 12 + t:13 + t],
                                                            AF.Exp, scale=-0.5),
                                waits=[("s", ln_t)], inc=True)
            nmr_t = self.V(lambda e, t=t: e.scalar_tensor_tensor(
                st[:, 28 + t:29 + t], st[:, 8 + t:9 + t], -1.0, st[:, 24 + t:25 + t],
                op0=ALU.mult, op1=ALU.mult), waits=[("s", rstd_t)], inc=True)
            xsl2 = sb["regB"][:, (sl * 1024):(sl + 1) * 1024]
            xn_t = self.V(lambda e, xsl=xsl, xsl2=xsl2, t=t: e.tensor_scalar(
                xsl2, xsl, st[:, 24 + t:25 + t], st[:, 28 + t:29 + t],
                op0=ALU.mult, op1=ALU.add),
                waits=[("v", nmr_t)] + self.xnt_free[sl], inc=True)
            self.xt_free[sl] = [("v", xn_t)]
            EV = "vs"
            if parts < 1:
                self.xnt_free[sl] = []
                last_evs = []
                continue
            evs = []
            for bi, bank in enumerate(("A0", "A1")):
                tt = None
                for d in range(4 * bi, 4 * bi + 4):
                    tt = self.T(lambda e, bank=bank, d=d, xsl2=xsl2: e.transpose(
                        ps[bank][:, (d % 4) * 128:((d % 4) + 1) * 128],
                        xsl2[:, d * 128:(d + 1) * 128], sb["identf"][:]),
                        waits=([("v", xn_t)] + self.bankfree[bank]) if d % 4 == 0 else (),
                        inc=(d % 4 == 3))
                tt = self.tick("t")
                import os as _os
                if EV == "none":
                    yv = self.V(lambda e: e.tensor_copy(sb["stats"][:, 5:6], sb["stats"][:, 6:7]),
                                waits=[("t", tt)], inc=True)
                    xv = self.S(lambda e: e.copy(sb["stats"][:, 6:7], sb["stats"][:, 7:8]),
                                waits=[("t", tt)], inc=True)
                elif EV == "v":
                    yv = self.V(lambda e, bank=bank, bi=bi, t=t: e.tensor_copy(
                        sb["y"][:].rearrange("p (d m) -> p d m", d=8)[:, 4 * bi:4 * bi + 4, t * 128:(t + 1) * 128],
                        ps[bank][:].rearrange("p (d m) -> p d m", d=4)),
                        waits=[("t", tt)] + (self.free["y"] if (t == 0 and bi == 0) else []), inc=True)
                    xv = self.S(lambda e: e.copy(sb["stats"][:, 6:7], sb["stats"][:, 7:8]),
                                waits=[("t", tt)], inc=True)
                else:
                    yv = self.V(lambda e, bank=bank, bi=bi, t=t: e.tensor_copy(
                        sb["y"][:].rearrange("p (d m) -> p d m", d=8)[:, 4 * bi:4 * bi + 4, t * 128:(t + 1) * 128],
                        ps[bank][:].rearrange("p (d m) -> p d m", d=4)),
                        waits=[("t", tt)] + (self.free["y"] if (t == 0 and bi == 0) else []), inc=True)
                    xv = None
                    for dd in range(4):
                        d2 = 4 * bi + dd
                        xv = self.S(lambda e, d2=d2, t=t: e.copy(
                            sb["xnb"][:, d2 * CH + t * 128: d2 * CH + (t + 1) * 128],
                            sb["y"][:, d2 * CH + t * 128: d2 * CH + (t + 1) * 128]),
                            waits=([("v", yv)] + (self.free["xnb"] if (t == 0 and bi == 0) else []))
                            if dd == 0 else (),
                            inc=(dd == 3))
                self.bankfree[bank] = [("v", yv), ("s", xv)]
                evs += [("v", yv), ("s", xv)]
            self.xnt_free[sl] = [("t", self.tick("t"))]
            last_evs = evs
        self.xn_ready = last_evs
        del last_evs
        if parts < 2:
            self.y_ready = self.xn_ready
            self.kv_tick = 0
            return
        self.y_ready = self.xn_ready
        if parts < 1:
            self.kv_tick = 0
            return
        gsp = self.G(lambda e, s=s, c=c: e.dma_start(
            out=dr["xnsp"].rearrange("(k d p) m -> k p d m", d=8, p=128)[s * NCH + c],
            in_=sb["xnb"][:].rearrange("p (d m) -> p d m", d=8)),
            waits=self.xn_ready, sem="dspx")
        self.xnb_spilled = gsp
        if parts < 3:
            self.kv_tick = 0
            return
        wk = dr["wkT"].rearrange("(t p) m -> p t m", p=128)
        wv = dr["wvT"].rearrange("(t p) m -> p t m", p=128)
        for wi, (wsrc, dst) in enumerate(((wk, "qp"), (wv, "kp"))):
            for blk in range(8):
                cb, ct = self.stream_w(wsrc[:, :, blk * 128:(blk + 1) * 128], 1024, key=("okv", wi, blk))
                wsu = self.wsem_used
                bank = ("B0", "B1")[blk % 2]
                mmt = None
                for t in range(4):
                    for d in range(8):
                        mmt = self.T(lambda e, bank=bank, t=t, d=d, cb=cb: e.matmul(
                            ps[bank][:, t * 128:(t + 1) * 128],
                            sb["xnb"][:, d * 512 + t * 128: d * 512 + (t + 1) * 128],
                            sb["wbf"][:, cb + d * 128: cb + (d + 1) * 128],
                            start=(d == 0), stop=(d == 7)),
                            waits=([(wsu, ct)] + self.xn_ready + self.bankfree[bank]) if (t == 0 and d == 0) else (),
                            inc=(t == 3 and d == 7))
                mmt = self.tick("t")
                self.wdone(cb, mmt)
                ev = self.V(lambda e, dst=dst, blk=blk, bank=bank: e.tensor_copy(
                    sb[dst][:].rearrange("p (t m) -> p t m", t=4)[:, :, blk * 128:(blk + 1) * 128],
                    ps[bank][:].rearrange("p (t m) -> p t m", t=4)),
                    waits=[("t", mmt)] + (self.free["qp" if dst == "qp" else "kp"] if blk == 0 else []),
                    inc=True)
                self.bankfree[bank] = [("v", ev)]
        kvready = [("v", self.tick("v"))]
        self.free["xnb"] = [("t", self.tick("t"))]
        if parts < 4:
            self.kv_tick = 0
            return
        mmt = None
        for t in range(4):
            for h in range(H):
                bank = "C0" if h < 8 else "C1"
                mmt = self.T(lambda e, t=t, h=h, bank=bank: e.matmul(
                    ps[bank][0:64, (h % 8) * 64: (h % 8 + 1) * 64],
                    sb["qp"][:, t * 1024 + h * 64: t * 1024 + (h + 1) * 64],
                    sb["kp"][:, t * 1024 + h * 64: t * 1024 + (h + 1) * 64],
                    start=(t == 0 and h % 8 == 0), stop=(t == 3 and h % 8 == 7),
                    skip_group_check=True),
                    waits=(kvready + self.bankfree["C0"] + self.bankfree["C1"])
                          if (t == 0 and h == 0) else (),
                    inc=(t == 3 and h == H - 1))
        self.free["qp"] = [("t", mmt)]
        self.free["kp"] = [("t", mmt)]
        kvv0 = self.V(lambda e: e.tensor_tensor(sb["kv"][:, 0:512], sb["kv"][:, 0:512],
                                                ps["C0"][0:64, :], op=ALU.add),
                      waits=[("t", mmt), ("v", self.seq_v)], inc=True)
        kvv = self.V(lambda e: e.tensor_tensor(sb["kv"][:, 512:1024], sb["kv"][:, 512:1024],
                                               ps["C1"][0:64, :], op=ALU.add), inc=True)
        self.bankfree["C0"] = [("v", kvv0)]
        self.bankfree["C1"] = [("v", kvv)]
        self.kv_tick = kvv

    def attn(self, s, c, l):
        sb, dr, ps = self.sb, self.dr, self.ps
        P = self.P
        ccol = c * CH
        x0 = self.V(lambda e: e.tensor_copy(sb["xs"][:, 0:4 * CH], sb["y"][:, 0:4 * CH]),
                    waits=self.free["xs"] + self.y_state, inc=True)
        for d in range(4, 8):
            self.V(lambda e, d=d: e.tensor_copy(sb["xs"][:, d * CH + 1:(d + 1) * CH],
                                                sb["y"][:, d * CH:d * CH + CH - 1]))
        self.V(lambda e, l=l: e.tensor_copy(
            sb["xs"][:].rearrange("p (d m) -> p d m", d=8)[:, 4:8, 0:1],
            sb["acar"][:].rearrange("p (l d) -> p l d", l=NL)[:, l, :, None]))
        acv = self.V(lambda e, l=l: e.tensor_copy(
            sb["acar"][:].rearrange("p (l d) -> p l d", l=NL)[:, l, :],
            sb["y"][:].rearrange("p (d m) -> p d m", d=8)[:, 4:8, CH - 1]), inc=True)
        xs_done = [("v", acv)]
        self.free["xs"] = []

        wq3 = dr["wqkv"].rearrange("l (t p) m -> l p t m", p=128)[l]
        for which, (dst, ssbank, tabs, scl) in enumerate(
                (("qp", "C0", ("cs8b", "ss8b"), "qss"), ("kp", "C1", ("cik b", "sikb"), "kss"))):
            tabs = (tabs[0].replace(" ", ""), tabs[1])
            rqs_t = self.V(lambda e, scl=scl, l=l: e.tensor_scalar_mul(
                sb["rqs"][:], sb["rmat"][:], sb[scl][:, l:l + 1]),
                waits=[("t", self.tick("t"))], inc=True)
            asq_rd = [[], []]
            for dt in range(8):
                col0 = (which * 8 + dt) * 128
                cb, ct = self.stream_w(wq3[:, :, col0:col0 + 128], 1024, key=("qkv", l, col0))
                wsu = self.wsem_used
                bank = ("A0", "A1")[dt % 2]
                for d in range(8):
                    mmt = self.T(lambda e, bank=bank, d=d, cb=cb: e.matmul(
                        ps[bank][:], sb["wbf"][:, cb + d * 128: cb + (d + 1) * 128],
                        sb["xs"][:, d * CH:(d + 1) * CH],
                        start=(d == 0), stop=(d == 7), skip_group_check=True),
                        waits=([(wsu, ct)] + xs_done + self.bankfree[bank] +
                               (self.free[dst] if dt == 0 else [])) if d == 0 else (),
                        inc=(d == 7))
                self.wdone(cb, mmt)
                ev = self.V(lambda e, dst=dst, dt=dt, bank=bank: e.tensor_copy(
                    sb[dst][:, dt * CH:(dt + 1) * CH], ps[bank][:]),
                    waits=[("t", mmt)], inc=True)
                self.bankfree[bank] = [("v", ev)]
                sq = self.S(lambda e, dst=dst, dt=dt: e.activation(
                    sb["sqbf"][:, (dt % 2) * CH:(dt % 2 + 1) * CH], sb[dst][:, dt * CH:(dt + 1) * CH],
                    AF.Square), waits=[("v", ev)] + asq_rd[dt % 2], inc=True)
                cst = self.T(lambda e, dt=dt, ssbank=ssbank: e.matmul(
                    ps[ssbank][0:16, :], sb["l2ones"][:, dt * 16:(dt + 1) * 16],
                    sb["sqbf"][:, (dt % 2) * CH:(dt % 2 + 1) * CH],
                    start=(dt == 0), stop=(dt == 7), skip_group_check=True),
                    waits=[("s", sq)] + (self.bankfree[ssbank] if dt == 0 else []), inc=True)
                asq_rd[dt % 2] = [("t", cst)]
            mx = self.V(lambda e, ssbank=ssbank: e.tensor_single_scalar(
                sb["invt"][:, 0:512], ps[ssbank][0:16, :], 1e-24, op=ALU.max),
                waits=[("t", cst)], inc=True)
            self.bankfree[ssbank] = [("v", mx)]
            lnt_ = self.S(lambda e: e.activation(sb["invt"][:, 512:1024], sb["invt"][:, 0:512], AF.Ln),
                          waits=[("v", mx)], inc=True)
            ivt = self.S(lambda e: e.activation(sb["invtb"][:, :], sb["invt"][:, 512:1024],
                                                AF.Exp, scale=-0.5), waits=[("s", lnt_)], inc=True)
            rq_rd = [list(asq_rd[0]), list(asq_rd[1])]
            for dt in range(8):
                rqb = ("B0", "B1")[dt % 2]
                ivb = ("C0", "C1")[dt % 2]
                rq = self.T(lambda e, rqb=rqb, dst=dst, dt=dt: e.matmul(
                    ps[rqb][:], sb["rqs"][:], sb[dst][:, dt * CH:(dt + 1) * CH],
                    start=True, stop=True),
                    waits=([("v", rqs_t)] if dt == 0 else []) + self.bankfree[rqb], inc=True)
                iv = self.T(lambda e, ivb=ivb, dt=dt: e.matmul(
                    ps[ivb][:], sb["l2pat"][:, dt * 128:(dt + 1) * 128], sb["invtb"][:, :],
                    start=True, stop=True),
                    waits=[("s", ivt)] + self.bankfree[ivb], inc=True)
                rqc = self.S(lambda e, rqb=rqb, dt=dt: e.copy(
                    sb["sqbf"][:, (dt % 2) * CH:(dt % 2 + 1) * CH], ps[rqb][:]),
                    waits=[("t", rq)] + rq_rd[dt % 2], inc=True)
                self.bankfree[rqb] = [("s", rqc)]
                self.V(lambda e, dst=dst, dt=dt, tabs=tabs, ccol=ccol: e.tensor_tensor(
                    sb["rot"][:, 0:CH], sb[dst][:, dt * CH:(dt + 1) * CH],
                    sb[tabs[0]][:, ccol:ccol + CH], op=ALU.mult))
                t1s = self.V(lambda e, scl=scl, l=l: e.tensor_scalar_mul(
                    sb["rot"][:, 3 * CH:4 * CH], sb["rot"][:, 0:CH], sb[scl][:, l:l + 1]),
                    inc=True)
                t2r = self.V(lambda e, dt=dt, tabs=tabs, ccol=ccol: e.tensor_tensor(
                    sb["rot"][:, CH:2 * CH], sb["sqbf"][:, (dt % 2) * CH:(dt % 2 + 1) * CH],
                    sb[tabs[1]][:, ccol:ccol + CH], op=ALU.mult),
                    waits=[("s", rqc)], inc=True)
                rq_rd[dt % 2] = [("v", t2r)]
                t3 = self.V(lambda e: e.tensor_add(sb["rot"][:, 2 * CH:3 * CH],
                                                   sb["rot"][:, 3 * CH:4 * CH],
                                                   sb["rot"][:, CH:2 * CH]),
                            waits=[("v", t1s)], inc=True)
                qf = self.V(lambda e, dst=dst, dt=dt, ivb=ivb: e.tensor_tensor(
                    sb[dst][:, dt * CH:(dt + 1) * CH], sb["rot"][:, 2 * CH:3 * CH],
                    ps[ivb][:], op=ALU.mult), waits=[("t", iv), ("v", t3)], inc=True)
                self.bankfree[ivb] = [("v", qf)]
            if which == 0:
                qdone = [("v", self.tick("v"))]
            else:
                kdone = [("v", self.tick("v"))]
        for blk in range(8):
            cb, ct = self.stream_w(wq3[:, :, 2048 + blk * 128:2048 + (blk + 1) * 128], 1024, key=("qkv", l, 2048 + blk * 128))
            wsu = self.wsem_used
            bank = ("A0", "A1")[blk % 2]
            for d in range(8):
                mmt = self.T(lambda e, bank=bank, d=d, cb=cb: e.matmul(
                    ps[bank][:], sb["wbf"][:, cb + d * 128: cb + (d + 1) * 128],
                    sb["xs"][:, d * CH:(d + 1) * CH],
                    start=(d == 0), stop=(d == 7), skip_group_check=True),
                    waits=([(wsu, ct)] + self.bankfree[bank] +
                           (self.free["vfs"] + self.xt_free[0] + self.xt_free[1]
                            if blk == 0 else [])) if d == 0 else (),
                    inc=(d == 7))
            self.wdone(cb, mmt)
            ev = self.V(lambda e, blk=blk, bank=bank: e.tensor_copy(
                sb["vfs"][:, blk * CH:(blk + 1) * CH], ps[bank][:]),
                waits=[("t", mmt)], inc=True)
            self.bankfree[bank] = [("v", ev)]
        vfs_done = [("v", self.tick("v"))]
        for st in range(4):
            ptb = ("PT0", "PT1")[st % 2]
            for blk in range(8):
                tt = self.T(lambda e, ptb=ptb, st=st, blk=blk: e.transpose(
                    ps[ptb][:, blk * 128:(blk + 1) * 128],
                    sb["vfs"][:, blk * CH + st * 128: blk * CH + (st + 1) * 128],
                    sb["identb"][:]),
                    waits=(vfs_done + self.bankfree[ptb] + (self.free["vaug"] if (st == 0 and blk == 0) else [])) if blk == 0 else (),
                    inc=(blk == 7))
            ev = self.V(lambda e, ptb=ptb, st=st: e.tensor_copy(
                sb["vaug"][:, st * 1040:(st + 1) * 1040].rearrange("p (h q) -> p h q", q=65)[:, :, 0:64],
                ps[ptb][:].rearrange("p (h q) -> p h q", q=64)),
                waits=[("t", tt)], inc=True)
            self.bankfree[ptb] = [("v", ev)]
        vaug_done = [("v", self.tick("v"))]
        self.xt_free[0] = [("t", self.tick("t"))]
        self.xt_free[1] = [("t", self.tick("t"))]
        self.free["vfs"] = [("t", self.tick("t"))]
        return qdone, kdone, vaug_done

    def attn_core(self, s, c, l, qdone, kdone, vaug_done):
        sb, dr, ps = self.sb, self.dr, self.ps
        first = (c == 0)
        nmx_i = [0]
        pbuf_i = [0]
        pbuf_free = [[] for _ in range(4)]
        nmax_free = [[] for _ in range(8)]
        ptsb_free = [[], []]
        for w in (0, 1):
            for p in range(8):
                simb = {0: ("A0", "A1"), 1: ("B0", "B1")}[p % 2]
                ptb = ("PT0", "PT1")[p % 2]
                ptev = [None, None]
                for par in (0, 1):
                    bank = simb[par]
                    r0 = par * 64
                    for i in (0, 1):
                        segs = []
                        if w == 0:
                            kc = sb["kcar"][:].rearrange("p (L q m) -> p L q m", L=NL, q=8)
                            if i == 0:
                                segs.append((kc[r0:r0 + 64, l, p, 0:256], 0, 256))
                                segs.append((sb["kp"][r0:r0 + 64, p * CH: p * CH + 128], 256, 128))
                            else:
                                segs.append((kc[r0:r0 + 64, l, p, 128:256], 0, 128))
                                segs.append((sb["kp"][r0:r0 + 64, p * CH: p * CH + 256], 128, 256))
                        else:
                            j0 = i * 128
                            segs.append((sb["kp"][r0:r0 + 64, p * CH + j0: p * CH + j0 + 384], 0, 384))
                        lq = sb["qp"][r0:r0 + 64, p * CH + w * 256 + i * 128: p * CH + w * 256 + (i + 1) * 128]
                        w0waits = qdone + kdone + list(self.bankfree[bank])
                        mmt = None
                        for si, (rhs, o, n) in enumerate(segs):
                            mmt = self.T(lambda e, bank=bank, lq=lq, rhs=rhs, o=o, n=n, si=si: e.matmul(
                                ps[bank][:, o:o + n], lq, rhs, start=(si == 0), stop=False,
                                skip_group_check=True),
                                waits=w0waits if si == 0 else (), inc=False)
                        mi = (2 if (first and w == 0) else 0) + i
                        mmt = self.T(lambda e, bank=bank, mi=mi: e.matmul(
                            ps[bank][:, 0:384], sb["identb"][:], sb["masks"][:, mi * 384:(mi + 1) * 384],
                            start=False, stop=True, skip_group_check=True), inc=True)
                        nc_ = nmx_i[0] % 8
                        nmx_i[0] += 1
                        rmx = self.V(lambda e, bank=bank, nc_=nc_: e.reduce_max(
                            sb["nmax"][:, nc_:nc_ + 1], ps[bank][:, 0:384], axis=AX, negate=True),
                            waits=[("t", mmt)] + nmax_free[nc_], inc=True)
                        pb = pbuf_i[0] % 4
                        pbuf_i[0] += 1
                        ex = self.S(lambda e, bank=bank, nc_=nc_, pb=pb: e.activation(
                            sb["pbuf"][:, pb * CH: pb * CH + 384], ps[bank][:, 0:384], AF.Exp,
                            bias=sb["nmax"][:, nc_:nc_ + 1]),
                            waits=[("v", rmx)] + pbuf_free[pb], inc=True)
                        nmax_free[nc_] = [("s", ex)]
                        self.bankfree[bank] = [("s", ex)]
                        for jj in range(3):
                            jt = jj + i
                            tt = self.T(lambda e, ptb=ptb, pb=pb, jj=jj, jt=jt, i=i: e.transpose(
                                ps[ptb][:, jt * 256 + i * 128: jt * 256 + i * 128 + 128],
                                sb["pbuf"][:, pb * CH + jj * 128: pb * CH + (jj + 1) * 128],
                                sb["identb"][:]),
                                waits=([("s", ex)] + (self.bankfree[ptb] if (i == 0 and jj == 0) else [])),
                                inc=(jj == 2))
                        pbuf_free[pb] = [("t", tt)]
                    pev = self.V(lambda e, ptb=ptb, par=par: e.tensor_copy(
                        sb["ptsb"][:, par * 1024:(par + 1) * 1024], ps[ptb][:]),
                        waits=[("t", tt)] + ptsb_free[par], inc=True)
                    self.bankfree[ptb] = [("v", pev)]
                    ptev[par] = pev
                for par in (0, 1):
                    h = 2 * p + par
                    pvb = ("C0", "C1")[par]
                    vsrcs = []
                    for jt in range(4):
                        if w == 0 and jt < 2:
                            vc = sb["vcar"][:].rearrange("p (L q m) -> p L q m", L=NL, q=2)
                            vsrcs.append(vc[:, l, jt, h * 65:(h + 1) * 65])
                        else:
                            st = (jt - 2) if w == 0 else jt
                            vsrcs.append(sb["vaug"][:, st * 1040 + h * 65: st * 1040 + (h + 1) * 65])
                    mmt = None
                    order = (1, 2, 0, 3)
                    for oi, jt in enumerate(order):
                        ilo = 128 if jt == 3 else 0
                        ihi = 128 if jt == 0 else 256
                        mmt = self.T(lambda e, pvb=pvb, jt=jt, par=par, vs=vsrcs[jt], ilo=ilo, ihi=ihi,
                                     oi=oi: e.matmul(
                            ps[pvb][0:65, ilo:ihi], vs,
                            sb["ptsb"][:, par * 1024 + jt * 256 + ilo: par * 1024 + jt * 256 + ihi],
                            start=(oi == 0), stop=(oi == 3), skip_group_check=True),
                            waits=([("v", ptev[par])] + vaug_done + self.bankfree[pvb]) if oi == 0 else (),
                            inc=(oi == 3))
                    ptsb_free[par] = [("t", mmt)]
                    rssl = par
                    rc = self.V(lambda e, pvb=pvb, rssl=rssl: e.reciprocal(
                        sb["rs"][0:1, rssl * 256:(rssl + 1) * 256], ps[pvb][64:65, 0:256]),
                        waits=[("t", mmt)], inc=True)
                    rbt = self.T(lambda e, pvb=pvb, rssl=rssl: e.matmul(
                        ps[pvb][0:64, 256:512], sb["onesf"][0:1, 0:64],
                        sb["rs"][0:1, rssl * 256:(rssl + 1) * 256], start=True, stop=True,
                        skip_group_check=True),
                        waits=[("v", rc)], inc=True)
                    rbe = self.V(lambda e, pvb=pvb, rssl=rssl: e.tensor_copy(
                        sb["rb"][:, rssl * 256:(rssl + 1) * 256], ps[pvb][0:64, 256:512]),
                        waits=[("t", rbt)], inc=True)
                    at = self.V(lambda e, pvb=pvb, rssl=rssl, r0=par * 64, p=p, w=w: e.tensor_tensor(
                        sb["afm"][r0:r0 + 64, p * CH + w * 256: p * CH + (w + 1) * 256],
                        ps[pvb][0:64, 0:256], sb["rb"][:, rssl * 256:(rssl + 1) * 256], op=ALU.mult),
                        waits=[("v", rbe)] + (self.free["afm"] if (w == 0 and p == 0 and par == 0) else []),
                        inc=True)
                    self.bankfree[pvb] = [("v", at)]
            if w == 0:
                tnow = self.tick("t")
                kcv = self.V(lambda e, l=l: e.tensor_copy(
                    sb["kcar"][:].rearrange("p (L q m) -> p L q m", L=NL, q=8)[:, l],
                    sb["kp"][:].rearrange("p (q m) -> p q m", q=8)[:, :, 256:512]),
                    waits=[("t", tnow)], inc=True)
                vcv = self.V(lambda e, l=l: e.tensor_copy(
                    sb["vcar"][:].rearrange("p (L q m) -> p L q m", L=NL, q=2)[:, l],
                    sb["vaug"][:].rearrange("p (q m) -> p q m", q=4)[:, 2:4]),
                    waits=[("t", tnow)], inc=True)
        afm_done = [("v", self.tick("v"))]
        self.free["qp"] = [("t", self.tick("t"))]
        self.free["kp"] = [("t", self.tick("t"))]
        self.free["vaug"] = [("t", self.tick("t"))]
        self.free["xs"] = [("t", self.tick("t"))]
        wo3 = dr["wo_i"].rearrange("l (t p) m -> l p t m", p=128)[l]
        for dt in range(8):
            cb, ct = self.stream_w(wo3[:, :, dt * 128:(dt + 1) * 128], 1024, key=("woi", l, dt))
            wsu = self.wsem_used
            bank = ("A0", "A1")[dt % 2]
            for d in range(8):
                mmt = self.T(lambda e, bank=bank, d=d, cb=cb: e.matmul(
                    ps[bank][:], sb["wbf"][:, cb + d * 128: cb + (d + 1) * 128],
                    sb["afm"][:, d * CH:(d + 1) * CH],
                    start=(d == 0), stop=(d == 7), skip_group_check=True),
                    waits=([(wsu, ct)] + afm_done + self.bankfree[bank]) if d == 0 else (),
                    inc=(d == 7))
            self.wdone(cb, mmt)
            yv = self.V(lambda e, dt=dt, bank=bank: e.scalar_tensor_tensor(
                sb["y"][:, dt * CH:(dt + 1) * CH], ps[bank][:], 1.0,
                sb["y"][:, dt * CH:(dt + 1) * CH], op0=ALU.mult, op1=ALU.add),
                waits=[("t", mmt)], inc=True)
            self.bankfree[bank] = [("v", yv)]
        self.y_state = [("v", self.tick("v"))]
        self.free["afm"] = [("t", self.tick("t"))]
        self.free["regB"] = [("t", self.tick("t"))]

    def ffn(self, s, c, l):
        sb, dr, ps = self.sb, self.dr, self.ps
        x0 = self.V(lambda e: e.tensor_copy(sb["xs"][:, 0:4 * CH], sb["y"][:, 0:4 * CH]),
                    waits=self.free["xs"] + self.y_state, inc=True)
        for d in range(4, 8):
            self.V(lambda e, d=d: e.tensor_copy(sb["xs"][:, d * CH + 1:(d + 1) * CH],
                                                sb["y"][:, d * CH:d * CH + CH - 1]))
        xcp = self.V(lambda e, l=l: e.tensor_copy(
            sb["xs"][:].rearrange("p (d m) -> p d m", d=8)[:, 4:8, 0:1],
            sb["fcar"][:].rearrange("p (l d) -> p l d", l=NL)[:, l, :, None]), inc=True)
        xs_done = [("v", xcp)]
        cst = sst = None
        sq_rd = [[], []]
        SQV = False
        for d in range(8):
            if SQV:
                sq = self.V(lambda e, d=d: e.tensor_mul(
                    sb["sqbf"][:, (d % 2) * CH:(d % 2 + 1) * CH], sb["xs"][:, d * CH:(d + 1) * CH],
                    sb["xs"][:, d * CH:(d + 1) * CH]),
                    waits=(list(xs_done) if d == 0 else []) + sq_rd[d % 2], inc=True)
                sqw = [("v", sq)]
            else:
                sq = self.S(lambda e, d=d: e.activation(
                    sb["sqbf"][:, (d % 2) * CH:(d % 2 + 1) * CH], sb["xs"][:, d * CH:(d + 1) * CH],
                    AF.Square), waits=(list(xs_done) if d == 0 else []) + sq_rd[d % 2], inc=True)
                sqw = [("s", sq)]
            cst = self.T(lambda e, d=d: e.matmul(
                ps["C0"][0:1, :], sb["onesb"][:, 0:1], sb["xs"][:, d * CH:(d + 1) * CH],
                start=(d == 0), stop=(d == 7), skip_group_check=True),
                waits=(xs_done + self.bankfree["C0"]) if d == 0 else (), inc=True)
            sst = self.T(lambda e, d=d: e.matmul(
                ps["C1"][0:1, :], sb["onesb"][:, 0:1],
                sb["sqbf"][:, (d % 2) * CH:(d % 2 + 1) * CH],
                start=(d == 0), stop=(d == 7), skip_group_check=True),
                waits=sqw + (self.bankfree["C1"] if d == 0 else []), inc=True)
            sq_rd[d % 2] = [("t", sst)]
        mu = self.V(lambda e: e.tensor_scalar_mul(sb["lnt"][0:1, 0:512], ps["C0"][0:1, :], 1.0 / 1024.0),
                    waits=[("t", cst)], inc=True)
        m2 = self.V(lambda e: e.tensor_scalar_mul(sb["lnt"][0:1, 512:1024], ps["C1"][0:1, :], 1.0 / 1024.0),
                    waits=[("t", sst)], inc=True)
        mu2 = self.V(lambda e: e.tensor_mul(sb["lnt"][0:1, 1024:1536], sb["lnt"][0:1, 0:512],
                                            sb["lnt"][0:1, 0:512]), waits=[("v", mu)], inc=True)
        var = self.V(lambda e: e.tensor_sub(sb["lnt"][0:1, 1024:1536], sb["lnt"][0:1, 512:1024],
                                            sb["lnt"][0:1, 1024:1536]),
                     waits=[("v", mu2), ("v", m2)], inc=True)
        lnv = self.S(lambda e: e.activation(sb["lnt"][0:1, 512:1024], sb["lnt"][0:1, 1024:1536], AF.Ln,
                                            bias=self.cap[1e-5][0:1, 0:1]), waits=[("v", var)], inc=True)
        rstd = self.S(lambda e: e.activation(sb["lnt"][0:1, 1024:1536], sb["lnt"][0:1, 512:1024],
                                             AF.Exp, scale=-0.5), waits=[("s", lnv)], inc=True)
        nmr = self.V(lambda e: e.scalar_tensor_tensor(
            sb["lnt"][0:1, 1536:2048], sb["lnt"][0:1, 0:512], -1.0, sb["lnt"][0:1, 1024:1536],
            op0=ALU.mult, op1=ALU.mult), waits=[("s", rstd)], inc=True)
        rb_t = self.T(lambda e: e.matmul(ps["C0"][:], sb["onesf"][0:1, :], sb["lnt"][0:1, 1024:1536],
                                         start=True, stop=True, skip_group_check=True),
                      waits=[("v", nmr)], inc=True)
        nb_t = self.T(lambda e: e.matmul(ps["C1"][:], sb["onesf"][0:1, :], sb["lnt"][0:1, 1536:2048],
                                         start=True, stop=True, skip_group_check=True),
                      waits=[("v", nmr)], inc=True)
        for d in range(8):
            if d < 4:
                t1 = self.V(lambda e, d=d: e.tensor_tensor(
                    sb["otm"][:, 0:CH], sb["y"][:, d * CH:(d + 1) * CH], ps["C0"][:], op=ALU.mult),
                    waits=([("t", rb_t), ("t", nb_t)] + self.free["otm"]) if d == 0 else (), inc=True)
            else:
                t1c = self.V(lambda e, d=d, l=l: e.tensor_tensor(
                    sb["otm"][:, 0:1],
                    sb["fcar"][:].rearrange("p (l d) -> p l d", l=NL)[:, l, d - 4, None], ps["C0"][:, 0:1],
                    op=ALU.mult),
                    waits=(([("t", rb_t), ("t", nb_t)] + self.free["otm"]) if d == 4 else []), inc=True)
                t1 = self.V(lambda e, d=d: e.tensor_tensor(
                    sb["otm"][:, 1:CH], sb["y"][:, d * CH:d * CH + CH - 1], ps["C0"][:, 1:CH],
                    op=ALU.mult), inc=True)
                t2w = [("v", t1), ("v", t1c)]
            if d < 4:
                t2w = [("v", t1)]
            t2 = self.V(lambda e: e.tensor_tensor(sb["otm"][:, CH:2 * CH], sb["otm"][:, 0:CH],
                                                  ps["C1"][:], op=ALU.add), waits=t2w, inc=True)
            xv = self.V(lambda e, d=d, l=l: e.tensor_scalar(
                sb["xnb"][:, d * CH:(d + 1) * CH], sb["otm"][:, CH:2 * CH],
                sb["lngs"][:, l * 8 + d:l * 8 + d + 1], sb["lnbs"][:, l * 8 + d:l * 8 + d + 1],
                op0=ALU.mult, op1=ALU.add),
                waits=[("v", t2)] + (self.free["xnb"] if d == 0 else []), inc=True)
        self.V(lambda e, l=l: e.tensor_copy(
            sb["fcar"][:].rearrange("p (l d) -> p l d", l=NL)[:, l, :],
            sb["y"][:].rearrange("p (d m) -> p d m", d=8)[:, 4:8, CH - 1]), inc=True)
        xn_done = [("v", self.tick("v"))]
        self.bankfree["C0"] = [("v", self.tick("v")), ("s", self.tick("s"))]
        self.bankfree["C1"] = [("v", self.tick("v")), ("s", self.tick("s"))]
        self.free["otm"] = [("s", self.tick("s"))]
        w13 = dr["w1"].rearrange("l (t p) m -> l p t m", p=128)[l]
        gb = sb["regB"].bitcast(BF16)
        gel_free = [list(self.free["gel"]), list(self.free["gel"])]
        for ht in range(HT):
            cba, cta = self.stream_w(w13[:, :, ht * 128:(ht + 1) * 128], 1024, key=("w1", l, ht))
            wsua = self.wsem_used
            banka = ("A0", "A1")[ht % 2]
            for d in range(8):
                mma = self.T(lambda e, bank=banka, d=d, cb=cba: e.matmul(
                    ps[bank][:], sb["wbf"][:, cb + d * 128: cb + (d + 1) * 128],
                    sb["xnb"][:, d * CH:(d + 1) * CH],
                    start=(d == 0), stop=(d == 7), skip_group_check=True),
                    waits=([(wsua, cta)] + xn_done + self.bankfree[banka] +
                           (self.free["regB"] if ht == 0 else [])) if d == 0 else (),
                    inc=(d == 7))
            self.wdone(cba, mma)
            cbg, ctg = self.stream_w(w13[:, :, HID + ht * 128:HID + (ht + 1) * 128], 1024, key=("w1g", l, ht))
            wsug = self.wsem_used
            bankg = ("B0", "B1")[ht % 2]
            for d in range(8):
                mmg = self.T(lambda e, bank=bankg, d=d, cb=cbg: e.matmul(
                    ps[bank][:], sb["wbf"][:, cb + d * 128: cb + (d + 1) * 128],
                    sb["xnb"][:, d * CH:(d + 1) * CH],
                    start=(d == 0), stop=(d == 7), skip_group_check=True),
                    waits=([(wsug, ctg)] + self.bankfree[bankg]) if d == 0 else (),
                    inc=(d == 7))
            self.wdone(cbg, mmg)
            gl = self.S(lambda e, bank=bankg, ht=ht, l=l: e.activation(
                sb["gel"][:, (ht % 2) * CH:(ht % 2 + 1) * CH], ps[bank][:], AF.Gelu,
                bias=sb["b1s"][:, l * 2 * HT + HT + ht:l * 2 * HT + HT + ht + 1]),
                waits=[("t", mmg)] + gel_free[ht % 2], inc=True)
            self.bankfree[bankg] = [("s", gl)]
            gv = self.V(lambda e, bank=banka, ht=ht, l=l: e.scalar_tensor_tensor(
                gb[:, ht * CH:(ht + 1) * CH], ps[bank][:],
                sb["b1s"][:, l * 2 * HT + ht:l * 2 * HT + ht + 1],
                sb["gel"][:, (ht % 2) * CH:(ht % 2 + 1) * CH], op0=ALU.add, op1=ALU.mult),
                waits=[("t", mma), ("s", gl)], inc=True)
            gel_free[ht % 2] = [("v", gv)]
            self.bankfree[banka] = [("v", gv)]
        gated_done = [("v", self.tick("v"))]
        w23 = dr["w2"].rearrange("l (k p) m -> l p k m", p=128)[l]
        for dt in range(8):
            mmt = None
            for half in range(2):
                cb, ct = self.stream_w(w23[:, :, dt * 128:(dt + 1) * 128][:, half * 11:(half + 1) * 11],
                                       11 * 128, key=("w2", l, dt, half))
                wsu = self.wsem_used
                for k in range(11):
                    kk = half * 11 + k
                    mmt = self.T(lambda e, dt=dt, k=k, kk=kk, cb=cb: e.matmul(
                        ps[("A0", "A1")[dt % 2]][:], sb["wbf"][:, cb + k * 128: cb + (k + 1) * 128],
                        gb[:, kk * CH:(kk + 1) * CH],
                        start=(kk == 0), stop=(kk == 21), skip_group_check=True),
                        waits=([(wsu, ct)] + gated_done +
                               (self.bankfree[("A0", "A1")[dt % 2]] if kk == 0 else [])) if k == 0 else (),
                        inc=(k == 10))
                self.wdone(cb, mmt)
            yv = self.V(lambda e, dt=dt: e.scalar_tensor_tensor(
                sb["y"][:, dt * CH:(dt + 1) * CH], ps[("A0", "A1")[dt % 2]][:],
                sb["b2s"][:, l * 8 + dt:l * 8 + dt + 1],
                sb["y"][:, dt * CH:(dt + 1) * CH], op0=ALU.add, op1=ALU.add),
                waits=[("t", mmt)], inc=True)
            self.bankfree[("A0", "A1")[dt % 2]] = [("v", yv)]
        self.y_state = [("v", self.tick("v"))]
        self.free["xs"] = [("t", self.tick("t"))]
        self.free["xnb"] = [("t", self.tick("t"))]
        self.free["regB"] = [("t", self.tick("t"))]
        self.free["afm"] = [("t", self.tick("t"))]
        self.free["gel"] = [("t", self.tick("t"))]

    def chunk_end(self, s, c):

        sb, dr = self.sb, self.dr
        gt = self.G(lambda e, s=s, c=c: e.dma_start(
            out=dr["ysp"].rearrange("(k d p) m -> k p d m", d=8, p=128)[s * NCH + c],
            in_=sb["y"][:].rearrange("p (d m) -> p d m", d=8)),
            waits=self.y_state, sem="dspy")
        self.free["y"] = [("dspy", gt)]

    def output(self, s):
        OSTAGE = 9
        sb, dr, ps = self.sb, self.dr, self.ps
        k1 = None
        if False:
            pass
        else:
          for h in range(H):
            par, p = h % 2, h // 2
            k1 = self.S(lambda e, par=par, p=p, h=h: e.activation(
                sb["kvb2"][par * 64:(par + 1) * 64, p * 64:(p + 1) * 64],
                sb["kv"][:, h * 64:(h + 1) * 64], AF.Copy, scale=0.125),
                waits=[("v", self.kv_tick)] if h == 0 else (), inc=(h == H - 1))
          kvb_done = [("s", k1)]
        woc = dr["woT"].rearrange("(t p) m -> p t m", p=128)
        wqc = dr["wqT"].rearrange("(t p) m -> p t m", p=128)
        for c in range(self.cfg.get("nchunks", NCH)):
            if True:
                gx = self.G(lambda e, s=s, c=c: e.dma_start(
                    out=sb["xnb"][:].rearrange("p (d m) -> p d m", d=8),
                    in_=dr["xnsp"].rearrange("(k d p) m -> k p d m", d=8, p=128)[s * NCH + c]),
                    waits=self.free["xnb"] + [("dspx", self.tick("dspx"))], sem="dgx")
            else:
                gx = 0
            if True:
                gy = self.G(lambda e, s=s, c=c: e.dma_start(
                    out=sb["y"][:].rearrange("p (d m) -> p d m", d=8),
                    in_=dr["ysp"].rearrange("(k d p) m -> k p d m", d=8, p=128)[s * NCH + c]),
                    waits=self.free["y"] + [("dspy", self.tick("dspy"))], sem="dgy")
            else:
                gy = 0
            if OSTAGE < 2:
                continue
            for dt in range(8):
                cb, ct = self.stream_w(wqc[:, :, dt * 128:(dt + 1) * 128], 1024, key=("wq", dt))
                wsu = self.wsem_used
                bank = ("A0", "A1")[dt % 2]
                for d in range(8):
                    mmt = self.T(lambda e, bank=bank, d=d, cb=cb: e.matmul(
                        ps[bank][:], sb["wbf"][:, cb + d * 128: cb + (d + 1) * 128],
                        sb["xnb"][:, d * CH:(d + 1) * CH],
                        start=(d == 0), stop=(d == 7), skip_group_check=True),
                        waits=([(wsu, ct), ("dgx", gx)] + self.bankfree[bank] +
                               (self.free["qp"] if dt == 0 else [])) if d == 0 else (),
                        inc=(d == 7))
                self.wdone(cb, mmt)
                ev = self.V(lambda e, dt=dt, bank=bank: e.tensor_copy(
                    sb["qp"][:, dt * CH:(dt + 1) * CH], ps[bank][:]),
                    waits=[("t", mmt)], inc=True)
                self.bankfree[bank] = [("v", ev)]
            qo_done = [("v", self.tick("v"))]
            if OSTAGE < 3:
                continue
            for p in range(8):
                for par in (0, 1):
                    bank = ("B0", "B1")[par]
                    r0 = par * 64
                    mmt = self.T(lambda e, bank=bank, p=p, r0=r0: e.matmul(
                        ps[bank][0:64, :], sb["kvb2"][r0:r0 + 64, p * 64:(p + 1) * 64],
                        sb["qp"][r0:r0 + 64, p * CH:(p + 1) * CH], start=True, stop=True,
                        skip_group_check=True),
                        waits=kvb_done + qo_done + self.bankfree[bank] +
                              (self.free["xs"] if (p == 0 and par == 0) else []),
                        inc=True)
                    ev = self.V(lambda e, bank=bank, p=p, r0=r0: e.tensor_copy(
                        sb["xs"][r0:r0 + 64, p * CH:(p + 1) * CH], ps[bank][0:64, :]),
                        waits=[("t", mmt)], inc=True)
                    self.bankfree[bank] = [("v", ev)]
            sp_done = [("v", self.tick("v"))]
            if OSTAGE < 4:
                continue
            for t in range(4):
                for bi, bank in enumerate(("A0", "A1")):
                    tt = None
                    for d in range(4 * bi, 4 * bi + 4):
                        tt = self.T(lambda e, bank=bank, d=d, t=t: e.transpose(
                            ps[bank][:, (d % 4) * 128:((d % 4) + 1) * 128],
                            sb["y"][:, d * CH + t * 128: d * CH + (t + 1) * 128], sb["identf"][:]),
                            waits=([("dgy", gy)] + self.bankfree[bank]) if d % 4 == 0 else (),
                            inc=(d % 4 == 3))
                    gv = self.S(lambda e, bank=bank, bi=bi, t=t: e.activation(
                        sb["gtm"][:, t * 1024 + bi * 512: t * 1024 + (bi + 1) * 512],
                        ps[bank][:], AF.Silu),
                        waits=[("t", tt)] + (self.free["gtm"] + self.free["regB"]
                                             if (t == 0 and bi == 0) else []), inc=True)
                    self.bankfree[bank] = [("s", gv)]
            g_done = [("s", self.tick("s"))]
            base = s * S + c * CH
            st_dg = []
            for t in range(4):
                dx = self.Sy(lambda e, t=t, base=base: e.dma_start(
                    out=sb["xt"][:, (t % 2) * 1024:(t % 2 + 1) * 1024],
                    in_=dr["xin"][base + t * 128: base + (t + 1) * 128, :]),
                    waits=self.xt_free[t % 2], sem=("dxA", "dxB")[t % 2])
                for half in range(2):
                    bank = ("B0", "B1")[half]
                    for blk in range(4 * half, 4 * half + 4):
                        cb, ct = self.stream_w(woc[:, :, blk * 128:(blk + 1) * 128], 1024, key=("wo", blk))
                        wsu = self.wsem_used
                        for p in range(8):
                            mmt = self.T(lambda e, bank=bank, p=p, t=t, blk=blk, cb=cb: e.matmul(
                                ps[bank][:, (blk % 4) * 128:(blk % 4 + 1) * 128],
                                sb["xs"][:, p * CH + t * 128: p * CH + (t + 1) * 128],
                                sb["wbf"][:, cb + p * 128: cb + (p + 1) * 128],
                                start=(p == 0), stop=(p == 7), skip_group_check=True),
                                waits=([(wsu, ct)] + sp_done +
                                       (self.bankfree[bank] if blk % 4 == 0 else [])) if p == 0 else (),
                                inc=(p == 7))
                        self.wdone(cb, mmt)
                    c0 = half * 512
                    t1 = self.V(lambda e, bank=bank, t=t, c0=c0: e.tensor_tensor(
                        sb["otm"][:, c0:c0 + 512], ps[bank][:],
                        sb["gtm"][:, t * 1024 + c0: t * 1024 + c0 + 512], op=ALU.mult),
                        waits=[("t", mmt)] + g_done + (self.free["otm"] if half == 0 else []),
                        inc=True)
                    t2 = self.V(lambda e, t=t, c0=c0: e.tensor_tensor(
                        sb["otm"][:, c0:c0 + 512], sb["otm"][:, c0:c0 + 512],
                        sb["xt"][:, (t % 2) * 1024 + c0: (t % 2) * 1024 + c0 + 512], op=ALU.add),
                        waits=[(("dxA", "dxB")[t % 2], dx), ("v", t1)], inc=True)
                    self.bankfree[bank] = [("v", t2)]
                gs = self.G(lambda e, t=t, base=base: e.dma_start(
                    out=dr["yout"][base + t * 128: base + (t + 1) * 128, :], in_=sb["otm"][:, 0:1024]),
                    waits=[("v", self.tick("v"))], sem="dgo")
                st_dg.append(gs)
                self.xt_free[t % 2] = [("v", self.tick("v"))]
                self.free["otm"] = [("dgo", gs)]
            self.free["xs"] = [("t", self.tick("t"))]
            self.free["gtm"] = [("v", self.tick("v"))]
            self.free["regB"] = [("v", self.tick("v"))]
            self.free["xnb"] = [("t", self.tick("t"))]
            self.free["y"] = [("t", self.tick("t"))]
            self.free["qp"] = [("t", self.tick("t"))]
        self.out_done = [("dgo", self.tick("dgo"))]

    TAPS = {
        "y": ("y", F32), "xnb": ("xnb", BF16), "qp": ("qp", BF16), "kp": ("kp", BF16),
        "xs": ("xs", BF16), "vaug": ("vaug", BF16), "afm": ("afm", BF16),
        "kv": ("kv", F32), "ptsb": ("ptsb", BF16), "gtm": ("gtm", F32),
        "stats": ("stats", F32), "xt": ("xt", F32), "gated": ("afm", BF16),
        "small": ("small", F32), "otm": ("otm", F32), "kvb2": ("kvb2", BF16),
    }

    def emit_taps(self):
        names = self.cfg.get("taps", ())
        if not names:
            return
        sb = self.sb
        fw = []
        vq = self.V(lambda e: e.tensor_copy(sb["stats"][:, 0:1], sb["stats"][:, 1:2]), inc=True)
        sq = self.S(lambda e: e.copy(sb["stats"][:, 2:3], sb["stats"][:, 3:4]), inc=True)
        fw = [("v", vq), ("s", sq), ("t", self.tick("t"))]
        caps = {"afm": 8 * CH, "gtm": 4096, "gel": 1024, "gated": HT * CH}
        for n in names:
            src, dt = self.TAPS[n]
            hh = self.sb[src]
            shape = list(hh.shape)
            if n in caps:
                shape = [shape[0], caps[n]]
            tap = self.nc.dram_tensor("tap_" + n, shape, dt, kind="ExternalOutput").ap()
            self.G(lambda e, tap=tap, hh=hh, shape=shape: e.dma_start(
                out=tap[:, :], in_=hh[0:shape[0], 0:shape[1]]),
                   waits=fw, sem="dgo")

    def emit_phases(self):
        self.y_state = []
        self.kv_tick = 0
        nl = self.cfg.get("nlayers", NL)
        for s in range(self.cfg.get("nseq", NSEQ)):
            self.seq_init(s)
            for c in range(self.cfg.get("nchunks", NCH)):
                self.phase1(s, c)
                self.y_state = list(self.y_ready)
                for l in range(nl):
                    qd, kd, vd = self.attn(s, c, l)
                    self.attn_core(s, c, l, qd, kd, vd)
                    if not self.cfg.get("attn_only", False):
                        self.ffn(s, c, l)
                if self.cfg.get("output", True):
                    self.chunk_end(s, c)
            if self.cfg.get("output", True):
                self.output(s)

    def build(self):
        self.declare_dram()
        with self.es:
            self.alloc()
            self.init_stream()
            self.prologue()
            saved_q = {k: list(v) for k, v in self.P.q.items()}
            saved_semc = dict(self.P.semc)
            saved_pending = {k: list(v) for k, v in getattr(self.P, "pending", {e: [] for e in self.P.ENGINES}).items()}
            saved_free = {k: list(v) for k, v in self.free.items()}
            saved_bank = {k: list(v) for k, v in self.bankfree.items()}
            self.collect = True
            self.emit_phases()
            self.collect = False
            self.P.q = saved_q
            self.P.semc = saved_semc
            self.P.pending = saved_pending
            self.free = saved_free
            self.bankfree = saved_bank
            self.xt_free = [[], []]
            self.xnt_free = [[], []]
            self.wslot = 0
            self.wfree = [[], []]
            self.emit_weight_cache()
            self.emit_phases()
            self.emit_taps()
            final = self.tick("dgo")
            if final:
                self.P.op("sync", lambda e: None, waits=[("dgo", final)])
            block = self.es.enter_context(self.nc.Block())
            self.P.finalize(block)
        return self.nc


def build_core(cfg):
    core = Core(cfg)
    nc = core.build()
    return nc


_NCORES = 8


def kernel(**inputs):
    from concourse.bass_utils import run_bass_kernel_spmd
    x = np.ascontiguousarray(np.asarray(inputs["L0_attn_x"], dtype=np.float32))
    B = x.shape[0]
    consts = make_consts()
    w = make_weights({k: np.asarray(v) for k, v in inputs.items()})
    base = {}
    for k in ("wqT", "wkT", "wvT", "woT", "wqkv", "wo_i", "w1", "w2",
              "b1", "b2", "lng", "lnb", "qs", "ks"):
        base[k] = w[k]
    for k in ("cs8", "ss8", "cik", "sik"):
        base[k] = consts[k]
    base["masks"] = np.ascontiguousarray(
        np.concatenate([consts["mask_n0"], consts["mask_n1"], consts["mask_f0"], consts["mask_f1"]], 1))
    base["identb"] = consts["ident_b"]
    base["identf"] = consts["ident_f"]
    base["rmat"] = consts["rmat"]
    base["l2ones"] = consts["l2ones"]
    base["l2pat"] = consts["l2pat"]
    base["onesb"] = consts["ones"]
    cfg = {"nseq": NSEQ, "nchunks": NCH, "nlayers": NL, "output": True, "taps": ()}
    nc = build_core(cfg)
    in_maps = []
    per = B // _NCORES
    for ci in range(_NCORES):
        m = dict(base)
        m["xin"] = np.ascontiguousarray(x[ci * per:(ci + 1) * per].reshape(per * S, D))
        in_maps.append(m)
    res = run_bass_kernel_spmd(nc, in_maps, core_ids=list(range(_NCORES)))
    out = np.empty((B, S, D), dtype=np.float32)
    for ci in range(_NCORES):
        out[ci * per:(ci + 1) * per] = np.asarray(
            res.results[ci]["yout"], dtype=np.float32).reshape(per, S, D)
    return out


# version 04
# speedup vs baseline: 1.2914x; 1.1237x over previous
import numpy as np
import ml_dtypes
import contextlib
import concourse.bass as bass
import concourse.mybir as mybir

F32 = mybir.dt.float32
BF16 = mybir.dt.bfloat16
AF = mybir.ActivationFunctionType
ALU = mybir.AluOpType
AX = mybir.AxisListType.X

D = 1024
CH = 512
S = 2048
NCH = S // CH
NSEQ = 2
H = 16
DH = 64
W = 256
NL = 6
HID = 2816
HT = HID // 128
NEG = -1.0e30

bf = ml_dtypes.bfloat16


def make_consts():
    c = {}
    c["ident_b"] = np.eye(128, dtype=bf)
    c["ident_f"] = np.eye(128, dtype=np.float32)
    R = np.zeros((128, 128), dtype=np.float64)
    for h0 in (0, 64):
        for m in range(32):
            R[h0 + m + 32, h0 + m] = -1.0
            R[h0 + m, h0 + m + 32] = 1.0
    c["rmat"] = R.astype(bf)
    lo = np.zeros((128, 8, 16), dtype=np.float64)
    for f in range(8):
        lo[0:64, f, 2 * f] = 1.0
        lo[64:128, f, 2 * f + 1] = 1.0
    c["l2ones"] = lo.reshape(128, 128).astype(bf)
    pat = np.zeros((16, 8, 128), dtype=np.float64)
    for f in range(8):
        pat[2 * f, f, 0:64] = 1.0
        pat[2 * f + 1, f, 64:128] = 1.0
    c["l2pat"] = pat.reshape(16, 1024).astype(bf)
    c["ones"] = np.ones((128, 128), dtype=bf)
    half = np.arange(0, DH, 2, dtype=np.float64)
    inv_freq = 1.0 / 10000.0 ** (half / DH)
    t = np.arange(S, dtype=np.float64)
    freqs = t[:, None] * inv_freq[None, :]
    freqs = np.concatenate([freqs, freqs], -1)
    base_scale = (half + 0.4 * DH) / (1.4 * DH)
    power = (t - S // 2) / (W // 2)
    scale = base_scale[None, :] ** power[:, None]
    scale = np.concatenate([scale, scale], -1)
    cos, sin = np.cos(freqs), np.sin(freqs)
    qk_scale = 8.0
    cs8 = (cos * scale * qk_scale).T
    ss8 = (sin * scale * qk_scale).T
    cik = (cos / scale).T
    sik = (sin / scale).T
    for name, tab in (("cs8", cs8), ("ss8", ss8), ("cik", cik), ("sik", sik)):
        c[name] = np.concatenate([tab, tab], 0).astype(np.float32)
    def mask_tile(i_base, j_base, first):
        m = np.zeros((128, 384), dtype=np.float64)
        for ii in range(128):
            i = i_base + ii
            for jj in range(384):
                j = j_base + jj
                valid = (i <= j <= i + W) and not (first and j < 256)
                if not valid:
                    m[ii, jj] = NEG
        return m.astype(bf)
    c["mask_n0"] = mask_tile(0, 0, False)
    c["mask_n1"] = mask_tile(128, 128, False)
    c["mask_f0"] = mask_tile(0, 0, True)
    c["mask_f1"] = mask_tile(128, 128, True)
    return c


def make_weights(iw):
    w = {}
    w["wqT"] = np.ascontiguousarray(iw["L0_attn_W_q"].T)
    w["wkT"] = np.ascontiguousarray(iw["L0_attn_W_k"].T)
    w["wvT"] = np.ascontiguousarray(iw["L0_attn_W_v"].T)
    w["woT"] = np.ascontiguousarray(iw["L0_attn_W_o"].T)
    w["wqkv"] = np.ascontiguousarray(iw["L0_ffn_Wqkv"])
    w["wo_i"] = np.ascontiguousarray(iw["L0_ffn_Wo"])
    W1 = iw["L0_ffn_W1"]
    w1p = np.zeros((NL, D, 2 * HID), dtype=np.float32)
    w1p[:, :, 0:2730] = W1[:, :, 0:2730]
    w1p[:, :, HID:HID + 2730] = W1[:, :, 2730:5460]
    w["w1"] = w1p
    b1 = iw["L0_ffn_b1"]
    b1p = np.zeros((NL, 2 * HID), dtype=np.float32)
    b1p[:, 0:2730] = b1[:, 0:2730]
    b1p[:, HID:HID + 2730] = b1[:, 2730:5460]
    w["b1"] = np.ascontiguousarray(b1p.reshape(NL, 2 * HT, 128).transpose(2, 0, 1).reshape(128, NL * 2 * HT))
    W2 = iw["L0_ffn_W2"]
    w2p = np.zeros((NL, HID, D), dtype=np.float32)
    w2p[:, 0:2730, :] = W2
    w["w2"] = w2p
    def pp(v):
        return np.ascontiguousarray(v.reshape(NL, 8, 128).transpose(2, 0, 1).reshape(128, NL * 8))
    w["b2"] = pp(iw["L0_ffn_b2"])
    w["lng"] = pp(iw["L0_ffn_ln_g"])
    w["lnb"] = pp(iw["L0_ffn_ln_b"])
    w["qs"] = np.ascontiguousarray(np.tile(iw["L0_ffn_q_scale"], (1, 2)).T)
    w["ks"] = np.ascontiguousarray(np.tile(iw["L0_ffn_k_scale"], (1, 2)).T)
    return {k: np.asarray(v, dtype=(bf if v.dtype == bf else np.float32)) for k, v in w.items()}


class Prog:
    ENGINES = ("sync", "tensor", "vector", "scalar", "gpsimd")

    def __init__(self, nc, es):
        self.nc = nc
        self.es = es
        self.q = {e: [] for e in self.ENGINES}
        self.semc = {}
        self.sems = {}

    def sem(self, name):
        h = self.es.enter_context(self.nc.semaphore(name))
        self.semc[name] = 0
        self.sems[name] = h
        return name

    def tick(self, name):
        return self.semc[name]

    DEFER = ("vector", "scalar")

    def set_spacer(self, eng, fn):
        self.spacer_fn = getattr(self, "spacer_fn", {})
        self.spacer_fn[eng] = fn

    def _emit_spacer(self, eng, carry):
        fn = self.spacer_fn[eng]

        def cb(e, fn=fn, carry=carry):
            ins = fn(e)
            for h, amt in carry:
                ins = ins.then_inc(h, amt)
        self.q[eng].append(cb)

    def op(self, eng, fn, waits=(), inc=None, amt=1):
        for s, v in waits:
            assert v <= self.semc[s], f"wait-ahead-of-inc on {s}: {v} > {self.semc[s]}"
        self.pending = getattr(self, "pending", {e: [] for e in self.ENGINES})
        waits_h = [(self.sems[s], v) for s, v in waits if v > 0]
        defer = eng in self.DEFER
        if defer and self.pending[eng] and waits_h:
            self._emit_spacer(eng, self.pending[eng])
            self.pending[eng] = []
        carry = list(self.pending[eng]) if defer else []
        if defer:
            self.pending[eng] = []
        cap_inc = None
        if inc is not None:
            self.semc[inc] += amt
            if not defer:
                cap_inc = (self.sems[inc], amt)

        def cb(e, fn=fn, waits_h=waits_h, cap_inc=cap_inc, carry=carry):
            for h, v in waits_h:
                e.wait_ge(h, v)
            ins = fn(e)
            for h, amt2 in carry:
                assert ins is not None, "carry needs a real instruction"
                ins = ins.then_inc(h, amt2)
            if cap_inc is not None:
                assert ins is not None, "inc requested but fn returned None"
                ins.then_inc(*cap_inc)
        self.q[eng].append(cb)
        if inc is not None and defer:
            self.pending[eng].append((self.sems[inc], amt))
        return self.semc[inc] if inc is not None else None

    def finalize(self, block):
        for eng in self.DEFER:
            if getattr(self, "pending", {}).get(eng):
                self._emit_spacer(eng, self.pending[eng])
                self.pending[eng] = []
        for name in self.ENGINES:
            cbs = self.q[name]
            if not cbs:
                continue
            def section(e, cbs=cbs):
                for cb in cbs:
                    cb(e)
            getattr(block, name)(section)


class Core:

    def __init__(self, cfg):
        self.cfg = cfg
        self.nc = bass.Bass("TRN2", target_bir_lowering=False)
        self.es = contextlib.ExitStack()
        self.taps = {}

    def declare_dram(self):
        nc = self.nc
        dr = {}
        dr["xin"] = nc.dram_tensor("xin", [NSEQ * S, D], F32, kind="ExternalInput").ap()
        dr["yout"] = nc.dram_tensor("yout", [NSEQ * S, D], F32, kind="ExternalOutput").ap()
        for n, shp in (("wqT", [D, D]), ("wkT", [D, D]), ("wvT", [D, D]), ("woT", [D, D])):
            dr[n] = nc.dram_tensor(n, shp, F32, kind="ExternalInput").ap()
        dr["wqkv"] = nc.dram_tensor("wqkv", [NL, D, 3072], F32, kind="ExternalInput").ap()
        dr["wo_i"] = nc.dram_tensor("wo_i", [NL, D, D], F32, kind="ExternalInput").ap()
        dr["w1"] = nc.dram_tensor("w1", [NL, D, 2 * HID], F32, kind="ExternalInput").ap()
        dr["w2"] = nc.dram_tensor("w2", [NL, HID, D], F32, kind="ExternalInput").ap()
        for n, cc in (("b1", NL * 2 * HT), ("b2", NL * 8), ("lng", NL * 8), ("lnb", NL * 8),
                      ("qs", NL), ("ks", NL)):
            dr[n] = nc.dram_tensor(n, [128, cc], F32, kind="ExternalInput").ap()
        for n in ("cs8", "ss8", "cik", "sik"):
            dr[n] = nc.dram_tensor(n, [128, S], F32, kind="ExternalInput").ap()
        dr["masks"] = nc.dram_tensor("masks", [128, 4 * 384], BF16, kind="ExternalInput").ap()
        dr["identb"] = nc.dram_tensor("identb", [128, 128], BF16, kind="ExternalInput").ap()
        dr["identf"] = nc.dram_tensor("identf", [128, 128], F32, kind="ExternalInput").ap()
        dr["rmat"] = nc.dram_tensor("rmat", [128, 128], BF16, kind="ExternalInput").ap()
        dr["l2ones"] = nc.dram_tensor("l2ones", [128, 128], BF16, kind="ExternalInput").ap()
        dr["l2pat"] = nc.dram_tensor("l2pat", [16, 1024], BF16, kind="ExternalInput").ap()
        dr["onesb"] = nc.dram_tensor("onesb", [128, 128], BF16, kind="ExternalInput").ap()
        dr["wbfd"] = nc.dram_tensor("wbfd", [128, self.cfg.get("wcols", 700000)], BF16, kind="Internal").ap()
        dr["xnsp"] = nc.dram_tensor("xnsp", [NSEQ * NCH * 8 * 128, CH], BF16, kind="Internal").ap()
        dr["ysp"] = nc.dram_tensor("ysp", [NSEQ * NCH * 8 * 128, CH], F32, kind="Internal").ap()
        self.dr = dr

    def alloc(self):
        nc, es = self.nc, self.es
        sb = {}

        def t(name, shape, dt):
            sb[name] = es.enter_context(nc.sbuf_tensor("s_" + name, shape, dt))
            return sb[name]

        t("y", [128, 8 * CH], F32)
        t("kcar", [128, NL * 8 * 256], BF16)
        t("vcar", [128, NL * 2 * 1040], BF16)
        t("acar", [128, NL * 4], F32)
        t("fcar", [128, NL * 4], F32)
        for n in ("cs8", "ss8", "cik", "sik"):
            t(n + "b", [128, S], BF16)
        t("masks", [128, 4 * 384], BF16)
        t("identb", [128, 128], BF16)
        t("identf", [128, 128], F32)
        t("rmat", [128, 128], BF16)
        t("l2ones", [128, 128], BF16)
        t("l2pat", [16, 1024], BF16)
        t("onesb", [128, 128], BF16)
        t("consts", [128, 8], F32)
        t("b1s", [128, NL * 2 * HT], F32)
        t("b2s", [128, NL * 8], F32)
        t("lngs", [128, NL * 8], F32)
        t("lnbs", [128, NL * 8], F32)
        t("qss", [128, NL], F32)
        t("kss", [128, NL], F32)
        t("kv", [64, 1024], F32)
        t("kvb2", [128, 512], BF16)
        t("xt", [128, 2 * 1024], F32)
        t("regB", [128, 22 * CH // 2], F32)
        t("xs", [128, 8 * CH], BF16)
        t("xnb", [128, 8 * CH], BF16)
        t("qp", [128, 8 * CH], BF16)
        t("kp", [128, 8 * CH], BF16)
        t("sqbf", [128, 2 * CH], BF16)
        t("rot", [128, 4 * CH], BF16)
        t("vaug", [128, 4 * 1040], BF16)
        t("pbuf", [128, 4 * CH], BF16)
        t("ptsb", [128, 2 * 1024], BF16)
        t("wsg", [128, 2 * 1408], F32)
        t("wbf", [128, 2 * 1408], BF16)
        t("stats", [128, 32], F32)
        t("nmax", [128, 8], F32)
        t("small", [64, 2048], F32)
        sb["invt"] = sb["small"][0:16, 0:1024]
        sb["lnt"] = sb["small"][0:1, 0:2048]
        sb["rs"] = sb["small"][0:1, 0:512]
        sb["rb"] = sb["small"][0:64, 512:1024]
        sb["invtb"] = sb["small"].bitcast(BF16)[0:16, 2048:2560]
        t("otm", [128, 1024], F32)
        sb["gtm"] = sb["regB"]
        sb["gel"] = sb["pbuf"][:, 0:1024]
        sb["afm"] = sb["regB"].bitcast(BF16)
        sb["vfs"] = sb["xt"].bitcast(BF16)
        self.sb = sb

        ps = {}
        for n in ("A0", "A1", "B0", "B1", "C0", "C1"):
            ps[n] = es.enter_context(nc.psum_tensor(n, [128, 512], F32))
        ps["PT0"] = es.enter_context(nc.psum_tensor("PT0", [128, 1024], BF16))
        ps["PT1"] = es.enter_context(nc.psum_tensor("PT1", [128, 1024], BF16))
        self.ps = ps

        t("spc", [1, 8], F32)
        t("onesf", [1, 128], F32)
        t("rqs", [128, 128], BF16)
        self.P = Prog(nc, es)
        self.P.set_spacer("vector", lambda e: e.tensor_copy(sb["spc"][0:1, 0:1], sb["spc"][0:1, 1:2]))
        self.P.set_spacer("scalar", lambda e: e.copy(sb["spc"][0:1, 2:3], sb["spc"][0:1, 3:4]))
        for s in ("dwA", "dwB", "dwC", "dwD", "dwE", "dwF", "dxA", "dxB", "dco",
                  "dspy", "dspx", "dgx", "dgy", "dgo", "dgw", "t", "v", "s"):
            self.P.sem(s)
        self.bankfree = {n: [] for n in ps}
        self.free = {n: [] for n in ("xnb", "y", "qp", "kp", "vaug", "afm", "xs", "regB",
                                     "qbf", "sqbf", "rot", "pbuf", "ptsb", "gel", "vfs",
                                     "xt", "gtm", "otm")}

    def T(self, fn, waits=(), inc=False):
        return self.P.op("tensor", fn, waits, "t" if inc else None)

    def V(self, fn, waits=(), inc=False):
        return self.P.op("vector", fn, waits, "v" if inc else None)

    def S(self, fn, waits=(), inc=False):
        return self.P.op("scalar", fn, waits, "s" if inc else None)

    def Sy(self, fn, waits=(), sem="dwA"):
        return self.P.op("sync", fn, waits, sem, amt=16)

    def G(self, fn, waits=(), sem="dgo"):
        return self.P.op("gpsimd", fn, waits, sem, amt=16)

    def tick(self, s):
        return self.P.tick(s)

    def Vsp(self):
        self.P.op("vector", lambda e: e.tensor_copy(
            self.sb["spc"][0:1, 4:5], self.sb["spc"][0:1, 5:6]))

    def Ssp(self):
        self.P.op("scalar", lambda e: e.copy(
            self.sb["spc"][0:1, 6:7], self.sb["spc"][0:1, 7:8]))

    def init_stream(self):
        self.wslot = 0
        self.wfree = [[] for _ in range(6)]
        self.collect = False
        self.wreg = {}
        self.wreg_order = []
        self.wreg_cols = 0
        self.wconv_waited = [False] * 6

    def stream_w(self, dram_ap3, ncols, key=None):
        assert key is not None
        if self.collect:
            if key not in self.wreg:
                self.wreg[key] = (self.wreg_cols, ncols)
                self.wreg_order.append((key, dram_ap3, ncols))
                self.wreg_cols += ncols
            self.wsem_used = "dwA"
            self.wbuf_used = self.sb["wbf"]
            self.cur_wslot = 0
            return 0, 0
        off, nc2 = self.wreg[key][0], self.wreg[key][1]
        assert nc2 == ncols
        sl = self.wslot
        self.wslot = (self.wslot + 1) % 6
        sb = self.sb
        wsem = ("dwA", "dwB", "dwC", "dwD", "dwE", "dwF")[sl]
        extra = [] if self.wconv_waited[sl] else [("dgw", self.tick("dgw"))]
        self.wconv_waited[sl] = True
        buf = sb["wbf"] if sl < 2 else sb["wsg"].bitcast(BF16)
        base = (sl if sl < 2 else sl - 2) * 1408
        dw = self.Sy(lambda e, off=off, buf=buf, base=base, ncols=ncols: e.dma_start(
            out=buf[:, base: base + ncols],
            in_=self.dr["wbfd"][:, off:off + ncols]),
            waits=self.wfree[sl] + extra, sem=wsem)
        self.cur_wslot = sl
        self.wsem_used = wsem
        self.wbuf_used = buf
        return base, dw

    def wdone(self, sl_base, t_tick):
        self.wfree[self.cur_consumed] = [("t", t_tick)]

    def emit_weight_cache(self):
        sb = self.sb
        wsg_free = [[], []]
        wbf_free = [[], []]
        for i, (key, ap3, ncols) in enumerate(self.wreg_order):
            sl = i % 2
            kt = ncols // 128
            wsem = ("dxA", "dxB")[sl]
            dw = self.Sy(lambda e, a=ap3, sl=sl, kt=kt: e.dma_start(
                out=sb["wsg"][:, sl * 1408: sl * 1408 + kt * 128].rearrange("p (k m) -> p k m", k=kt),
                in_=a), waits=wsg_free[sl], sem=wsem)
            ct = self.S(lambda e, sl=sl, n=ncols: e.copy(
                sb["wbf"][:, sl * 1408: sl * 1408 + n], sb["wsg"][:, sl * 1408: sl * 1408 + n]),
                waits=[(wsem, dw)] + wbf_free[sl], inc=True)
            wsg_free[sl] = [("s", ct)]
            off = self.wreg[key][0]
            gw = self.G(lambda e, sl=sl, off=off, n=ncols: e.dma_start(
                out=self.dr["wbfd"][:, off:off + n],
                in_=sb["wbf"][:, sl * 1408: sl * 1408 + n]),
                waits=[("s", ct)], sem="dgw")
            wbf_free[sl] = [("dgw", gw)]
        self.wfree = [[] for _ in range(6)]

    def prologue(self):
        sb, dr = self.sb, self.dr
        self.V(lambda e: e.memset(sb["onesf"][0:1, :], 1.0))
        self.V(lambda e: e.memset(sb["spc"][0:1, 0:2], 0.0))
        self.V(lambda e: e.memset(sb["spc"][0:1, 4:6], 0.0))
        self.S(lambda e: e.memzero(sb["spc"][0:1, 2:4]))
        self.S(lambda e: e.memzero(sb["spc"][0:1, 6:8]))
        for n, d in (("masks", "masks"), ("identb", "identb"), ("identf", "identf"),
                     ("rmat", "rmat"), ("l2ones", "l2ones"), ("l2pat", "l2pat"),
                     ("onesb", "onesb"), ("b1s", "b1"), ("b2s", "b2"), ("lngs", "lng"),
                     ("lnbs", "lnb"), ("qss", "qs"), ("kss", "ks")):
            self.Sy(lambda e, n=n, d=d: e.dma_start(out=sb[n][:], in_=dr[d][:, :]), sem="dco")
        for rnd, (na, nb) in enumerate((("cs8", "ss8"), ("cik", "sik"))):
            wv = [("v", self.tick("v"))] if rnd else []
            ta = self.Sy(lambda e, na=na, r=rnd: e.dma_start(out=sb["gtm"][:, 0:2048], in_=dr[na][:, :]),
                         waits=wv, sem="dxA")
            tb = self.Sy(lambda e, nb=nb: e.dma_start(out=sb["gtm"][:, 2048:4096], in_=dr[nb][:, :]),
                         waits=wv, sem="dxB")
            self.V(lambda e, na=na: e.tensor_copy(sb[na + "b"][:], sb["gtm"][:, 0:2048]),
                   waits=[("dxA", ta)], inc=True)
            self.V(lambda e, nb=nb: e.tensor_copy(sb[nb + "b"][:], sb["gtm"][:, 2048:4096]),
                   waits=[("dxB", tb)], inc=True)
        cvals = [(-1.0, 0), (1e-5, 1), (1.0 / 1024.0, 2), (1e-24, 3), (0.125, 4), (1.0, 5)]
        for v, i in cvals:
            self.V(lambda e, v=v, i=i: e.memset(sb["consts"][:, i:i + 1], v))
            self.nc.const_aps.register(F32, v, sb["consts"][:, i:i + 1])
        self.cap = {v: sb["consts"][:, i:i + 1] for v, i in cvals}
        self.V(lambda e: e.memset(sb["small"][:], 0.0))
        self.V(lambda e: e.memset(sb["kcar"][:], 0.0))
        self.V(lambda e: e.memset(sb["vcar"][:], 0.0))
        self.V(lambda e: e.memset(
            sb["vaug"][:].rearrange("p (t h c) -> p t h c", t=4, c=65)[:, :, :, 64:65], 1.0), inc=True)
        self.prol_v = self.tick("v")
        self.free["regB"] = [("v", self.prol_v)]
        pw = [("dco", self.tick("dco"))]
        for eng in ("tensor", "vector", "scalar", "gpsimd"):
            self.P.op(eng, lambda e: None, waits=pw)

    def seq_init(self, s):
        sb = self.sb
        self.V(lambda e: e.memset(sb["acar"][:], 0.0))
        self.V(lambda e: e.memset(sb["fcar"][:], 0.0))
        self.V(lambda e: e.memset(sb["kv"][:], 0.0), inc=True)
        self.seq_v = self.tick("v")

    def phase1(self, s, c):
        sb, dr, ps = self.sb, self.dr, self.ps
        parts = 9
        ntiles = 4
        base = s * S + c * CH
        st = sb["stats"]
        xn_done_s = None
        self.xt_free = getattr(self, "xt_free", [[], []])
        self.xnt_free = getattr(self, "xnt_free", [[], []])
        for t in range(ntiles):
            sl = t % 2
            xsem = ("dxA", "dxB")[sl]
            dx = self.Sy(lambda e, t=t, sl=sl: e.dma_start(
                out=sb["xt"][:, sl * 1024:(sl + 1) * 1024],
                in_=dr["xin"][base + t * 128: base + (t + 1) * 128, :]),
                waits=self.xt_free[sl] + ([("v", self.prol_v)] if (s == 0 and c == 0 and t == 0) else []),
                sem=xsem)
            xsl = sb["xt"][:, sl * 1024:(sl + 1) * 1024]
            NOSTATS = False
            if NOSTATS:
                ss_t = self.S(lambda e, xsl=xsl, t=t: e.copy(
                    sb["regB"][:, 2048:2176], xsl[:, 0:128]),
                    waits=[(xsem, dx)] + (self.free["regB"] if t == 0 else []), inc=True)
            else:
                ss_t = self.S(lambda e, xsl=xsl, t=t: e.activation(
                    sb["regB"][:, 2048:3072], xsl, AF.Square, accum_out=st[:, t:t + 1]),
                    waits=[(xsem, dx)] + (self.free["regB"] if t == 0 else []), inc=True)
            sm_t = self.V(lambda e, xsl=xsl, t=t: e.reduce_sum(st[:, 4 + t:5 + t], xsl, axis=AX),
                          waits=[(xsem, dx)], inc=True)
            mu_t = self.V(lambda e, t=t: e.tensor_scalar_mul(st[:, 8 + t:9 + t], st[:, 4 + t:5 + t], 1.0 / 1024.0),
                          waits=[("v", sm_t)], inc=True)
            m2_t = self.V(lambda e, t=t: e.tensor_scalar_mul(st[:, 12 + t:13 + t], st[:, t:t + 1], 1.0 / 1024.0),
                          waits=[("s", ss_t)], inc=True)
            mu2_t = self.V(lambda e, t=t: e.tensor_mul(st[:, 16 + t:17 + t], st[:, 8 + t:9 + t], st[:, 8 + t:9 + t]),
                           waits=[("v", mu_t)], inc=True)
            var_t = self.V(lambda e, t=t: e.tensor_sub(st[:, 20 + t:21 + t], st[:, 12 + t:13 + t],
                                                       st[:, 16 + t:17 + t]),
                           waits=[("v", mu2_t), ("v", m2_t)], inc=True)
            if NOSTATS:
                ln_t = self.S(lambda e, t=t: e.copy(st[:, 12 + t:13 + t], st[:, 20 + t:21 + t]),
                              waits=[("v", var_t)], inc=True)
                rstd_t = self.S(lambda e, t=t: e.copy(st[:, 24 + t:25 + t], st[:, 12 + t:13 + t]),
                                waits=[("s", ln_t)], inc=True)
            else:
                ln_t = self.S(lambda e, t=t: e.activation(st[:, 12 + t:13 + t], st[:, 20 + t:21 + t], AF.Ln,
                                                          bias=self.cap[1e-5], scale=1.0),
                              waits=[("v", var_t)], inc=True)
                rstd_t = self.S(lambda e, t=t: e.activation(st[:, 24 + t:25 + t], st[:, 12 + t:13 + t],
                                                            AF.Exp, scale=-0.5),
                                waits=[("s", ln_t)], inc=True)
            nmr_t = self.V(lambda e, t=t: e.scalar_tensor_tensor(
                st[:, 28 + t:29 + t], st[:, 8 + t:9 + t], -1.0, st[:, 24 + t:25 + t],
                op0=ALU.mult, op1=ALU.mult), waits=[("s", rstd_t)], inc=True)
            xsl2 = sb["regB"][:, (sl * 1024):(sl + 1) * 1024]
            xn_t = self.V(lambda e, xsl=xsl, xsl2=xsl2, t=t: e.tensor_scalar(
                xsl2, xsl, st[:, 24 + t:25 + t], st[:, 28 + t:29 + t],
                op0=ALU.mult, op1=ALU.add),
                waits=[("v", nmr_t)] + self.xnt_free[sl], inc=True)
            self.xt_free[sl] = [("v", xn_t)]
            EV = "vs"
            if parts < 1:
                self.xnt_free[sl] = []
                last_evs = []
                continue
            evs = []
            for bi, bank in enumerate(("A0", "A1")):
                tt = None
                for d in range(4 * bi, 4 * bi + 4):
                    tt = self.T(lambda e, bank=bank, d=d, xsl2=xsl2: e.transpose(
                        ps[bank][:, (d % 4) * 128:((d % 4) + 1) * 128],
                        xsl2[:, d * 128:(d + 1) * 128], sb["identf"][:]),
                        waits=([("v", xn_t)] + self.bankfree[bank]) if d % 4 == 0 else (),
                        inc=(d % 4 == 3))
                tt = self.tick("t")
                import os as _os
                if EV == "none":
                    yv = self.V(lambda e: e.tensor_copy(sb["stats"][:, 5:6], sb["stats"][:, 6:7]),
                                waits=[("t", tt)], inc=True)
                    xv = self.S(lambda e: e.copy(sb["stats"][:, 6:7], sb["stats"][:, 7:8]),
                                waits=[("t", tt)], inc=True)
                elif EV == "v":
                    yv = self.V(lambda e, bank=bank, bi=bi, t=t: e.tensor_copy(
                        sb["y"][:].rearrange("p (d m) -> p d m", d=8)[:, 4 * bi:4 * bi + 4, t * 128:(t + 1) * 128],
                        ps[bank][:].rearrange("p (d m) -> p d m", d=4)),
                        waits=[("t", tt)] + (self.free["y"] if (t == 0 and bi == 0) else []), inc=True)
                    xv = self.S(lambda e: e.copy(sb["stats"][:, 6:7], sb["stats"][:, 7:8]),
                                waits=[("t", tt)], inc=True)
                else:
                    yv = self.V(lambda e, bank=bank, bi=bi, t=t: e.tensor_copy(
                        sb["y"][:].rearrange("p (d m) -> p d m", d=8)[:, 4 * bi:4 * bi + 4, t * 128:(t + 1) * 128],
                        ps[bank][:].rearrange("p (d m) -> p d m", d=4)),
                        waits=[("t", tt)] + (self.free["y"] if (t == 0 and bi == 0) else []), inc=True)
                    xv = None
                    for dd in range(4):
                        d2 = 4 * bi + dd
                        xv = self.S(lambda e, d2=d2, t=t: e.copy(
                            sb["xnb"][:, d2 * CH + t * 128: d2 * CH + (t + 1) * 128],
                            sb["y"][:, d2 * CH + t * 128: d2 * CH + (t + 1) * 128]),
                            waits=([("v", yv)] + (self.free["xnb"] if (t == 0 and bi == 0) else []))
                            if dd == 0 else (),
                            inc=(dd == 3))
                self.bankfree[bank] = [("v", yv), ("s", xv)]
                evs += [("v", yv), ("s", xv)]
            self.xnt_free[sl] = [("t", self.tick("t"))]
            last_evs = evs
        self.xn_ready = last_evs
        del last_evs
        if parts < 2:
            self.y_ready = self.xn_ready
            self.kv_tick = 0
            return
        self.y_ready = self.xn_ready
        if parts < 1:
            self.kv_tick = 0
            return
        gsp = self.G(lambda e, s=s, c=c: e.dma_start(
            out=dr["xnsp"].rearrange("(k d p) m -> k p d m", d=8, p=128)[s * NCH + c],
            in_=sb["xnb"][:].rearrange("p (d m) -> p d m", d=8)),
            waits=self.xn_ready, sem="dspx")
        self.xnb_spilled = gsp
        if parts < 3:
            self.kv_tick = 0
            return
        wk = dr["wkT"].rearrange("(t p) m -> p t m", p=128)
        wv = dr["wvT"].rearrange("(t p) m -> p t m", p=128)
        for wi, (wsrc, dst) in enumerate(((wk, "qp"), (wv, "kp"))):
            for blk in range(8):
                cb, ct = self.stream_w(wsrc[:, :, blk * 128:(blk + 1) * 128], 1024, key=("okv", wi, blk))
                wsu = self.wsem_used
                wb = self.wbuf_used
                wcs = self.cur_wslot
                bank = ("B0", "B1")[blk % 2]
                mmt = None
                for t in range(4):
                    for d in range(8):
                        mmt = self.T(lambda e, bank=bank, t=t, d=d, cb=cb, wb=wb: e.matmul(
                            ps[bank][:, t * 128:(t + 1) * 128],
                            sb["xnb"][:, d * 512 + t * 128: d * 512 + (t + 1) * 128],
                            wb[:, cb + d * 128: cb + (d + 1) * 128],
                            start=(d == 0), stop=(d == 7)),
                            waits=([(wsu, ct)] + self.xn_ready + self.bankfree[bank]) if (t == 0 and d == 0) else (),
                            inc=(t == 3 and d == 7))
                mmt = self.tick("t")
                self.cur_consumed = wcs; self.wdone(cb, mmt)
                ev = self.V(lambda e, dst=dst, blk=blk, bank=bank: e.tensor_copy(
                    sb[dst][:].rearrange("p (t m) -> p t m", t=4)[:, :, blk * 128:(blk + 1) * 128],
                    ps[bank][:].rearrange("p (t m) -> p t m", t=4)),
                    waits=[("t", mmt)] + (self.free["qp" if dst == "qp" else "kp"] if blk == 0 else []),
                    inc=True)
                self.bankfree[bank] = [("v", ev)]
        kvready = [("v", self.tick("v"))]
        self.free["xnb"] = [("t", self.tick("t"))]
        if parts < 4:
            self.kv_tick = 0
            return
        mmt = None
        for t in range(4):
            for h in range(H):
                bank = "C0" if h < 8 else "C1"
                mmt = self.T(lambda e, t=t, h=h, bank=bank: e.matmul(
                    ps[bank][0:64, (h % 8) * 64: (h % 8 + 1) * 64],
                    sb["qp"][:, t * 1024 + h * 64: t * 1024 + (h + 1) * 64],
                    sb["kp"][:, t * 1024 + h * 64: t * 1024 + (h + 1) * 64],
                    start=(t == 0 and h % 8 == 0), stop=(t == 3 and h % 8 == 7),
                    skip_group_check=True),
                    waits=(kvready + self.bankfree["C0"] + self.bankfree["C1"])
                          if (t == 0 and h == 0) else (),
                    inc=(t == 3 and h == H - 1))
        self.free["qp"] = [("t", mmt)]
        self.free["kp"] = [("t", mmt)]
        kvv0 = self.V(lambda e: e.tensor_tensor(sb["kv"][:, 0:512], sb["kv"][:, 0:512],
                                                ps["C0"][0:64, :], op=ALU.add),
                      waits=[("t", mmt), ("v", self.seq_v)], inc=True)
        kvv = self.V(lambda e: e.tensor_tensor(sb["kv"][:, 512:1024], sb["kv"][:, 512:1024],
                                               ps["C1"][0:64, :], op=ALU.add), inc=True)
        self.bankfree["C0"] = [("v", kvv0)]
        self.bankfree["C1"] = [("v", kvv)]
        self.kv_tick = kvv

    def attn(self, s, c, l):
        sb, dr, ps = self.sb, self.dr, self.ps
        P = self.P
        ccol = c * CH
        x0 = self.V(lambda e: e.tensor_copy(sb["xs"][:, 0:4 * CH], sb["y"][:, 0:4 * CH]),
                    waits=self.free["xs"] + self.y_state, inc=True)
        for d in range(4, 8):
            self.V(lambda e, d=d: e.tensor_copy(sb["xs"][:, d * CH + 1:(d + 1) * CH],
                                                sb["y"][:, d * CH:d * CH + CH - 1]))
        self.V(lambda e, l=l: e.tensor_copy(
            sb["xs"][:].rearrange("p (d m) -> p d m", d=8)[:, 4:8, 0:1],
            sb["acar"][:].rearrange("p (l d) -> p l d", l=NL)[:, l, :, None]))
        acv = self.V(lambda e, l=l: e.tensor_copy(
            sb["acar"][:].rearrange("p (l d) -> p l d", l=NL)[:, l, :],
            sb["y"][:].rearrange("p (d m) -> p d m", d=8)[:, 4:8, CH - 1]), inc=True)
        xs_done = [("v", acv)]
        self.free["xs"] = []

        wq3 = dr["wqkv"].rearrange("l (t p) m -> l p t m", p=128)[l]
        for which, (dst, ssbank, tabs, scl) in enumerate(
                (("qp", "C0", ("cs8b", "ss8b"), "qss"), ("kp", "C1", ("cik b", "sikb"), "kss"))):
            tabs = (tabs[0].replace(" ", ""), tabs[1])
            rqs_t = self.V(lambda e, scl=scl, l=l: e.tensor_scalar_mul(
                sb["rqs"][:], sb["rmat"][:], sb[scl][:, l:l + 1]),
                waits=[("t", self.tick("t"))], inc=True)
            asq_rd = [[], []]
            for dt in range(8):
                col0 = (which * 8 + dt) * 128
                cb, ct = self.stream_w(wq3[:, :, col0:col0 + 128], 1024, key=("qkv", l, col0))
                wsu = self.wsem_used
                wb = self.wbuf_used
                wcs = self.cur_wslot
                bank = ("A0", "A1")[dt % 2]
                for d in range(8):
                    mmt = self.T(lambda e, bank=bank, d=d, cb=cb, wb=wb: e.matmul(
                        ps[bank][:], wb[:, cb + d * 128: cb + (d + 1) * 128],
                        sb["xs"][:, d * CH:(d + 1) * CH],
                        start=(d == 0), stop=(d == 7), skip_group_check=True),
                        waits=([(wsu, ct)] + xs_done + self.bankfree[bank] +
                               (self.free[dst] if dt == 0 else [])) if d == 0 else (),
                        inc=(d == 7))
                self.cur_consumed = wcs; self.wdone(cb, mmt)
                ev = self.V(lambda e, dst=dst, dt=dt, bank=bank: e.tensor_copy(
                    sb[dst][:, dt * CH:(dt + 1) * CH], ps[bank][:]),
                    waits=[("t", mmt)], inc=True)
                self.bankfree[bank] = [("v", ev)]
                sq = self.S(lambda e, dst=dst, dt=dt: e.activation(
                    sb["sqbf"][:, (dt % 2) * CH:(dt % 2 + 1) * CH], sb[dst][:, dt * CH:(dt + 1) * CH],
                    AF.Square), waits=[("v", ev)] + asq_rd[dt % 2], inc=True)
                cst = self.T(lambda e, dt=dt, ssbank=ssbank: e.matmul(
                    ps[ssbank][0:16, :], sb["l2ones"][:, dt * 16:(dt + 1) * 16],
                    sb["sqbf"][:, (dt % 2) * CH:(dt % 2 + 1) * CH],
                    start=(dt == 0), stop=(dt == 7), skip_group_check=True),
                    waits=[("s", sq)] + (self.bankfree[ssbank] if dt == 0 else []), inc=True)
                asq_rd[dt % 2] = [("t", cst)]
            mx = self.V(lambda e, ssbank=ssbank: e.tensor_single_scalar(
                sb["invt"][:, 0:512], ps[ssbank][0:16, :], 1e-24, op=ALU.max),
                waits=[("t", cst)], inc=True)
            self.bankfree[ssbank] = [("v", mx)]
            lnt_ = self.S(lambda e: e.activation(sb["invt"][:, 512:1024], sb["invt"][:, 0:512], AF.Ln),
                          waits=[("v", mx)], inc=True)
            ivt = self.S(lambda e: e.activation(sb["invtb"][:, :], sb["invt"][:, 512:1024],
                                                AF.Exp, scale=-0.5), waits=[("s", lnt_)], inc=True)
            rq_rd = [list(asq_rd[0]), list(asq_rd[1])]
            for dt in range(8):
                rqb = ("B0", "B1")[dt % 2]
                ivb = ("C0", "C1")[dt % 2]
                rq = self.T(lambda e, rqb=rqb, dst=dst, dt=dt: e.matmul(
                    ps[rqb][:], sb["rqs"][:], sb[dst][:, dt * CH:(dt + 1) * CH],
                    start=True, stop=True),
                    waits=([("v", rqs_t)] if dt == 0 else []) + self.bankfree[rqb], inc=True)
                iv = self.T(lambda e, ivb=ivb, dt=dt: e.matmul(
                    ps[ivb][:], sb["l2pat"][:, dt * 128:(dt + 1) * 128], sb["invtb"][:, :],
                    start=True, stop=True),
                    waits=[("s", ivt)] + self.bankfree[ivb], inc=True)
                rqc = self.S(lambda e, rqb=rqb, dt=dt: e.copy(
                    sb["sqbf"][:, (dt % 2) * CH:(dt % 2 + 1) * CH], ps[rqb][:]),
                    waits=[("t", rq)] + rq_rd[dt % 2], inc=True)
                self.bankfree[rqb] = [("s", rqc)]
                self.V(lambda e, dst=dst, dt=dt, tabs=tabs, ccol=ccol: e.tensor_tensor(
                    sb["rot"][:, 0:CH], sb[dst][:, dt * CH:(dt + 1) * CH],
                    sb[tabs[0]][:, ccol:ccol + CH], op=ALU.mult))
                t1s = self.V(lambda e, scl=scl, l=l: e.tensor_scalar_mul(
                    sb["rot"][:, 3 * CH:4 * CH], sb["rot"][:, 0:CH], sb[scl][:, l:l + 1]),
                    inc=True)
                t2r = self.V(lambda e, dt=dt, tabs=tabs, ccol=ccol: e.tensor_tensor(
                    sb["rot"][:, CH:2 * CH], sb["sqbf"][:, (dt % 2) * CH:(dt % 2 + 1) * CH],
                    sb[tabs[1]][:, ccol:ccol + CH], op=ALU.mult),
                    waits=[("s", rqc)], inc=True)
                rq_rd[dt % 2] = [("v", t2r)]
                t3 = self.V(lambda e: e.tensor_add(sb["rot"][:, 2 * CH:3 * CH],
                                                   sb["rot"][:, 3 * CH:4 * CH],
                                                   sb["rot"][:, CH:2 * CH]),
                            waits=[("v", t1s)], inc=True)
                qf = self.V(lambda e, dst=dst, dt=dt, ivb=ivb: e.tensor_tensor(
                    sb[dst][:, dt * CH:(dt + 1) * CH], sb["rot"][:, 2 * CH:3 * CH],
                    ps[ivb][:], op=ALU.mult), waits=[("t", iv), ("v", t3)], inc=True)
                self.bankfree[ivb] = [("v", qf)]
            if which == 0:
                qdone = [("v", self.tick("v"))]
            else:
                kdone = [("v", self.tick("v"))]
        for blk in range(8):
            cb, ct = self.stream_w(wq3[:, :, 2048 + blk * 128:2048 + (blk + 1) * 128], 1024, key=("qkv", l, 2048 + blk * 128))
            wsu = self.wsem_used
            wb = self.wbuf_used
            wcs = self.cur_wslot
            bank = ("A0", "A1")[blk % 2]
            for d in range(8):
                mmt = self.T(lambda e, bank=bank, d=d, cb=cb, wb=wb: e.matmul(
                    ps[bank][:], wb[:, cb + d * 128: cb + (d + 1) * 128],
                    sb["xs"][:, d * CH:(d + 1) * CH],
                    start=(d == 0), stop=(d == 7), skip_group_check=True),
                    waits=([(wsu, ct)] + self.bankfree[bank] +
                           (self.free["vfs"] + self.xt_free[0] + self.xt_free[1]
                            if blk == 0 else [])) if d == 0 else (),
                    inc=(d == 7))
            self.cur_consumed = wcs; self.wdone(cb, mmt)
            ev = self.V(lambda e, blk=blk, bank=bank: e.tensor_copy(
                sb["vfs"][:, blk * CH:(blk + 1) * CH], ps[bank][:]),
                waits=[("t", mmt)], inc=True)
            self.bankfree[bank] = [("v", ev)]
        vfs_done = [("v", self.tick("v"))]
        for st in range(4):
            ptb = ("PT0", "PT1")[st % 2]
            for blk in range(8):
                tt = self.T(lambda e, ptb=ptb, st=st, blk=blk: e.transpose(
                    ps[ptb][:, blk * 128:(blk + 1) * 128],
                    sb["vfs"][:, blk * CH + st * 128: blk * CH + (st + 1) * 128],
                    sb["identb"][:]),
                    waits=(vfs_done + self.bankfree[ptb] + (self.free["vaug"] if (st == 0 and blk == 0) else [])) if blk == 0 else (),
                    inc=(blk == 7))
            ev = self.V(lambda e, ptb=ptb, st=st: e.tensor_copy(
                sb["vaug"][:, st * 1040:(st + 1) * 1040].rearrange("p (h q) -> p h q", q=65)[:, :, 0:64],
                ps[ptb][:].rearrange("p (h q) -> p h q", q=64)),
                waits=[("t", tt)], inc=True)
            self.bankfree[ptb] = [("v", ev)]
        vaug_done = [("v", self.tick("v"))]
        self.xt_free[0] = [("t", self.tick("t"))]
        self.xt_free[1] = [("t", self.tick("t"))]
        self.free["vfs"] = [("t", self.tick("t"))]
        return qdone, kdone, vaug_done

    def attn_core(self, s, c, l, qdone, kdone, vaug_done):
        sb, dr, ps = self.sb, self.dr, self.ps
        first = (c == 0)
        nmx_i = [0]
        pbuf_i = [0]
        pbuf_free = [[] for _ in range(4)]
        nmax_free = [[] for _ in range(8)]
        ptsb_free = [[], []]
        for w in (0, 1):
            for p in range(8):
                simb = {0: ("A0", "A1"), 1: ("B0", "B1")}[p % 2]
                ptb = ("PT0", "PT1")[p % 2]
                ptev = [None, None]
                for par in (0, 1):
                    bank = simb[par]
                    r0 = par * 64
                    for i in (0, 1):
                        segs = []
                        if w == 0:
                            kc = sb["kcar"][:].rearrange("p (L q m) -> p L q m", L=NL, q=8)
                            if i == 0:
                                segs.append((kc[r0:r0 + 64, l, p, 0:256], 0, 256))
                                segs.append((sb["kp"][r0:r0 + 64, p * CH: p * CH + 128], 256, 128))
                            else:
                                segs.append((kc[r0:r0 + 64, l, p, 128:256], 0, 128))
                                segs.append((sb["kp"][r0:r0 + 64, p * CH: p * CH + 256], 128, 256))
                        else:
                            j0 = i * 128
                            segs.append((sb["kp"][r0:r0 + 64, p * CH + j0: p * CH + j0 + 384], 0, 384))
                        lq = sb["qp"][r0:r0 + 64, p * CH + w * 256 + i * 128: p * CH + w * 256 + (i + 1) * 128]
                        w0waits = qdone + kdone + list(self.bankfree[bank])
                        mmt = None
                        for si, (rhs, o, n) in enumerate(segs):
                            mmt = self.T(lambda e, bank=bank, lq=lq, rhs=rhs, o=o, n=n, si=si: e.matmul(
                                ps[bank][:, o:o + n], lq, rhs, start=(si == 0), stop=False,
                                skip_group_check=True),
                                waits=w0waits if si == 0 else (), inc=False)
                        mi = (2 if (first and w == 0) else 0) + i
                        mmt = self.T(lambda e, bank=bank, mi=mi: e.matmul(
                            ps[bank][:, 0:384], sb["identb"][:], sb["masks"][:, mi * 384:(mi + 1) * 384],
                            start=False, stop=True, skip_group_check=True), inc=True)
                        nc_ = nmx_i[0] % 8
                        nmx_i[0] += 1
                        rmx = self.V(lambda e, bank=bank, nc_=nc_: e.reduce_max(
                            sb["nmax"][:, nc_:nc_ + 1], ps[bank][:, 0:384], axis=AX, negate=True),
                            waits=[("t", mmt)] + nmax_free[nc_], inc=True)
                        pb = pbuf_i[0] % 4
                        pbuf_i[0] += 1
                        ex = self.S(lambda e, bank=bank, nc_=nc_, pb=pb: e.activation(
                            sb["pbuf"][:, pb * CH: pb * CH + 384], ps[bank][:, 0:384], AF.Exp,
                            bias=sb["nmax"][:, nc_:nc_ + 1]),
                            waits=[("v", rmx)] + pbuf_free[pb], inc=True)
                        nmax_free[nc_] = [("s", ex)]
                        self.bankfree[bank] = [("s", ex)]
                        for jj in range(3):
                            jt = jj + i
                            tt = self.T(lambda e, ptb=ptb, pb=pb, jj=jj, jt=jt, i=i: e.transpose(
                                ps[ptb][:, jt * 256 + i * 128: jt * 256 + i * 128 + 128],
                                sb["pbuf"][:, pb * CH + jj * 128: pb * CH + (jj + 1) * 128],
                                sb["identb"][:]),
                                waits=([("s", ex)] + (self.bankfree[ptb] if (i == 0 and jj == 0) else [])),
                                inc=(jj == 2))
                        pbuf_free[pb] = [("t", tt)]
                    pev = self.V(lambda e, ptb=ptb, par=par: e.tensor_copy(
                        sb["ptsb"][:, par * 1024:(par + 1) * 1024], ps[ptb][:]),
                        waits=[("t", tt)] + ptsb_free[par], inc=True)
                    self.bankfree[ptb] = [("v", pev)]
                    ptev[par] = pev
                for par in (0, 1):
                    h = 2 * p + par
                    pvb = ("C0", "C1")[par]
                    vsrcs = []
                    for jt in range(4):
                        if w == 0 and jt < 2:
                            vc = sb["vcar"][:].rearrange("p (L q m) -> p L q m", L=NL, q=2)
                            vsrcs.append(vc[:, l, jt, h * 65:(h + 1) * 65])
                        else:
                            st = (jt - 2) if w == 0 else jt
                            vsrcs.append(sb["vaug"][:, st * 1040 + h * 65: st * 1040 + (h + 1) * 65])
                    mmt = None
                    order = (1, 2, 0, 3)
                    for oi, jt in enumerate(order):
                        ilo = 128 if jt == 3 else 0
                        ihi = 128 if jt == 0 else 256
                        mmt = self.T(lambda e, pvb=pvb, jt=jt, par=par, vs=vsrcs[jt], ilo=ilo, ihi=ihi,
                                     oi=oi: e.matmul(
                            ps[pvb][0:65, ilo:ihi], vs,
                            sb["ptsb"][:, par * 1024 + jt * 256 + ilo: par * 1024 + jt * 256 + ihi],
                            start=(oi == 0), stop=(oi == 3), skip_group_check=True),
                            waits=([("v", ptev[par])] + vaug_done + self.bankfree[pvb]) if oi == 0 else (),
                            inc=(oi == 3))
                    ptsb_free[par] = [("t", mmt)]
                    rssl = par
                    rc0 = self.S(lambda e, pvb=pvb, rssl=rssl: e.activation(
                        sb["small"][0:1, 1024 + rssl * 256:1024 + (rssl + 1) * 256],
                        ps[pvb][64:65, 0:256], AF.Ln),
                        waits=[("t", mmt)], inc=True)
                    rc = self.S(lambda e, rssl=rssl: e.activation(
                        sb["rs"][0:1, rssl * 256:(rssl + 1) * 256],
                        sb["small"][0:1, 1024 + rssl * 256:1024 + (rssl + 1) * 256],
                        AF.Exp, scale=-1.0),
                        waits=[("s", rc0)], inc=True)
                    rbt = self.T(lambda e, pvb=pvb, rssl=rssl: e.matmul(
                        ps[pvb][0:64, 256:512], sb["onesf"][0:1, 0:64],
                        sb["rs"][0:1, rssl * 256:(rssl + 1) * 256], start=True, stop=True,
                        skip_group_check=True),
                        waits=[("s", rc)], inc=True)
                    rbe = self.V(lambda e, pvb=pvb, rssl=rssl: e.tensor_copy(
                        sb["rb"][:, rssl * 256:(rssl + 1) * 256], ps[pvb][0:64, 256:512]),
                        waits=[("t", rbt)], inc=True)
                    at = self.V(lambda e, pvb=pvb, rssl=rssl, r0=par * 64, p=p, w=w: e.tensor_tensor(
                        sb["afm"][r0:r0 + 64, p * CH + w * 256: p * CH + (w + 1) * 256],
                        ps[pvb][0:64, 0:256], sb["rb"][:, rssl * 256:(rssl + 1) * 256], op=ALU.mult),
                        waits=[("v", rbe)] + (self.free["afm"] if (w == 0 and p == 0 and par == 0) else []),
                        inc=True)
                    self.bankfree[pvb] = [("v", at)]
            if w == 0:
                tnow = self.tick("t")
                kcv = self.V(lambda e, l=l: e.tensor_copy(
                    sb["kcar"][:].rearrange("p (L q m) -> p L q m", L=NL, q=8)[:, l],
                    sb["kp"][:].rearrange("p (q m) -> p q m", q=8)[:, :, 256:512]),
                    waits=[("t", tnow)], inc=True)
                vcv = self.V(lambda e, l=l: e.tensor_copy(
                    sb["vcar"][:].rearrange("p (L q m) -> p L q m", L=NL, q=2)[:, l],
                    sb["vaug"][:].rearrange("p (q m) -> p q m", q=4)[:, 2:4]),
                    waits=[("t", tnow)], inc=True)
        afm_done = [("v", self.tick("v"))]
        self.free["qp"] = [("t", self.tick("t"))]
        self.free["kp"] = [("t", self.tick("t"))]
        self.free["vaug"] = [("t", self.tick("t"))]
        self.free["xs"] = [("t", self.tick("t"))]
        wo3 = dr["wo_i"].rearrange("l (t p) m -> l p t m", p=128)[l]
        for dt in range(8):
            cb, ct = self.stream_w(wo3[:, :, dt * 128:(dt + 1) * 128], 1024, key=("woi", l, dt))
            wsu = self.wsem_used
            wb = self.wbuf_used
            wcs = self.cur_wslot
            bank = ("A0", "A1")[dt % 2]
            for d in range(8):
                mmt = self.T(lambda e, bank=bank, d=d, cb=cb, wb=wb: e.matmul(
                    ps[bank][:], wb[:, cb + d * 128: cb + (d + 1) * 128],
                    sb["afm"][:, d * CH:(d + 1) * CH],
                    start=(d == 0), stop=(d == 7), skip_group_check=True),
                    waits=([(wsu, ct)] + afm_done + self.bankfree[bank]) if d == 0 else (),
                    inc=(d == 7))
            self.cur_consumed = wcs; self.wdone(cb, mmt)
            yv = self.V(lambda e, dt=dt, bank=bank: e.scalar_tensor_tensor(
                sb["y"][:, dt * CH:(dt + 1) * CH], ps[bank][:], 1.0,
                sb["y"][:, dt * CH:(dt + 1) * CH], op0=ALU.mult, op1=ALU.add),
                waits=[("t", mmt)], inc=True)
            self.bankfree[bank] = [("v", yv)]
        self.y_state = [("v", self.tick("v"))]
        self.free["afm"] = [("t", self.tick("t"))]
        self.free["regB"] = [("t", self.tick("t"))]

    def ffn(self, s, c, l):
        sb, dr, ps = self.sb, self.dr, self.ps
        x0 = self.V(lambda e: e.tensor_copy(sb["xs"][:, 0:4 * CH], sb["y"][:, 0:4 * CH]),
                    waits=self.free["xs"] + self.y_state, inc=True)
        for d in range(4, 8):
            self.V(lambda e, d=d: e.tensor_copy(sb["xs"][:, d * CH + 1:(d + 1) * CH],
                                                sb["y"][:, d * CH:d * CH + CH - 1]))
        xcp = self.V(lambda e, l=l: e.tensor_copy(
            sb["xs"][:].rearrange("p (d m) -> p d m", d=8)[:, 4:8, 0:1],
            sb["fcar"][:].rearrange("p (l d) -> p l d", l=NL)[:, l, :, None]), inc=True)
        xs_done = [("v", xcp)]
        cst = sst = None
        sq_rd = [[], []]
        SQV = False
        for d in range(8):
            if SQV:
                sq = self.V(lambda e, d=d: e.tensor_mul(
                    sb["sqbf"][:, (d % 2) * CH:(d % 2 + 1) * CH], sb["xs"][:, d * CH:(d + 1) * CH],
                    sb["xs"][:, d * CH:(d + 1) * CH]),
                    waits=(list(xs_done) if d == 0 else []) + sq_rd[d % 2], inc=True)
                sqw = [("v", sq)]
            else:
                sq = self.S(lambda e, d=d: e.activation(
                    sb["sqbf"][:, (d % 2) * CH:(d % 2 + 1) * CH], sb["xs"][:, d * CH:(d + 1) * CH],
                    AF.Square), waits=(list(xs_done) if d == 0 else []) + sq_rd[d % 2], inc=True)
                sqw = [("s", sq)]
            cst = self.T(lambda e, d=d: e.matmul(
                ps["C0"][0:1, :], sb["onesb"][:, 0:1], sb["xs"][:, d * CH:(d + 1) * CH],
                start=(d == 0), stop=(d == 7), skip_group_check=True),
                waits=(xs_done + self.bankfree["C0"]) if d == 0 else (), inc=True)
            sst = self.T(lambda e, d=d: e.matmul(
                ps["C1"][0:1, :], sb["onesb"][:, 0:1],
                sb["sqbf"][:, (d % 2) * CH:(d % 2 + 1) * CH],
                start=(d == 0), stop=(d == 7), skip_group_check=True),
                waits=sqw + (self.bankfree["C1"] if d == 0 else []), inc=True)
            sq_rd[d % 2] = [("t", sst)]
        mu = self.V(lambda e: e.tensor_scalar_mul(sb["lnt"][0:1, 0:512], ps["C0"][0:1, :], 1.0 / 1024.0),
                    waits=[("t", cst)], inc=True)
        m2 = self.V(lambda e: e.tensor_scalar_mul(sb["lnt"][0:1, 512:1024], ps["C1"][0:1, :], 1.0 / 1024.0),
                    waits=[("t", sst)], inc=True)
        mu2 = self.V(lambda e: e.tensor_mul(sb["lnt"][0:1, 1024:1536], sb["lnt"][0:1, 0:512],
                                            sb["lnt"][0:1, 0:512]), waits=[("v", mu)], inc=True)
        var = self.V(lambda e: e.tensor_sub(sb["lnt"][0:1, 1024:1536], sb["lnt"][0:1, 512:1024],
                                            sb["lnt"][0:1, 1024:1536]),
                     waits=[("v", mu2), ("v", m2)], inc=True)
        lnv = self.S(lambda e: e.activation(sb["lnt"][0:1, 512:1024], sb["lnt"][0:1, 1024:1536], AF.Ln,
                                            bias=self.cap[1e-5][0:1, 0:1]), waits=[("v", var)], inc=True)
        rstd = self.S(lambda e: e.activation(sb["lnt"][0:1, 1024:1536], sb["lnt"][0:1, 512:1024],
                                             AF.Exp, scale=-0.5), waits=[("s", lnv)], inc=True)
        nmr = self.V(lambda e: e.scalar_tensor_tensor(
            sb["lnt"][0:1, 1536:2048], sb["lnt"][0:1, 0:512], -1.0, sb["lnt"][0:1, 1024:1536],
            op0=ALU.mult, op1=ALU.mult), waits=[("s", rstd)], inc=True)
        rb_t = self.T(lambda e: e.matmul(ps["C0"][:], sb["onesf"][0:1, :], sb["lnt"][0:1, 1024:1536],
                                         start=True, stop=True, skip_group_check=True),
                      waits=[("v", nmr)], inc=True)
        nb_t = self.T(lambda e: e.matmul(ps["C1"][:], sb["onesf"][0:1, :], sb["lnt"][0:1, 1536:2048],
                                         start=True, stop=True, skip_group_check=True),
                      waits=[("v", nmr)], inc=True)
        for d in range(8):
            if d < 4:
                t1 = self.V(lambda e, d=d: e.tensor_tensor(
                    sb["otm"][:, 0:CH], sb["y"][:, d * CH:(d + 1) * CH], ps["C0"][:], op=ALU.mult),
                    waits=([("t", rb_t), ("t", nb_t)] + self.free["otm"]) if d == 0 else (), inc=True)
            else:
                t1c = self.V(lambda e, d=d, l=l: e.tensor_tensor(
                    sb["otm"][:, 0:1],
                    sb["fcar"][:].rearrange("p (l d) -> p l d", l=NL)[:, l, d - 4, None], ps["C0"][:, 0:1],
                    op=ALU.mult),
                    waits=(([("t", rb_t), ("t", nb_t)] + self.free["otm"]) if d == 4 else []), inc=True)
                t1 = self.V(lambda e, d=d: e.tensor_tensor(
                    sb["otm"][:, 1:CH], sb["y"][:, d * CH:d * CH + CH - 1], ps["C0"][:, 1:CH],
                    op=ALU.mult), inc=True)
                t2w = [("v", t1), ("v", t1c)]
            if d < 4:
                t2w = [("v", t1)]
            t2 = self.V(lambda e: e.tensor_tensor(sb["otm"][:, CH:2 * CH], sb["otm"][:, 0:CH],
                                                  ps["C1"][:], op=ALU.add), waits=t2w, inc=True)
            xv = self.V(lambda e, d=d, l=l: e.tensor_scalar(
                sb["xnb"][:, d * CH:(d + 1) * CH], sb["otm"][:, CH:2 * CH],
                sb["lngs"][:, l * 8 + d:l * 8 + d + 1], sb["lnbs"][:, l * 8 + d:l * 8 + d + 1],
                op0=ALU.mult, op1=ALU.add),
                waits=[("v", t2)] + (self.free["xnb"] if d == 0 else []), inc=True)
        self.V(lambda e, l=l: e.tensor_copy(
            sb["fcar"][:].rearrange("p (l d) -> p l d", l=NL)[:, l, :],
            sb["y"][:].rearrange("p (d m) -> p d m", d=8)[:, 4:8, CH - 1]), inc=True)
        xn_done = [("v", self.tick("v"))]
        self.bankfree["C0"] = [("v", self.tick("v")), ("s", self.tick("s"))]
        self.bankfree["C1"] = [("v", self.tick("v")), ("s", self.tick("s"))]
        self.free["otm"] = [("s", self.tick("s"))]
        w13 = dr["w1"].rearrange("l (t p) m -> l p t m", p=128)[l]
        gb = sb["regB"].bitcast(BF16)
        gel_free = [list(self.free["gel"]), list(self.free["gel"])]
        for ht in range(HT):
            cba, cta = self.stream_w(w13[:, :, ht * 128:(ht + 1) * 128], 1024, key=("w1", l, ht))
            wsua = self.wsem_used
            wba = self.wbuf_used
            wcsa = self.cur_wslot
            banka = ("A0", "A1")[ht % 2]
            for d in range(8):
                mma = self.T(lambda e, bank=banka, d=d, cb=cba, wb=wba: e.matmul(
                    ps[bank][:], wb[:, cb + d * 128: cb + (d + 1) * 128],
                    sb["xnb"][:, d * CH:(d + 1) * CH],
                    start=(d == 0), stop=(d == 7), skip_group_check=True),
                    waits=([(wsua, cta)] + xn_done + self.bankfree[banka] +
                           (self.free["regB"] if ht == 0 else [])) if d == 0 else (),
                    inc=(d == 7))
            self.cur_consumed = wcsa; self.wdone(cba, mma)
            cbg, ctg = self.stream_w(w13[:, :, HID + ht * 128:HID + (ht + 1) * 128], 1024, key=("w1g", l, ht))
            wsug = self.wsem_used
            wbg = self.wbuf_used
            wcsg = self.cur_wslot
            bankg = ("B0", "B1")[ht % 2]
            for d in range(8):
                mmg = self.T(lambda e, bank=bankg, d=d, cb=cbg, wb=wbg: e.matmul(
                    ps[bank][:], wb[:, cb + d * 128: cb + (d + 1) * 128],
                    sb["xnb"][:, d * CH:(d + 1) * CH],
                    start=(d == 0), stop=(d == 7), skip_group_check=True),
                    waits=([(wsug, ctg)] + self.bankfree[bankg]) if d == 0 else (),
                    inc=(d == 7))
            self.cur_consumed = wcsg; self.wdone(cbg, mmg)
            gl = self.S(lambda e, bank=bankg, ht=ht, l=l: e.activation(
                sb["gel"][:, (ht % 2) * CH:(ht % 2 + 1) * CH], ps[bank][:], AF.Gelu,
                bias=sb["b1s"][:, l * 2 * HT + HT + ht:l * 2 * HT + HT + ht + 1]),
                waits=[("t", mmg)] + gel_free[ht % 2], inc=True)
            self.bankfree[bankg] = [("s", gl)]
            gv = self.V(lambda e, bank=banka, ht=ht, l=l: e.scalar_tensor_tensor(
                gb[:, ht * CH:(ht + 1) * CH], ps[bank][:],
                sb["b1s"][:, l * 2 * HT + ht:l * 2 * HT + ht + 1],
                sb["gel"][:, (ht % 2) * CH:(ht % 2 + 1) * CH], op0=ALU.add, op1=ALU.mult),
                waits=[("t", mma), ("s", gl)], inc=True)
            gel_free[ht % 2] = [("v", gv)]
            self.bankfree[banka] = [("v", gv)]
        gated_done = [("v", self.tick("v"))]
        w23 = dr["w2"].rearrange("l (k p) m -> l p k m", p=128)[l]
        for dt in range(8):
            mmt = None
            for half in range(2):
                cb, ct = self.stream_w(w23[:, :, dt * 128:(dt + 1) * 128][:, half * 11:(half + 1) * 11],
                                       11 * 128, key=("w2", l, dt, half))
                wsu = self.wsem_used
                wb = self.wbuf_used
                wcs = self.cur_wslot
                for k in range(11):
                    kk = half * 11 + k
                    mmt = self.T(lambda e, dt=dt, k=k, kk=kk, cb=cb, wb=wb: e.matmul(
                        ps[("A0", "A1")[dt % 2]][:], wb[:, cb + k * 128: cb + (k + 1) * 128],
                        gb[:, kk * CH:(kk + 1) * CH],
                        start=(kk == 0), stop=(kk == 21), skip_group_check=True),
                        waits=([(wsu, ct)] + gated_done +
                               (self.bankfree[("A0", "A1")[dt % 2]] if kk == 0 else [])) if k == 0 else (),
                        inc=(k == 10))
                self.cur_consumed = wcs; self.wdone(cb, mmt)
            yv = self.V(lambda e, dt=dt: e.scalar_tensor_tensor(
                sb["y"][:, dt * CH:(dt + 1) * CH], ps[("A0", "A1")[dt % 2]][:],
                sb["b2s"][:, l * 8 + dt:l * 8 + dt + 1],
                sb["y"][:, dt * CH:(dt + 1) * CH], op0=ALU.add, op1=ALU.add),
                waits=[("t", mmt)], inc=True)
            self.bankfree[("A0", "A1")[dt % 2]] = [("v", yv)]
        self.y_state = [("v", self.tick("v"))]
        self.free["xs"] = [("t", self.tick("t"))]
        self.free["xnb"] = [("t", self.tick("t"))]
        self.free["regB"] = [("t", self.tick("t"))]
        self.free["afm"] = [("t", self.tick("t"))]
        self.free["gel"] = [("t", self.tick("t"))]

    def chunk_end(self, s, c):

        sb, dr = self.sb, self.dr
        gt = self.G(lambda e, s=s, c=c: e.dma_start(
            out=dr["ysp"].rearrange("(k d p) m -> k p d m", d=8, p=128)[s * NCH + c],
            in_=sb["y"][:].rearrange("p (d m) -> p d m", d=8)),
            waits=self.y_state, sem="dspy")
        self.free["y"] = [("dspy", gt)]

    def output(self, s):
        OSTAGE = 9
        sb, dr, ps = self.sb, self.dr, self.ps
        k1 = None
        if False:
            pass
        else:
          for h in range(H):
            par, p = h % 2, h // 2
            k1 = self.S(lambda e, par=par, p=p, h=h: e.activation(
                sb["kvb2"][par * 64:(par + 1) * 64, p * 64:(p + 1) * 64],
                sb["kv"][:, h * 64:(h + 1) * 64], AF.Copy, scale=0.125),
                waits=[("v", self.kv_tick)] if h == 0 else (), inc=(h == H - 1))
          kvb_done = [("s", k1)]
        woc = dr["woT"].rearrange("(t p) m -> p t m", p=128)
        wqc = dr["wqT"].rearrange("(t p) m -> p t m", p=128)
        for c in range(self.cfg.get("nchunks", NCH)):
            if True:
                gx = self.G(lambda e, s=s, c=c: e.dma_start(
                    out=sb["xnb"][:].rearrange("p (d m) -> p d m", d=8),
                    in_=dr["xnsp"].rearrange("(k d p) m -> k p d m", d=8, p=128)[s * NCH + c]),
                    waits=self.free["xnb"] + [("dspx", self.tick("dspx"))], sem="dgx")
            else:
                gx = 0
            if True:
                gy = self.G(lambda e, s=s, c=c: e.dma_start(
                    out=sb["y"][:].rearrange("p (d m) -> p d m", d=8),
                    in_=dr["ysp"].rearrange("(k d p) m -> k p d m", d=8, p=128)[s * NCH + c]),
                    waits=self.free["y"] + [("dspy", self.tick("dspy"))], sem="dgy")
            else:
                gy = 0
            if OSTAGE < 2:
                continue
            for dt in range(8):
                cb, ct = self.stream_w(wqc[:, :, dt * 128:(dt + 1) * 128], 1024, key=("wq", dt))
                wsu = self.wsem_used
                wb = self.wbuf_used
                wcs = self.cur_wslot
                bank = ("A0", "A1")[dt % 2]
                for d in range(8):
                    mmt = self.T(lambda e, bank=bank, d=d, cb=cb, wb=wb: e.matmul(
                        ps[bank][:], wb[:, cb + d * 128: cb + (d + 1) * 128],
                        sb["xnb"][:, d * CH:(d + 1) * CH],
                        start=(d == 0), stop=(d == 7), skip_group_check=True),
                        waits=([(wsu, ct), ("dgx", gx)] + self.bankfree[bank] +
                               (self.free["qp"] if dt == 0 else [])) if d == 0 else (),
                        inc=(d == 7))
                self.cur_consumed = wcs; self.wdone(cb, mmt)
                ev = self.V(lambda e, dt=dt, bank=bank: e.tensor_copy(
                    sb["qp"][:, dt * CH:(dt + 1) * CH], ps[bank][:]),
                    waits=[("t", mmt)], inc=True)
                self.bankfree[bank] = [("v", ev)]
            qo_done = [("v", self.tick("v"))]
            if OSTAGE < 3:
                continue
            for p in range(8):
                for par in (0, 1):
                    bank = ("B0", "B1")[par]
                    r0 = par * 64
                    mmt = self.T(lambda e, bank=bank, p=p, r0=r0: e.matmul(
                        ps[bank][0:64, :], sb["kvb2"][r0:r0 + 64, p * 64:(p + 1) * 64],
                        sb["qp"][r0:r0 + 64, p * CH:(p + 1) * CH], start=True, stop=True,
                        skip_group_check=True),
                        waits=kvb_done + qo_done + self.bankfree[bank] +
                              (self.free["xs"] if (p == 0 and par == 0) else []),
                        inc=True)
                    ev = self.V(lambda e, bank=bank, p=p, r0=r0: e.tensor_copy(
                        sb["xs"][r0:r0 + 64, p * CH:(p + 1) * CH], ps[bank][0:64, :]),
                        waits=[("t", mmt)], inc=True)
                    self.bankfree[bank] = [("v", ev)]
            sp_done = [("v", self.tick("v"))]
            if OSTAGE < 4:
                continue
            for t in range(4):
                for bi, bank in enumerate(("A0", "A1")):
                    tt = None
                    for d in range(4 * bi, 4 * bi + 4):
                        tt = self.T(lambda e, bank=bank, d=d, t=t: e.transpose(
                            ps[bank][:, (d % 4) * 128:((d % 4) + 1) * 128],
                            sb["y"][:, d * CH + t * 128: d * CH + (t + 1) * 128], sb["identf"][:]),
                            waits=([("dgy", gy)] + self.bankfree[bank]) if d % 4 == 0 else (),
                            inc=(d % 4 == 3))
                    gv = self.S(lambda e, bank=bank, bi=bi, t=t: e.activation(
                        sb["gtm"][:, t * 1024 + bi * 512: t * 1024 + (bi + 1) * 512],
                        ps[bank][:], AF.Silu),
                        waits=[("t", tt)] + (self.free["gtm"] + self.free["regB"]
                                             if (t == 0 and bi == 0) else []), inc=True)
                    self.bankfree[bank] = [("s", gv)]
            g_done = [("s", self.tick("s"))]
            base = s * S + c * CH
            st_dg = []
            for t in range(4):
                dx = self.Sy(lambda e, t=t, base=base: e.dma_start(
                    out=sb["xt"][:, (t % 2) * 1024:(t % 2 + 1) * 1024],
                    in_=dr["xin"][base + t * 128: base + (t + 1) * 128, :]),
                    waits=self.xt_free[t % 2], sem=("dxA", "dxB")[t % 2])
                for half in range(2):
                    bank = ("B0", "B1")[half]
                    for blk in range(4 * half, 4 * half + 4):
                        cb, ct = self.stream_w(woc[:, :, blk * 128:(blk + 1) * 128], 1024, key=("wo", blk))
                        wsu = self.wsem_used
                        wb = self.wbuf_used
                        wcs = self.cur_wslot
                        for p in range(8):
                            mmt = self.T(lambda e, bank=bank, p=p, t=t, blk=blk, cb=cb, wb=wb: e.matmul(
                                ps[bank][:, (blk % 4) * 128:(blk % 4 + 1) * 128],
                                sb["xs"][:, p * CH + t * 128: p * CH + (t + 1) * 128],
                                wb[:, cb + p * 128: cb + (p + 1) * 128],
                                start=(p == 0), stop=(p == 7), skip_group_check=True),
                                waits=([(wsu, ct)] + sp_done +
                                       (self.bankfree[bank] if blk % 4 == 0 else [])) if p == 0 else (),
                                inc=(p == 7))
                        self.cur_consumed = wcs; self.wdone(cb, mmt)
                    c0 = half * 512
                    t1 = self.V(lambda e, bank=bank, t=t, c0=c0: e.tensor_tensor(
                        sb["otm"][:, c0:c0 + 512], ps[bank][:],
                        sb["gtm"][:, t * 1024 + c0: t * 1024 + c0 + 512], op=ALU.mult),
                        waits=[("t", mmt)] + g_done + (self.free["otm"] if half == 0 else []),
                        inc=True)
                    t2 = self.V(lambda e, t=t, c0=c0: e.tensor_tensor(
                        sb["otm"][:, c0:c0 + 512], sb["otm"][:, c0:c0 + 512],
                        sb["xt"][:, (t % 2) * 1024 + c0: (t % 2) * 1024 + c0 + 512], op=ALU.add),
                        waits=[(("dxA", "dxB")[t % 2], dx), ("v", t1)], inc=True)
                    self.bankfree[bank] = [("v", t2)]
                gs = self.G(lambda e, t=t, base=base: e.dma_start(
                    out=dr["yout"][base + t * 128: base + (t + 1) * 128, :], in_=sb["otm"][:, 0:1024]),
                    waits=[("v", self.tick("v"))], sem="dgo")
                st_dg.append(gs)
                self.xt_free[t % 2] = [("v", self.tick("v"))]
                self.free["otm"] = [("dgo", gs)]
            self.free["xs"] = [("t", self.tick("t"))]
            self.free["gtm"] = [("v", self.tick("v"))]
            self.free["regB"] = [("v", self.tick("v"))]
            self.free["xnb"] = [("t", self.tick("t"))]
            self.free["y"] = [("t", self.tick("t"))]
            self.free["qp"] = [("t", self.tick("t"))]
        self.out_done = [("dgo", self.tick("dgo"))]

    TAPS = {
        "y": ("y", F32), "xnb": ("xnb", BF16), "qp": ("qp", BF16), "kp": ("kp", BF16),
        "xs": ("xs", BF16), "vaug": ("vaug", BF16), "afm": ("afm", BF16),
        "kv": ("kv", F32), "ptsb": ("ptsb", BF16), "gtm": ("gtm", F32),
        "stats": ("stats", F32), "xt": ("xt", F32), "gated": ("afm", BF16),
        "small": ("small", F32), "otm": ("otm", F32), "kvb2": ("kvb2", BF16),
    }

    def emit_taps(self):
        names = self.cfg.get("taps", ())
        if not names:
            return
        sb = self.sb
        fw = []
        vq = self.V(lambda e: e.tensor_copy(sb["stats"][:, 0:1], sb["stats"][:, 1:2]), inc=True)
        sq = self.S(lambda e: e.copy(sb["stats"][:, 2:3], sb["stats"][:, 3:4]), inc=True)
        fw = [("v", vq), ("s", sq), ("t", self.tick("t"))]
        caps = {"afm": 8 * CH, "gtm": 4096, "gel": 1024, "gated": HT * CH}
        for n in names:
            src, dt = self.TAPS[n]
            hh = self.sb[src]
            shape = list(hh.shape)
            if n in caps:
                shape = [shape[0], caps[n]]
            tap = self.nc.dram_tensor("tap_" + n, shape, dt, kind="ExternalOutput").ap()
            self.G(lambda e, tap=tap, hh=hh, shape=shape: e.dma_start(
                out=tap[:, :], in_=hh[0:shape[0], 0:shape[1]]),
                   waits=fw, sem="dgo")

    def emit_phases(self):
        self.y_state = []
        self.kv_tick = 0
        nl = self.cfg.get("nlayers", NL)
        for s in range(self.cfg.get("nseq", NSEQ)):
            self.seq_init(s)
            for c in range(self.cfg.get("nchunks", NCH)):
                self.phase1(s, c)
                self.y_state = list(self.y_ready)
                for l in range(nl):
                    qd, kd, vd = self.attn(s, c, l)
                    self.attn_core(s, c, l, qd, kd, vd)
                    if not self.cfg.get("attn_only", False):
                        self.ffn(s, c, l)
                if self.cfg.get("output", True):
                    self.chunk_end(s, c)
            if self.cfg.get("output", True):
                self.output(s)

    def build(self):
        self.declare_dram()
        with self.es:
            self.alloc()
            self.init_stream()
            self.prologue()
            saved_q = {k: list(v) for k, v in self.P.q.items()}
            saved_semc = dict(self.P.semc)
            saved_pending = {k: list(v) for k, v in getattr(self.P, "pending", {e: [] for e in self.P.ENGINES}).items()}
            saved_free = {k: list(v) for k, v in self.free.items()}
            saved_bank = {k: list(v) for k, v in self.bankfree.items()}
            self.collect = True
            self.emit_phases()
            self.collect = False
            self.P.q = saved_q
            self.P.semc = saved_semc
            self.P.pending = saved_pending
            self.free = saved_free
            self.bankfree = saved_bank
            self.xt_free = [[], []]
            self.xnt_free = [[], []]
            self.wslot = 0
            self.wfree = [[] for _ in range(6)]
            self.wconv_waited = [False] * 6
            self.emit_weight_cache()
            self.emit_phases()
            self.emit_taps()
            final = self.tick("dgo")
            if final:
                self.P.op("sync", lambda e: None, waits=[("dgo", final)])
            block = self.es.enter_context(self.nc.Block())
            self.P.finalize(block)
        return self.nc


def build_core(cfg):
    core = Core(cfg)
    nc = core.build()
    return nc


_NCORES = 8


def kernel(**inputs):
    from concourse.bass_utils import run_bass_kernel_spmd
    x = np.ascontiguousarray(np.asarray(inputs["L0_attn_x"], dtype=np.float32))
    B = x.shape[0]
    consts = make_consts()
    w = make_weights({k: np.asarray(v) for k, v in inputs.items()})
    base = {}
    for k in ("wqT", "wkT", "wvT", "woT", "wqkv", "wo_i", "w1", "w2",
              "b1", "b2", "lng", "lnb", "qs", "ks"):
        base[k] = w[k]
    for k in ("cs8", "ss8", "cik", "sik"):
        base[k] = consts[k]
    base["masks"] = np.ascontiguousarray(
        np.concatenate([consts["mask_n0"], consts["mask_n1"], consts["mask_f0"], consts["mask_f1"]], 1))
    base["identb"] = consts["ident_b"]
    base["identf"] = consts["ident_f"]
    base["rmat"] = consts["rmat"]
    base["l2ones"] = consts["l2ones"]
    base["l2pat"] = consts["l2pat"]
    base["onesb"] = consts["ones"]
    cfg = {"nseq": NSEQ, "nchunks": NCH, "nlayers": NL, "output": True, "taps": ()}
    nc = build_core(cfg)
    in_maps = []
    per = B // _NCORES
    for ci in range(_NCORES):
        m = dict(base)
        m["xin"] = np.ascontiguousarray(x[ci * per:(ci + 1) * per].reshape(per * S, D))
        in_maps.append(m)
    res = run_bass_kernel_spmd(nc, in_maps, core_ids=list(range(_NCORES)))
    out = np.empty((B, S, D), dtype=np.float32)
    for ci in range(_NCORES):
        out[ci * per:(ci + 1) * per] = np.asarray(
            res.results[ci]["yout"], dtype=np.float32).reshape(per, S, D)
    return out


# version 05
# speedup vs baseline: 1.3342x; 1.0332x over previous
import numpy as np
import ml_dtypes
import contextlib
import concourse.bass as bass
import concourse.mybir as mybir

F32 = mybir.dt.float32
BF16 = mybir.dt.bfloat16
AF = mybir.ActivationFunctionType
ALU = mybir.AluOpType
AX = mybir.AxisListType.X

D = 1024
CH = 512
S = 2048
NCH = S // CH
NSEQ = 2
H = 16
DH = 64
W = 256
NL = 6
HID = 2816
HT = HID // 128
NEG = -1.0e30

bf = ml_dtypes.bfloat16


def make_consts():
    c = {}
    c["ident_b"] = np.eye(128, dtype=bf)
    c["ident_f"] = np.eye(128, dtype=np.float32)
    R = np.zeros((128, 128), dtype=np.float64)
    for h0 in (0, 64):
        for m in range(32):
            R[h0 + m + 32, h0 + m] = -1.0
            R[h0 + m, h0 + m + 32] = 1.0
    c["rmat"] = R.astype(bf)
    lo = np.zeros((128, 8, 16), dtype=np.float64)
    for f in range(8):
        lo[0:64, f, 2 * f] = 1.0
        lo[64:128, f, 2 * f + 1] = 1.0
    c["l2ones"] = lo.reshape(128, 128).astype(bf)
    pat = np.zeros((16, 8, 128), dtype=np.float64)
    for f in range(8):
        pat[2 * f, f, 0:64] = 1.0
        pat[2 * f + 1, f, 64:128] = 1.0
    c["l2pat"] = pat.reshape(16, 1024).astype(bf)
    c["ones"] = np.ones((128, 128), dtype=bf)
    half = np.arange(0, DH, 2, dtype=np.float64)
    inv_freq = 1.0 / 10000.0 ** (half / DH)
    t = np.arange(S, dtype=np.float64)
    freqs = t[:, None] * inv_freq[None, :]
    freqs = np.concatenate([freqs, freqs], -1)
    base_scale = (half + 0.4 * DH) / (1.4 * DH)
    power = (t - S // 2) / (W // 2)
    scale = base_scale[None, :] ** power[:, None]
    scale = np.concatenate([scale, scale], -1)
    cos, sin = np.cos(freqs), np.sin(freqs)
    qk_scale = 8.0
    cs8 = (cos * scale * qk_scale).T
    ss8 = (sin * scale * qk_scale).T
    cik = (cos / scale).T
    sik = (sin / scale).T
    for name, tab in (("cs8", cs8), ("ss8", ss8), ("cik", cik), ("sik", sik)):
        c[name] = np.concatenate([tab, tab], 0).astype(np.float32)
    def mask_tile(i_base, j_base, first):
        m = np.zeros((128, 384), dtype=np.float64)
        for ii in range(128):
            i = i_base + ii
            for jj in range(384):
                j = j_base + jj
                valid = (i <= j <= i + W) and not (first and j < 256)
                if not valid:
                    m[ii, jj] = NEG
        return m.astype(bf)
    c["mask_n0"] = mask_tile(0, 0, False)
    c["mask_n1"] = mask_tile(128, 128, False)
    c["mask_f0"] = mask_tile(0, 0, True)
    c["mask_f1"] = mask_tile(128, 128, True)
    return c


def make_weights(iw):
    w = {}
    w["wqT"] = np.ascontiguousarray(iw["L0_attn_W_q"].T)
    w["wkT"] = np.ascontiguousarray(iw["L0_attn_W_k"].T)
    w["wvT"] = np.ascontiguousarray(iw["L0_attn_W_v"].T)
    w["woT"] = np.ascontiguousarray(iw["L0_attn_W_o"].T)
    w["wqkv"] = np.ascontiguousarray(iw["L0_ffn_Wqkv"])
    w["wo_i"] = np.ascontiguousarray(iw["L0_ffn_Wo"])
    W1 = iw["L0_ffn_W1"]
    w1p = np.zeros((NL, D, 2 * HID), dtype=np.float32)
    w1p[:, :, 0:2730] = W1[:, :, 0:2730]
    w1p[:, :, HID:HID + 2730] = W1[:, :, 2730:5460]
    w["w1"] = w1p
    b1 = iw["L0_ffn_b1"]
    b1p = np.zeros((NL, 2 * HID), dtype=np.float32)
    b1p[:, 0:2730] = b1[:, 0:2730]
    b1p[:, HID:HID + 2730] = b1[:, 2730:5460]
    w["b1"] = np.ascontiguousarray(b1p.reshape(NL, 2 * HT, 128).transpose(2, 0, 1).reshape(128, NL * 2 * HT))
    W2 = iw["L0_ffn_W2"]
    w2p = np.zeros((NL, HID, D), dtype=np.float32)
    w2p[:, 0:2730, :] = W2
    w["w2"] = w2p
    def pp(v):
        return np.ascontiguousarray(v.reshape(NL, 8, 128).transpose(2, 0, 1).reshape(128, NL * 8))
    w["b2"] = pp(iw["L0_ffn_b2"])
    w["lng"] = pp(iw["L0_ffn_ln_g"])
    w["lnb"] = pp(iw["L0_ffn_ln_b"])
    w["qs"] = np.ascontiguousarray(np.tile(iw["L0_ffn_q_scale"], (1, 2)).T)
    w["ks"] = np.ascontiguousarray(np.tile(iw["L0_ffn_k_scale"], (1, 2)).T)
    return {k: np.asarray(v, dtype=(bf if v.dtype == bf else np.float32)) for k, v in w.items()}


class Prog:
    ENGINES = ("sync", "tensor", "vector", "scalar", "gpsimd")

    def __init__(self, nc, es):
        self.nc = nc
        self.es = es
        self.q = {e: [] for e in self.ENGINES}
        self.semc = {}
        self.sems = {}

    def sem(self, name):
        h = self.es.enter_context(self.nc.semaphore(name))
        self.semc[name] = 0
        self.sems[name] = h
        return name

    def tick(self, name):
        return self.semc[name]

    DEFER = ("vector", "scalar")

    def set_spacer(self, eng, fn):
        self.spacer_fn = getattr(self, "spacer_fn", {})
        self.spacer_fn[eng] = fn

    def _emit_spacer(self, eng, carry):
        fn = self.spacer_fn[eng]

        def cb(e, fn=fn, carry=carry):
            ins = fn(e)
            for h, amt in carry:
                ins = ins.then_inc(h, amt)
        self.q[eng].append(cb)

    def op(self, eng, fn, waits=(), inc=None, amt=1):
        for s, v in waits:
            assert v <= self.semc[s], f"wait-ahead-of-inc on {s}: {v} > {self.semc[s]}"
        self.pending = getattr(self, "pending", {e: [] for e in self.ENGINES})
        waits_h = [(self.sems[s], v) for s, v in waits if v > 0]
        defer = eng in self.DEFER
        if defer and self.pending[eng] and waits_h:
            self._emit_spacer(eng, self.pending[eng])
            self.pending[eng] = []
        carry = list(self.pending[eng]) if defer else []
        if defer:
            self.pending[eng] = []
        cap_inc = None
        if inc is not None:
            self.semc[inc] += amt
            if not defer:
                cap_inc = (self.sems[inc], amt)

        def cb(e, fn=fn, waits_h=waits_h, cap_inc=cap_inc, carry=carry):
            for h, v in waits_h:
                e.wait_ge(h, v)
            ins = fn(e)
            for h, amt2 in carry:
                assert ins is not None, "carry needs a real instruction"
                ins = ins.then_inc(h, amt2)
            if cap_inc is not None:
                assert ins is not None, "inc requested but fn returned None"
                ins.then_inc(*cap_inc)
        self.q[eng].append(cb)
        if inc is not None and defer:
            self.pending[eng].append((self.sems[inc], amt))
        return self.semc[inc] if inc is not None else None

    def finalize(self, block):
        for eng in self.DEFER:
            if getattr(self, "pending", {}).get(eng):
                self._emit_spacer(eng, self.pending[eng])
                self.pending[eng] = []
        for name in self.ENGINES:
            cbs = self.q[name]
            if not cbs:
                continue
            def section(e, cbs=cbs):
                for cb in cbs:
                    cb(e)
            getattr(block, name)(section)


class Core:

    def __init__(self, cfg):
        self.cfg = cfg
        self.nc = bass.Bass("TRN2", target_bir_lowering=False)
        self.es = contextlib.ExitStack()
        self.taps = {}

    def declare_dram(self):
        nc = self.nc
        dr = {}
        dr["xin"] = nc.dram_tensor("xin", [NSEQ * S, D], F32, kind="ExternalInput").ap()
        dr["yout"] = nc.dram_tensor("yout", [NSEQ * S, D], F32, kind="ExternalOutput").ap()
        for n, shp in (("wqT", [D, D]), ("wkT", [D, D]), ("wvT", [D, D]), ("woT", [D, D])):
            dr[n] = nc.dram_tensor(n, shp, F32, kind="ExternalInput").ap()
        dr["wqkv"] = nc.dram_tensor("wqkv", [NL, D, 3072], F32, kind="ExternalInput").ap()
        dr["wo_i"] = nc.dram_tensor("wo_i", [NL, D, D], F32, kind="ExternalInput").ap()
        dr["w1"] = nc.dram_tensor("w1", [NL, D, 2 * HID], F32, kind="ExternalInput").ap()
        dr["w2"] = nc.dram_tensor("w2", [NL, HID, D], F32, kind="ExternalInput").ap()
        for n, cc in (("b1", NL * 2 * HT), ("b2", NL * 8), ("lng", NL * 8), ("lnb", NL * 8),
                      ("qs", NL), ("ks", NL)):
            dr[n] = nc.dram_tensor(n, [128, cc], F32, kind="ExternalInput").ap()
        for n in ("cs8", "ss8", "cik", "sik"):
            dr[n] = nc.dram_tensor(n, [128, S], F32, kind="ExternalInput").ap()
        dr["masks"] = nc.dram_tensor("masks", [128, 4 * 384], BF16, kind="ExternalInput").ap()
        dr["identb"] = nc.dram_tensor("identb", [128, 128], BF16, kind="ExternalInput").ap()
        dr["identf"] = nc.dram_tensor("identf", [128, 128], F32, kind="ExternalInput").ap()
        dr["rmat"] = nc.dram_tensor("rmat", [128, 128], BF16, kind="ExternalInput").ap()
        dr["l2ones"] = nc.dram_tensor("l2ones", [128, 128], BF16, kind="ExternalInput").ap()
        dr["l2pat"] = nc.dram_tensor("l2pat", [16, 1024], BF16, kind="ExternalInput").ap()
        dr["onesb"] = nc.dram_tensor("onesb", [128, 128], BF16, kind="ExternalInput").ap()
        dr["wbfd"] = nc.dram_tensor("wbfd", [128, self.cfg.get("wcols", 700000)], BF16, kind="Internal").ap()
        dr["xnsp"] = nc.dram_tensor("xnsp", [NSEQ * NCH * 8 * 128, CH], BF16, kind="Internal").ap()
        dr["ysp"] = nc.dram_tensor("ysp", [NSEQ * NCH * 8 * 128, CH], F32, kind="Internal").ap()
        self.dr = dr

    def alloc(self):
        nc, es = self.nc, self.es
        sb = {}

        def t(name, shape, dt):
            sb[name] = es.enter_context(nc.sbuf_tensor("s_" + name, shape, dt))
            return sb[name]

        t("y", [128, 8 * CH], F32)
        t("kcar", [128, NL * 8 * 256], BF16)
        t("vcar", [128, NL * 2 * 1040], BF16)
        t("acar", [128, NL * 4], F32)
        t("fcar", [128, NL * 4], F32)
        for n in ("cs8", "ss8", "cik", "sik"):
            t(n + "b", [128, S], BF16)
        t("masks", [128, 4 * 384], BF16)
        t("identb", [128, 128], BF16)
        t("identf", [128, 128], F32)
        t("rmat", [128, 128], BF16)
        t("l2ones", [128, 128], BF16)
        t("l2pat", [16, 1024], BF16)
        t("onesb", [128, 128], BF16)
        t("consts", [128, 8], F32)
        t("b1s", [128, NL * 2 * HT], F32)
        t("b2s", [128, NL * 8], F32)
        t("lngs", [128, NL * 8], F32)
        t("lnbs", [128, NL * 8], F32)
        t("qss", [128, NL], F32)
        t("kss", [128, NL], F32)
        t("kv", [64, 1024], F32)
        t("kvb2", [128, 512], BF16)
        t("xt", [128, 2 * 1024], F32)
        t("regB", [128, 22 * CH // 2], F32)
        t("xs", [128, 8 * CH], BF16)
        t("xnb", [128, 8 * CH], BF16)
        t("qp", [128, 8 * CH], BF16)
        t("kp", [128, 8 * CH], BF16)
        t("sqbf", [128, 2 * CH], BF16)
        t("rot", [128, 4 * CH], BF16)
        t("vaug", [128, 4 * 1040], BF16)
        t("pbuf", [128, 4 * CH], BF16)
        t("ptsb", [128, 2 * 1024], BF16)
        t("wsg", [128, 2 * 1408], F32)
        t("wbf", [128, 2 * 1408], BF16)
        t("stats", [128, 32], F32)
        t("nmax", [128, 8], F32)
        t("small", [64, 2048], F32)
        sb["invt"] = sb["small"][0:16, 0:1024]
        sb["lnt"] = sb["small"][0:1, 0:2048]
        sb["rs"] = sb["small"][0:1, 0:512]
        sb["rb"] = sb["small"][0:64, 512:1024]
        sb["invtb"] = sb["small"].bitcast(BF16)[0:16, 2048:2560]
        t("otm", [128, 1024], F32)
        sb["gtm"] = sb["regB"]
        sb["gel"] = sb["pbuf"][:, 0:1024]
        sb["afm"] = sb["regB"].bitcast(BF16)
        sb["vfs"] = sb["xt"].bitcast(BF16)
        self.sb = sb

        ps = {}
        for n in ("A0", "A1", "B0", "B1", "C0", "C1"):
            ps[n] = es.enter_context(nc.psum_tensor(n, [128, 512], F32))
        ps["PT0"] = es.enter_context(nc.psum_tensor("PT0", [128, 1024], BF16))
        ps["PT1"] = es.enter_context(nc.psum_tensor("PT1", [128, 1024], BF16))
        self.ps = ps

        t("spc", [1, 8], F32)
        t("onesf", [1, 128], F32)
        t("rqs", [128, 128], BF16)
        self.P = Prog(nc, es)
        self.P.set_spacer("vector", lambda e: e.tensor_copy(sb["spc"][0:1, 0:1], sb["spc"][0:1, 1:2]))
        self.P.set_spacer("scalar", lambda e: e.copy(sb["spc"][0:1, 2:3], sb["spc"][0:1, 3:4]))
        for s in ("dwA", "dwB", "dwC", "dwD", "dwE", "dwF", "dxA", "dxB", "dco",
                  "dspy", "dspx", "dgx", "dgy", "dgo", "dgw", "t", "v", "s"):
            self.P.sem(s)
        self.bankfree = {n: [] for n in ps}
        self.free = {n: [] for n in ("xnb", "y", "qp", "kp", "vaug", "afm", "xs", "regB",
                                     "qbf", "sqbf", "rot", "pbuf", "ptsb", "gel", "vfs",
                                     "xt", "gtm", "otm")}

    def T(self, fn, waits=(), inc=False):
        return self.P.op("tensor", fn, waits, "t" if inc else None)

    def V(self, fn, waits=(), inc=False):
        return self.P.op("vector", fn, waits, "v" if inc else None)

    def S(self, fn, waits=(), inc=False):
        return self.P.op("scalar", fn, waits, "s" if inc else None)

    def Sy(self, fn, waits=(), sem="dwA"):
        return self.P.op("sync", fn, waits, sem, amt=16)

    def G(self, fn, waits=(), sem="dgo"):
        return self.P.op("gpsimd", fn, waits, sem, amt=16)

    def tick(self, s):
        return self.P.tick(s)

    def Vsp(self):
        self.P.op("vector", lambda e: e.tensor_copy(
            self.sb["spc"][0:1, 4:5], self.sb["spc"][0:1, 5:6]))

    def Ssp(self):
        self.P.op("scalar", lambda e: e.copy(
            self.sb["spc"][0:1, 6:7], self.sb["spc"][0:1, 7:8]))

    def init_stream(self):
        self.wslot = 0
        self.wfree = [[] for _ in range(6)]
        self.collect = False
        self.wreg = {}
        self.wreg_order = []
        self.wreg_cols = 0
        self.wconv_waited = [False] * 6

    def stream_w(self, dram_ap3, ncols, key=None):
        assert key is not None
        if self.collect:
            if key not in self.wreg:
                self.wreg[key] = (self.wreg_cols, ncols)
                self.wreg_order.append((key, dram_ap3, ncols))
                self.wreg_cols += ncols
            self.wsem_used = "dwA"
            self.wbuf_used = self.sb["wbf"]
            self.cur_wslot = 0
            return 0, 0
        off, nc2 = self.wreg[key][0], self.wreg[key][1]
        assert nc2 == ncols
        sl = self.wslot
        self.wslot = (self.wslot + 1) % 6
        sb = self.sb
        wsem = ("dwA", "dwB", "dwC", "dwD", "dwE", "dwF")[sl]
        extra = [] if self.wconv_waited[sl] else [("dgw", self.tick("dgw"))]
        self.wconv_waited[sl] = True
        buf = sb["wbf"] if sl < 2 else sb["wsg"].bitcast(BF16)
        base = (sl if sl < 2 else sl - 2) * 1408
        dw = self.Sy(lambda e, off=off, buf=buf, base=base, ncols=ncols: e.dma_start(
            out=buf[:, base: base + ncols],
            in_=self.dr["wbfd"][:, off:off + ncols]),
            waits=self.wfree[sl] + extra, sem=wsem)
        self.cur_wslot = sl
        self.wsem_used = wsem
        self.wbuf_used = buf
        return base, dw

    def wdone(self, sl_base, t_tick):
        self.wfree[self.cur_consumed] = [("t", t_tick)]

    def emit_weight_cache(self):
        sb = self.sb
        wsg_free = [[], []]
        wbf_free = [[], []]
        for i, (key, ap3, ncols) in enumerate(self.wreg_order):
            sl = i % 2
            kt = ncols // 128
            wsem = ("dxA", "dxB")[sl]
            dw = self.Sy(lambda e, a=ap3, sl=sl, kt=kt: e.dma_start(
                out=sb["wsg"][:, sl * 1408: sl * 1408 + kt * 128].rearrange("p (k m) -> p k m", k=kt),
                in_=a), waits=wsg_free[sl], sem=wsem)
            ct = self.S(lambda e, sl=sl, n=ncols: e.copy(
                sb["wbf"][:, sl * 1408: sl * 1408 + n], sb["wsg"][:, sl * 1408: sl * 1408 + n]),
                waits=[(wsem, dw)] + wbf_free[sl], inc=True)
            wsg_free[sl] = [("s", ct)]
            off = self.wreg[key][0]
            gw = self.G(lambda e, sl=sl, off=off, n=ncols: e.dma_start(
                out=self.dr["wbfd"][:, off:off + n],
                in_=sb["wbf"][:, sl * 1408: sl * 1408 + n]),
                waits=[("s", ct)], sem="dgw")
            wbf_free[sl] = [("dgw", gw)]
        self.wfree = [[] for _ in range(6)]

    def prologue(self):
        sb, dr = self.sb, self.dr
        self.V(lambda e: e.memset(sb["onesf"][0:1, :], 1.0))
        self.V(lambda e: e.memset(sb["spc"][0:1, 0:2], 0.0))
        self.V(lambda e: e.memset(sb["spc"][0:1, 4:6], 0.0))
        self.S(lambda e: e.memzero(sb["spc"][0:1, 2:4]))
        self.S(lambda e: e.memzero(sb["spc"][0:1, 6:8]))
        for n, d in (("masks", "masks"), ("identb", "identb"), ("identf", "identf"),
                     ("rmat", "rmat"), ("l2ones", "l2ones"), ("l2pat", "l2pat"),
                     ("onesb", "onesb"), ("b1s", "b1"), ("b2s", "b2"), ("lngs", "lng"),
                     ("lnbs", "lnb"), ("qss", "qs"), ("kss", "ks")):
            self.Sy(lambda e, n=n, d=d: e.dma_start(out=sb[n][:], in_=dr[d][:, :]), sem="dco")
        for rnd, (na, nb) in enumerate((("cs8", "ss8"), ("cik", "sik"))):
            wv = [("v", self.tick("v"))] if rnd else []
            ta = self.Sy(lambda e, na=na, r=rnd: e.dma_start(out=sb["gtm"][:, 0:2048], in_=dr[na][:, :]),
                         waits=wv, sem="dxA")
            tb = self.Sy(lambda e, nb=nb: e.dma_start(out=sb["gtm"][:, 2048:4096], in_=dr[nb][:, :]),
                         waits=wv, sem="dxB")
            self.V(lambda e, na=na: e.tensor_copy(sb[na + "b"][:], sb["gtm"][:, 0:2048]),
                   waits=[("dxA", ta)], inc=True)
            self.V(lambda e, nb=nb: e.tensor_copy(sb[nb + "b"][:], sb["gtm"][:, 2048:4096]),
                   waits=[("dxB", tb)], inc=True)
        cvals = [(-1.0, 0), (1e-5, 1), (1.0 / 1024.0, 2), (1e-24, 3), (0.125, 4), (1.0, 5)]
        for v, i in cvals:
            self.V(lambda e, v=v, i=i: e.memset(sb["consts"][:, i:i + 1], v))
            self.nc.const_aps.register(F32, v, sb["consts"][:, i:i + 1])
        self.cap = {v: sb["consts"][:, i:i + 1] for v, i in cvals}
        self.V(lambda e: e.memset(sb["small"][:], 0.0))
        self.V(lambda e: e.memset(sb["kcar"][:], 0.0))
        self.V(lambda e: e.memset(sb["vcar"][:], 0.0))
        self.V(lambda e: e.memset(
            sb["vaug"][:].rearrange("p (t h c) -> p t h c", t=4, c=65)[:, :, :, 64:65], 1.0), inc=True)
        self.prol_v = self.tick("v")
        self.free["regB"] = [("v", self.prol_v)]
        pw = [("dco", self.tick("dco"))]
        for eng in ("tensor", "vector", "scalar", "gpsimd"):
            self.P.op(eng, lambda e: None, waits=pw)

    def seq_init(self, s):
        sb = self.sb
        self.V(lambda e: e.memset(sb["acar"][:], 0.0))
        self.V(lambda e: e.memset(sb["fcar"][:], 0.0))
        self.V(lambda e: e.memset(sb["kv"][:], 0.0), inc=True)
        self.seq_v = self.tick("v")

    def phase1(self, s, c):
        sb, dr, ps = self.sb, self.dr, self.ps
        parts = 9
        ntiles = 4
        base = s * S + c * CH
        st = sb["stats"]
        xn_done_s = None
        self.xt_free = getattr(self, "xt_free", [[], []])
        self.xnt_free = getattr(self, "xnt_free", [[], []])
        for t in range(ntiles):
            sl = t % 2
            xsem = ("dxA", "dxB")[sl]
            dx = self.Sy(lambda e, t=t, sl=sl: e.dma_start(
                out=sb["xt"][:, sl * 1024:(sl + 1) * 1024],
                in_=dr["xin"][base + t * 128: base + (t + 1) * 128, :]),
                waits=self.xt_free[sl] + ([("v", self.prol_v)] if (s == 0 and c == 0 and t == 0) else []),
                sem=xsem)
            xsl = sb["xt"][:, sl * 1024:(sl + 1) * 1024]
            NOSTATS = False
            if NOSTATS:
                ss_t = self.S(lambda e, xsl=xsl, t=t: e.copy(
                    sb["regB"][:, 2048:2176], xsl[:, 0:128]),
                    waits=[(xsem, dx)] + (self.free["regB"] if t == 0 else []), inc=True)
            else:
                ss_t = self.S(lambda e, xsl=xsl, t=t: e.activation(
                    sb["regB"][:, 2048:3072], xsl, AF.Square, accum_out=st[:, t:t + 1]),
                    waits=[(xsem, dx)] + (self.free["regB"] if t == 0 else []), inc=True)
            sm_t = self.V(lambda e, xsl=xsl, t=t: e.reduce_sum(st[:, 4 + t:5 + t], xsl, axis=AX),
                          waits=[(xsem, dx)], inc=True)
            mu_t = self.V(lambda e, t=t: e.tensor_scalar_mul(st[:, 8 + t:9 + t], st[:, 4 + t:5 + t], 1.0 / 1024.0),
                          waits=[("v", sm_t)], inc=True)
            m2_t = self.V(lambda e, t=t: e.tensor_scalar_mul(st[:, 12 + t:13 + t], st[:, t:t + 1], 1.0 / 1024.0),
                          waits=[("s", ss_t)], inc=True)
            mu2_t = self.V(lambda e, t=t: e.tensor_mul(st[:, 16 + t:17 + t], st[:, 8 + t:9 + t], st[:, 8 + t:9 + t]),
                           waits=[("v", mu_t)], inc=True)
            var_t = self.V(lambda e, t=t: e.tensor_sub(st[:, 20 + t:21 + t], st[:, 12 + t:13 + t],
                                                       st[:, 16 + t:17 + t]),
                           waits=[("v", mu2_t), ("v", m2_t)], inc=True)
            if NOSTATS:
                ln_t = self.S(lambda e, t=t: e.copy(st[:, 12 + t:13 + t], st[:, 20 + t:21 + t]),
                              waits=[("v", var_t)], inc=True)
                rstd_t = self.S(lambda e, t=t: e.copy(st[:, 24 + t:25 + t], st[:, 12 + t:13 + t]),
                                waits=[("s", ln_t)], inc=True)
            else:
                ln_t = self.S(lambda e, t=t: e.activation(st[:, 12 + t:13 + t], st[:, 20 + t:21 + t], AF.Ln,
                                                          bias=self.cap[1e-5], scale=1.0),
                              waits=[("v", var_t)], inc=True)
                rstd_t = self.S(lambda e, t=t: e.activation(st[:, 24 + t:25 + t], st[:, 12 + t:13 + t],
                                                            AF.Exp, scale=-0.5),
                                waits=[("s", ln_t)], inc=True)
            nmr_t = self.V(lambda e, t=t: e.scalar_tensor_tensor(
                st[:, 28 + t:29 + t], st[:, 8 + t:9 + t], -1.0, st[:, 24 + t:25 + t],
                op0=ALU.mult, op1=ALU.mult), waits=[("s", rstd_t)], inc=True)
            xsl2 = sb["regB"][:, (sl * 1024):(sl + 1) * 1024]
            xn_t = self.V(lambda e, xsl=xsl, xsl2=xsl2, t=t: e.tensor_scalar(
                xsl2, xsl, st[:, 24 + t:25 + t], st[:, 28 + t:29 + t],
                op0=ALU.mult, op1=ALU.add),
                waits=[("v", nmr_t)] + self.xnt_free[sl], inc=True)
            self.xt_free[sl] = [("v", xn_t)]
            EV = "vs"
            if parts < 1:
                self.xnt_free[sl] = []
                last_evs = []
                continue
            evs = []
            for bi, bank in enumerate(("A0", "A1")):
                tt = None
                for d in range(4 * bi, 4 * bi + 4):
                    tt = self.T(lambda e, bank=bank, d=d, xsl2=xsl2: e.transpose(
                        ps[bank][:, (d % 4) * 128:((d % 4) + 1) * 128],
                        xsl2[:, d * 128:(d + 1) * 128], sb["identf"][:]),
                        waits=([("v", xn_t)] + self.bankfree[bank]) if d % 4 == 0 else (),
                        inc=(d % 4 == 3))
                tt = self.tick("t")
                import os as _os
                if EV == "none":
                    yv = self.V(lambda e: e.tensor_copy(sb["stats"][:, 5:6], sb["stats"][:, 6:7]),
                                waits=[("t", tt)], inc=True)
                    xv = self.S(lambda e: e.copy(sb["stats"][:, 6:7], sb["stats"][:, 7:8]),
                                waits=[("t", tt)], inc=True)
                elif EV == "v":
                    yv = self.V(lambda e, bank=bank, bi=bi, t=t: e.tensor_copy(
                        sb["y"][:].rearrange("p (d m) -> p d m", d=8)[:, 4 * bi:4 * bi + 4, t * 128:(t + 1) * 128],
                        ps[bank][:].rearrange("p (d m) -> p d m", d=4)),
                        waits=[("t", tt)] + (self.free["y"] if (t == 0 and bi == 0) else []), inc=True)
                    xv = self.S(lambda e: e.copy(sb["stats"][:, 6:7], sb["stats"][:, 7:8]),
                                waits=[("t", tt)], inc=True)
                else:
                    yv = self.V(lambda e, bank=bank, bi=bi, t=t: e.tensor_copy(
                        sb["y"][:].rearrange("p (d m) -> p d m", d=8)[:, 4 * bi:4 * bi + 4, t * 128:(t + 1) * 128],
                        ps[bank][:].rearrange("p (d m) -> p d m", d=4)),
                        waits=[("t", tt)] + (self.free["y"] if (t == 0 and bi == 0) else []), inc=True)
                    xv = None
                    for dd in range(4):
                        d2 = 4 * bi + dd
                        xv = self.S(lambda e, d2=d2, t=t: e.copy(
                            sb["xnb"][:, d2 * CH + t * 128: d2 * CH + (t + 1) * 128],
                            sb["y"][:, d2 * CH + t * 128: d2 * CH + (t + 1) * 128]),
                            waits=([("v", yv)] + (self.free["xnb"] if (t == 0 and bi == 0) else []))
                            if dd == 0 else (),
                            inc=(dd == 3))
                self.bankfree[bank] = [("v", yv), ("s", xv)]
                evs += [("v", yv), ("s", xv)]
            self.xnt_free[sl] = [("t", self.tick("t"))]
            last_evs = evs
        self.xn_ready = last_evs
        del last_evs
        if parts < 2:
            self.y_ready = self.xn_ready
            self.kv_tick = 0
            return
        self.y_ready = self.xn_ready
        if parts < 1:
            self.kv_tick = 0
            return
        gsp = self.G(lambda e, s=s, c=c: e.dma_start(
            out=dr["xnsp"].rearrange("(k d p) m -> k p d m", d=8, p=128)[s * NCH + c],
            in_=sb["xnb"][:].rearrange("p (d m) -> p d m", d=8)),
            waits=self.xn_ready, sem="dspx")
        self.xnb_spilled = gsp
        if parts < 3:
            self.kv_tick = 0
            return
        wk = dr["wkT"].rearrange("(t p) m -> p t m", p=128)
        wv = dr["wvT"].rearrange("(t p) m -> p t m", p=128)
        for wi, (wsrc, dst) in enumerate(((wk, "qp"), (wv, "kp"))):
            for blk in range(8):
                cb, ct = self.stream_w(wsrc[:, :, blk * 128:(blk + 1) * 128], 1024, key=("okv", wi, blk))
                wsu = self.wsem_used
                wb = self.wbuf_used
                wcs = self.cur_wslot
                bank = ("B0", "B1")[blk % 2]
                mmt = None
                for t in range(4):
                    for d in range(8):
                        mmt = self.T(lambda e, bank=bank, t=t, d=d, cb=cb, wb=wb: e.matmul(
                            ps[bank][:, t * 128:(t + 1) * 128],
                            sb["xnb"][:, d * 512 + t * 128: d * 512 + (t + 1) * 128],
                            wb[:, cb + d * 128: cb + (d + 1) * 128],
                            start=(d == 0), stop=(d == 7)),
                            waits=([(wsu, ct)] + self.xn_ready + self.bankfree[bank]) if (t == 0 and d == 0) else (),
                            inc=(t == 3 and d == 7))
                mmt = self.tick("t")
                self.cur_consumed = wcs; self.wdone(cb, mmt)
                ev = self.V(lambda e, dst=dst, blk=blk, bank=bank: e.tensor_copy(
                    sb[dst][:].rearrange("p (t m) -> p t m", t=4)[:, :, blk * 128:(blk + 1) * 128],
                    ps[bank][:].rearrange("p (t m) -> p t m", t=4)),
                    waits=[("t", mmt)] + (self.free["qp" if dst == "qp" else "kp"] if blk == 0 else []),
                    inc=True)
                self.bankfree[bank] = [("v", ev)]
        kvready = [("v", self.tick("v"))]
        self.free["xnb"] = [("t", self.tick("t"))]
        if parts < 4:
            self.kv_tick = 0
            return
        mmt = None
        for t in range(4):
            for h in range(H):
                bank = "C0" if h < 8 else "C1"
                mmt = self.T(lambda e, t=t, h=h, bank=bank: e.matmul(
                    ps[bank][0:64, (h % 8) * 64: (h % 8 + 1) * 64],
                    sb["qp"][:, t * 1024 + h * 64: t * 1024 + (h + 1) * 64],
                    sb["kp"][:, t * 1024 + h * 64: t * 1024 + (h + 1) * 64],
                    start=(t == 0 and h % 8 == 0), stop=(t == 3 and h % 8 == 7),
                    skip_group_check=True),
                    waits=(kvready + self.bankfree["C0"] + self.bankfree["C1"])
                          if (t == 0 and h == 0) else (),
                    inc=(t == 3 and h == H - 1))
        self.free["qp"] = [("t", mmt)]
        self.free["kp"] = [("t", mmt)]
        kvv0 = self.V(lambda e: e.tensor_tensor(sb["kv"][:, 0:512], sb["kv"][:, 0:512],
                                                ps["C0"][0:64, :], op=ALU.add),
                      waits=[("t", mmt), ("v", self.seq_v)], inc=True)
        kvv = self.V(lambda e: e.tensor_tensor(sb["kv"][:, 512:1024], sb["kv"][:, 512:1024],
                                               ps["C1"][0:64, :], op=ALU.add), inc=True)
        self.bankfree["C0"] = [("v", kvv0)]
        self.bankfree["C1"] = [("v", kvv)]
        self.kv_tick = kvv

    def attn(self, s, c, l):
        sb, dr, ps = self.sb, self.dr, self.ps
        P = self.P
        ccol = c * CH
        x0 = self.V(lambda e: e.tensor_copy(sb["xs"][:, 0:4 * CH], sb["y"][:, 0:4 * CH]),
                    waits=self.free["xs"] + self.y_state, inc=True)
        for d in range(4, 8):
            self.V(lambda e, d=d: e.tensor_copy(sb["xs"][:, d * CH + 1:(d + 1) * CH],
                                                sb["y"][:, d * CH:d * CH + CH - 1]))
        self.V(lambda e, l=l: e.tensor_copy(
            sb["xs"][:].rearrange("p (d m) -> p d m", d=8)[:, 4:8, 0:1],
            sb["acar"][:].rearrange("p (l d) -> p l d", l=NL)[:, l, :, None]))
        acv = self.V(lambda e, l=l: e.tensor_copy(
            sb["acar"][:].rearrange("p (l d) -> p l d", l=NL)[:, l, :],
            sb["y"][:].rearrange("p (d m) -> p d m", d=8)[:, 4:8, CH - 1]), inc=True)
        xs_done = [("v", acv)]
        self.free["xs"] = []

        wq3 = dr["wqkv"].rearrange("l (t p) m -> l p t m", p=128)[l]
        for which, (dst, ssbank, tabs, scl) in enumerate(
                (("qp", "C0", ("cs8b", "ss8b"), "qss"), ("kp", "C1", ("cik b", "sikb"), "kss"))):
            tabs = (tabs[0].replace(" ", ""), tabs[1])
            rqs_t = self.V(lambda e, scl=scl, l=l: e.tensor_scalar_mul(
                sb["rqs"][:], sb["rmat"][:], sb[scl][:, l:l + 1]),
                waits=[("t", self.tick("t"))], inc=True)
            asq_rd = [[], []]
            for dt in range(8):
                col0 = (which * 8 + dt) * 128
                cb, ct = self.stream_w(wq3[:, :, col0:col0 + 128], 1024, key=("qkv", l, col0))
                wsu = self.wsem_used
                wb = self.wbuf_used
                wcs = self.cur_wslot
                bank = ("A0", "A1")[dt % 2]
                for d in range(8):
                    mmt = self.T(lambda e, bank=bank, d=d, cb=cb, wb=wb: e.matmul(
                        ps[bank][:], wb[:, cb + d * 128: cb + (d + 1) * 128],
                        sb["xs"][:, d * CH:(d + 1) * CH],
                        start=(d == 0), stop=(d == 7), skip_group_check=True),
                        waits=([(wsu, ct)] + xs_done + self.bankfree[bank] +
                               (self.free[dst] if dt == 0 else [])) if d == 0 else (),
                        inc=(d == 7))
                self.cur_consumed = wcs; self.wdone(cb, mmt)
                ev = self.V(lambda e, dst=dst, dt=dt, bank=bank: e.tensor_copy(
                    sb[dst][:, dt * CH:(dt + 1) * CH], ps[bank][:]),
                    waits=[("t", mmt)], inc=True)
                self.bankfree[bank] = [("v", ev)]
                sq = self.S(lambda e, dst=dst, dt=dt: e.activation(
                    sb["sqbf"][:, (dt % 2) * CH:(dt % 2 + 1) * CH], sb[dst][:, dt * CH:(dt + 1) * CH],
                    AF.Square), waits=[("v", ev)] + asq_rd[dt % 2], inc=True)
                cst = self.T(lambda e, dt=dt, ssbank=ssbank: e.matmul(
                    ps[ssbank][0:16, :], sb["l2ones"][:, dt * 16:(dt + 1) * 16],
                    sb["sqbf"][:, (dt % 2) * CH:(dt % 2 + 1) * CH],
                    start=(dt == 0), stop=(dt == 7), skip_group_check=True),
                    waits=[("s", sq)] + (self.bankfree[ssbank] if dt == 0 else []), inc=True)
                asq_rd[dt % 2] = [("t", cst)]
            mx = self.V(lambda e, ssbank=ssbank: e.tensor_single_scalar(
                sb["invt"][:, 0:512], ps[ssbank][0:16, :], 1e-24, op=ALU.max),
                waits=[("t", cst)], inc=True)
            self.bankfree[ssbank] = [("v", mx)]
            lnt_ = self.S(lambda e: e.activation(sb["invt"][:, 512:1024], sb["invt"][:, 0:512], AF.Ln),
                          waits=[("v", mx)], inc=True)
            ivt = self.S(lambda e: e.activation(sb["invtb"][:, :], sb["invt"][:, 512:1024],
                                                AF.Exp, scale=-0.5), waits=[("s", lnt_)], inc=True)
            rq_rd = [list(asq_rd[0]), list(asq_rd[1])]
            for dt in range(8):
                rqb = ("B0", "B1")[dt % 2]
                ivb = ("C0", "C1")[dt % 2]
                rq = self.T(lambda e, rqb=rqb, dst=dst, dt=dt: e.matmul(
                    ps[rqb][:], sb["rqs"][:], sb[dst][:, dt * CH:(dt + 1) * CH],
                    start=True, stop=True),
                    waits=([("v", rqs_t)] if dt == 0 else []) + self.bankfree[rqb], inc=True)
                iv = self.T(lambda e, ivb=ivb, dt=dt: e.matmul(
                    ps[ivb][:], sb["l2pat"][:, dt * 128:(dt + 1) * 128], sb["invtb"][:, :],
                    start=True, stop=True),
                    waits=[("s", ivt)] + self.bankfree[ivb], inc=True)
                rqc = self.S(lambda e, rqb=rqb, dt=dt: e.copy(
                    sb["sqbf"][:, (dt % 2) * CH:(dt % 2 + 1) * CH], ps[rqb][:]),
                    waits=[("t", rq)] + rq_rd[dt % 2], inc=True)
                self.bankfree[rqb] = [("s", rqc)]
                self.V(lambda e, dst=dst, dt=dt, tabs=tabs, ccol=ccol: e.tensor_tensor(
                    sb["rot"][:, 0:CH], sb[dst][:, dt * CH:(dt + 1) * CH],
                    sb[tabs[0]][:, ccol:ccol + CH], op=ALU.mult))
                t1s = self.V(lambda e, scl=scl, l=l: e.tensor_scalar_mul(
                    sb["rot"][:, 3 * CH:4 * CH], sb["rot"][:, 0:CH], sb[scl][:, l:l + 1]),
                    inc=True)
                t2r = self.V(lambda e, dt=dt, tabs=tabs, ccol=ccol: e.tensor_tensor(
                    sb["rot"][:, CH:2 * CH], sb["sqbf"][:, (dt % 2) * CH:(dt % 2 + 1) * CH],
                    sb[tabs[1]][:, ccol:ccol + CH], op=ALU.mult),
                    waits=[("s", rqc)], inc=True)
                rq_rd[dt % 2] = [("v", t2r)]
                t3 = self.V(lambda e: e.tensor_add(sb["rot"][:, 2 * CH:3 * CH],
                                                   sb["rot"][:, 3 * CH:4 * CH],
                                                   sb["rot"][:, CH:2 * CH]),
                            waits=[("v", t1s)], inc=True)
                qf = self.V(lambda e, dst=dst, dt=dt, ivb=ivb: e.tensor_tensor(
                    sb[dst][:, dt * CH:(dt + 1) * CH], sb["rot"][:, 2 * CH:3 * CH],
                    ps[ivb][:], op=ALU.mult), waits=[("t", iv), ("v", t3)], inc=True)
                self.bankfree[ivb] = [("v", qf)]
            if which == 0:
                qdone = [("v", self.tick("v"))]
            else:
                kdone = [("v", self.tick("v"))]
        for blk in range(8):
            cb, ct = self.stream_w(wq3[:, :, 2048 + blk * 128:2048 + (blk + 1) * 128], 1024, key=("qkv", l, 2048 + blk * 128))
            wsu = self.wsem_used
            wb = self.wbuf_used
            wcs = self.cur_wslot
            bank = ("A0", "A1")[blk % 2]
            for d in range(8):
                mmt = self.T(lambda e, bank=bank, d=d, cb=cb, wb=wb: e.matmul(
                    ps[bank][:], wb[:, cb + d * 128: cb + (d + 1) * 128],
                    sb["xs"][:, d * CH:(d + 1) * CH],
                    start=(d == 0), stop=(d == 7), skip_group_check=True),
                    waits=([(wsu, ct)] + self.bankfree[bank] +
                           (self.free["vfs"] + self.xt_free[0] + self.xt_free[1]
                            if blk == 0 else [])) if d == 0 else (),
                    inc=(d == 7))
            self.cur_consumed = wcs; self.wdone(cb, mmt)
            ev = self.V(lambda e, blk=blk, bank=bank: e.tensor_copy(
                sb["vfs"][:, blk * CH:(blk + 1) * CH], ps[bank][:]),
                waits=[("t", mmt)], inc=True)
            self.bankfree[bank] = [("v", ev)]
        vfs_done = [("v", self.tick("v"))]
        for st in range(4):
            ptb = ("PT0", "PT1")[st % 2]
            for blk in range(8):
                tt = self.T(lambda e, ptb=ptb, st=st, blk=blk: e.transpose(
                    ps[ptb][:, blk * 128:(blk + 1) * 128],
                    sb["vfs"][:, blk * CH + st * 128: blk * CH + (st + 1) * 128],
                    sb["identb"][:]),
                    waits=(vfs_done + self.bankfree[ptb] + (self.free["vaug"] if (st == 0 and blk == 0) else [])) if blk == 0 else (),
                    inc=(blk == 7))
            ev = self.V(lambda e, ptb=ptb, st=st: e.tensor_copy(
                sb["vaug"][:, st * 1040:(st + 1) * 1040].rearrange("p (h q) -> p h q", q=65)[:, :, 0:64],
                ps[ptb][:].rearrange("p (h q) -> p h q", q=64)),
                waits=[("t", tt)], inc=True)
            self.bankfree[ptb] = [("v", ev)]
        vaug_done = [("v", self.tick("v"))]
        self.xt_free[0] = [("t", self.tick("t"))]
        self.xt_free[1] = [("t", self.tick("t"))]
        self.free["vfs"] = [("t", self.tick("t"))]
        return qdone, kdone, vaug_done

    def attn_core(self, s, c, l, qdone, kdone, vaug_done):
        sb, dr, ps = self.sb, self.dr, self.ps
        first = (c == 0)
        nmx_i = [0]
        pbuf_i = [0]
        pbuf_free = [[] for _ in range(4)]
        nmax_free = [[] for _ in range(8)]
        ptsb_free = [[], []]
        for w in (0, 1):
            for p in range(8):
                simb = {0: ("A0", "A1"), 1: ("B0", "B1")}[p % 2]
                ptb = ("PT0", "PT1")[p % 2]
                ptev = [None, None]
                for par in (0, 1):
                    bank = simb[par]
                    r0 = par * 64
                    for i in (0, 1):
                        segs = []
                        if w == 0:
                            kc = sb["kcar"][:].rearrange("p (L q m) -> p L q m", L=NL, q=8)
                            if i == 0:
                                segs.append((kc[r0:r0 + 64, l, p, 0:256], 0, 256))
                                segs.append((sb["kp"][r0:r0 + 64, p * CH: p * CH + 128], 256, 128))
                            else:
                                segs.append((kc[r0:r0 + 64, l, p, 128:256], 0, 128))
                                segs.append((sb["kp"][r0:r0 + 64, p * CH: p * CH + 256], 128, 256))
                        else:
                            j0 = i * 128
                            segs.append((sb["kp"][r0:r0 + 64, p * CH + j0: p * CH + j0 + 384], 0, 384))
                        lq = sb["qp"][r0:r0 + 64, p * CH + w * 256 + i * 128: p * CH + w * 256 + (i + 1) * 128]
                        w0waits = qdone + kdone + list(self.bankfree[bank])
                        mmt = None
                        for si, (rhs, o, n) in enumerate(segs):
                            mmt = self.T(lambda e, bank=bank, lq=lq, rhs=rhs, o=o, n=n, si=si: e.matmul(
                                ps[bank][:, o:o + n], lq, rhs, start=(si == 0), stop=False,
                                skip_group_check=True),
                                waits=w0waits if si == 0 else (), inc=False)
                        mi = (2 if (first and w == 0) else 0) + i
                        mmt = self.T(lambda e, bank=bank, mi=mi: e.matmul(
                            ps[bank][:, 0:384], sb["identb"][:], sb["masks"][:, mi * 384:(mi + 1) * 384],
                            start=False, stop=True, skip_group_check=True), inc=True)
                        nc_ = nmx_i[0] % 8
                        nmx_i[0] += 1
                        rmx = self.V(lambda e, bank=bank, nc_=nc_: e.reduce_max(
                            sb["nmax"][:, nc_:nc_ + 1], ps[bank][:, 0:384], axis=AX, negate=True),
                            waits=[("t", mmt)] + nmax_free[nc_], inc=True)
                        pb = pbuf_i[0] % 4
                        pbuf_i[0] += 1
                        ex = self.S(lambda e, bank=bank, nc_=nc_, pb=pb: e.activation(
                            sb["pbuf"][:, pb * CH: pb * CH + 384], ps[bank][:, 0:384], AF.Exp,
                            bias=sb["nmax"][:, nc_:nc_ + 1],
                            accum_out=sb["stats"][:, nc_:nc_ + 1]),
                            waits=[("v", rmx)] + pbuf_free[pb], inc=True)
                        nmax_free[nc_] = [("s", ex)]
                        self.bankfree[bank] = [("s", ex)]
                        rcp = self.V(lambda e, nc_=nc_: e.reciprocal(
                            sb["stats"][:, 8 + nc_:9 + nc_], sb["stats"][:, nc_:nc_ + 1]),
                            waits=[("s", ex)], inc=True)
                        scl_t = self.V(lambda e, nc_=nc_, pb=pb: e.tensor_scalar_mul(
                            sb["pbuf"][:, pb * CH: pb * CH + 384],
                            sb["pbuf"][:, pb * CH: pb * CH + 384],
                            sb["stats"][:, 8 + nc_:9 + nc_]),
                            waits=[("v", rcp)], inc=True)
                        for jj in range(3):
                            jt = jj + i
                            tt = self.T(lambda e, ptb=ptb, pb=pb, jj=jj, jt=jt, i=i: e.transpose(
                                ps[ptb][:, jt * 256 + i * 128: jt * 256 + i * 128 + 128],
                                sb["pbuf"][:, pb * CH + jj * 128: pb * CH + (jj + 1) * 128],
                                sb["identb"][:]),
                                waits=([("v", scl_t)] + (self.bankfree[ptb] if (i == 0 and jj == 0) else [])),
                                inc=(jj == 2))
                        pbuf_free[pb] = [("t", tt)]
                    pev = self.V(lambda e, ptb=ptb, par=par: e.tensor_copy(
                        sb["ptsb"][:, par * 1024:(par + 1) * 1024], ps[ptb][:]),
                        waits=[("t", tt)] + ptsb_free[par], inc=True)
                    self.bankfree[ptb] = [("v", pev)]
                    ptev[par] = pev
                for par in (0, 1):
                    h = 2 * p + par
                    pvb = ("C0", "C1")[par]
                    vsrcs = []
                    for jt in range(4):
                        if w == 0 and jt < 2:
                            vc = sb["vcar"][:].rearrange("p (L q m) -> p L q m", L=NL, q=2)
                            vsrcs.append(vc[:, l, jt, h * 65:h * 65 + 64])
                        else:
                            st = (jt - 2) if w == 0 else jt
                            vsrcs.append(sb["vaug"][:, st * 1040 + h * 65: st * 1040 + h * 65 + 64])
                    mmt = None
                    order = (1, 2, 0, 3)
                    for oi, jt in enumerate(order):
                        ilo = 128 if jt == 3 else 0
                        ihi = 128 if jt == 0 else 256
                        mmt = self.T(lambda e, pvb=pvb, jt=jt, par=par, vs=vsrcs[jt], ilo=ilo, ihi=ihi,
                                     oi=oi: e.matmul(
                            ps[pvb][0:64, ilo:ihi], vs,
                            sb["ptsb"][:, par * 1024 + jt * 256 + ilo: par * 1024 + jt * 256 + ihi],
                            start=(oi == 0), stop=(oi == 3), skip_group_check=True),
                            waits=([("v", ptev[par])] + vaug_done + self.bankfree[pvb]) if oi == 0 else (),
                            inc=(oi == 3))
                    ptsb_free[par] = [("t", mmt)]
                    at = self.V(lambda e, pvb=pvb, r0=par * 64, p=p, w=w: e.tensor_copy(
                        sb["afm"][r0:r0 + 64, p * CH + w * 256: p * CH + (w + 1) * 256],
                        ps[pvb][0:64, 0:256]),
                        waits=[("t", mmt)] + (self.free["afm"] if (w == 0 and p == 0 and par == 0) else []),
                        inc=True)
                    self.bankfree[pvb] = [("v", at)]
            if w == 0:
                tnow = self.tick("t")
                kcv = self.V(lambda e, l=l: e.tensor_copy(
                    sb["kcar"][:].rearrange("p (L q m) -> p L q m", L=NL, q=8)[:, l],
                    sb["kp"][:].rearrange("p (q m) -> p q m", q=8)[:, :, 256:512]),
                    waits=[("t", tnow)], inc=True)
                vcv = self.V(lambda e, l=l: e.tensor_copy(
                    sb["vcar"][:].rearrange("p (L q m) -> p L q m", L=NL, q=2)[:, l],
                    sb["vaug"][:].rearrange("p (q m) -> p q m", q=4)[:, 2:4]),
                    waits=[("t", tnow)], inc=True)
        afm_done = [("v", self.tick("v"))]
        self.free["qp"] = [("t", self.tick("t"))]
        self.free["kp"] = [("t", self.tick("t"))]
        self.free["vaug"] = [("t", self.tick("t"))]
        self.free["xs"] = [("t", self.tick("t"))]
        wo3 = dr["wo_i"].rearrange("l (t p) m -> l p t m", p=128)[l]
        for dt in range(8):
            cb, ct = self.stream_w(wo3[:, :, dt * 128:(dt + 1) * 128], 1024, key=("woi", l, dt))
            wsu = self.wsem_used
            wb = self.wbuf_used
            wcs = self.cur_wslot
            bank = ("A0", "A1")[dt % 2]
            for d in range(8):
                mmt = self.T(lambda e, bank=bank, d=d, cb=cb, wb=wb: e.matmul(
                    ps[bank][:], wb[:, cb + d * 128: cb + (d + 1) * 128],
                    sb["afm"][:, d * CH:(d + 1) * CH],
                    start=(d == 0), stop=(d == 7), skip_group_check=True),
                    waits=([(wsu, ct)] + afm_done + self.bankfree[bank]) if d == 0 else (),
                    inc=(d == 7))
            self.cur_consumed = wcs; self.wdone(cb, mmt)
            yv = self.V(lambda e, dt=dt, bank=bank: e.scalar_tensor_tensor(
                sb["y"][:, dt * CH:(dt + 1) * CH], ps[bank][:], 1.0,
                sb["y"][:, dt * CH:(dt + 1) * CH], op0=ALU.mult, op1=ALU.add),
                waits=[("t", mmt)], inc=True)
            self.bankfree[bank] = [("v", yv)]
        self.y_state = [("v", self.tick("v"))]
        self.free["afm"] = [("t", self.tick("t"))]
        self.free["regB"] = [("t", self.tick("t"))]

    def ffn(self, s, c, l):
        sb, dr, ps = self.sb, self.dr, self.ps
        x0 = self.V(lambda e: e.tensor_copy(sb["xs"][:, 0:4 * CH], sb["y"][:, 0:4 * CH]),
                    waits=self.free["xs"] + self.y_state, inc=True)
        for d in range(4, 8):
            self.V(lambda e, d=d: e.tensor_copy(sb["xs"][:, d * CH + 1:(d + 1) * CH],
                                                sb["y"][:, d * CH:d * CH + CH - 1]))
        xcp = self.V(lambda e, l=l: e.tensor_copy(
            sb["xs"][:].rearrange("p (d m) -> p d m", d=8)[:, 4:8, 0:1],
            sb["fcar"][:].rearrange("p (l d) -> p l d", l=NL)[:, l, :, None]), inc=True)
        xs_done = [("v", xcp)]
        cst = sst = None
        sq_rd = [[], []]
        SQV = False
        for d in range(8):
            if SQV:
                sq = self.V(lambda e, d=d: e.tensor_mul(
                    sb["sqbf"][:, (d % 2) * CH:(d % 2 + 1) * CH], sb["xs"][:, d * CH:(d + 1) * CH],
                    sb["xs"][:, d * CH:(d + 1) * CH]),
                    waits=(list(xs_done) if d == 0 else []) + sq_rd[d % 2], inc=True)
                sqw = [("v", sq)]
            else:
                sq = self.S(lambda e, d=d: e.activation(
                    sb["sqbf"][:, (d % 2) * CH:(d % 2 + 1) * CH], sb["xs"][:, d * CH:(d + 1) * CH],
                    AF.Square), waits=(list(xs_done) if d == 0 else []) + sq_rd[d % 2], inc=True)
                sqw = [("s", sq)]
            cst = self.T(lambda e, d=d: e.matmul(
                ps["C0"][0:1, :], sb["onesb"][:, 0:1], sb["xs"][:, d * CH:(d + 1) * CH],
                start=(d == 0), stop=(d == 7), skip_group_check=True),
                waits=(xs_done + self.bankfree["C0"]) if d == 0 else (), inc=True)
            sst = self.T(lambda e, d=d: e.matmul(
                ps["C1"][0:1, :], sb["onesb"][:, 0:1],
                sb["sqbf"][:, (d % 2) * CH:(d % 2 + 1) * CH],
                start=(d == 0), stop=(d == 7), skip_group_check=True),
                waits=sqw + (self.bankfree["C1"] if d == 0 else []), inc=True)
            sq_rd[d % 2] = [("t", sst)]
        mu = self.V(lambda e: e.tensor_scalar_mul(sb["lnt"][0:1, 0:512], ps["C0"][0:1, :], 1.0 / 1024.0),
                    waits=[("t", cst)], inc=True)
        m2 = self.V(lambda e: e.tensor_scalar_mul(sb["lnt"][0:1, 512:1024], ps["C1"][0:1, :], 1.0 / 1024.0),
                    waits=[("t", sst)], inc=True)
        mu2 = self.V(lambda e: e.tensor_mul(sb["lnt"][0:1, 1024:1536], sb["lnt"][0:1, 0:512],
                                            sb["lnt"][0:1, 0:512]), waits=[("v", mu)], inc=True)
        var = self.V(lambda e: e.tensor_sub(sb["lnt"][0:1, 1024:1536], sb["lnt"][0:1, 512:1024],
                                            sb["lnt"][0:1, 1024:1536]),
                     waits=[("v", mu2), ("v", m2)], inc=True)
        lnv = self.S(lambda e: e.activation(sb["lnt"][0:1, 512:1024], sb["lnt"][0:1, 1024:1536], AF.Ln,
                                            bias=self.cap[1e-5][0:1, 0:1]), waits=[("v", var)], inc=True)
        rstd = self.S(lambda e: e.activation(sb["lnt"][0:1, 1024:1536], sb["lnt"][0:1, 512:1024],
                                             AF.Exp, scale=-0.5), waits=[("s", lnv)], inc=True)
        nmr = self.V(lambda e: e.scalar_tensor_tensor(
            sb["lnt"][0:1, 1536:2048], sb["lnt"][0:1, 0:512], -1.0, sb["lnt"][0:1, 1024:1536],
            op0=ALU.mult, op1=ALU.mult), waits=[("s", rstd)], inc=True)
        rb_t = self.T(lambda e: e.matmul(ps["C0"][:], sb["onesf"][0:1, :], sb["lnt"][0:1, 1024:1536],
                                         start=True, stop=True, skip_group_check=True),
                      waits=[("v", nmr)], inc=True)
        nb_t = self.T(lambda e: e.matmul(ps["C1"][:], sb["onesf"][0:1, :], sb["lnt"][0:1, 1536:2048],
                                         start=True, stop=True, skip_group_check=True),
                      waits=[("v", nmr)], inc=True)
        for d in range(8):
            if d < 4:
                t1 = self.V(lambda e, d=d: e.tensor_tensor(
                    sb["otm"][:, 0:CH], sb["y"][:, d * CH:(d + 1) * CH], ps["C0"][:], op=ALU.mult),
                    waits=([("t", rb_t), ("t", nb_t)] + self.free["otm"]) if d == 0 else (), inc=True)
            else:
                t1c = self.V(lambda e, d=d, l=l: e.tensor_tensor(
                    sb["otm"][:, 0:1],
                    sb["fcar"][:].rearrange("p (l d) -> p l d", l=NL)[:, l, d - 4, None], ps["C0"][:, 0:1],
                    op=ALU.mult),
                    waits=(([("t", rb_t), ("t", nb_t)] + self.free["otm"]) if d == 4 else []), inc=True)
                t1 = self.V(lambda e, d=d: e.tensor_tensor(
                    sb["otm"][:, 1:CH], sb["y"][:, d * CH:d * CH + CH - 1], ps["C0"][:, 1:CH],
                    op=ALU.mult), inc=True)
                t2w = [("v", t1), ("v", t1c)]
            if d < 4:
                t2w = [("v", t1)]
            t2 = self.V(lambda e: e.tensor_tensor(sb["otm"][:, CH:2 * CH], sb["otm"][:, 0:CH],
                                                  ps["C1"][:], op=ALU.add), waits=t2w, inc=True)
            xv = self.V(lambda e, d=d, l=l: e.tensor_scalar(
                sb["xnb"][:, d * CH:(d + 1) * CH], sb["otm"][:, CH:2 * CH],
                sb["lngs"][:, l * 8 + d:l * 8 + d + 1], sb["lnbs"][:, l * 8 + d:l * 8 + d + 1],
                op0=ALU.mult, op1=ALU.add),
                waits=[("v", t2)] + (self.free["xnb"] if d == 0 else []), inc=True)
        self.V(lambda e, l=l: e.tensor_copy(
            sb["fcar"][:].rearrange("p (l d) -> p l d", l=NL)[:, l, :],
            sb["y"][:].rearrange("p (d m) -> p d m", d=8)[:, 4:8, CH - 1]), inc=True)
        xn_done = [("v", self.tick("v"))]
        self.bankfree["C0"] = [("v", self.tick("v")), ("s", self.tick("s"))]
        self.bankfree["C1"] = [("v", self.tick("v")), ("s", self.tick("s"))]
        self.free["otm"] = [("s", self.tick("s"))]
        w13 = dr["w1"].rearrange("l (t p) m -> l p t m", p=128)[l]
        gb = sb["regB"].bitcast(BF16)
        gel_free = [list(self.free["gel"]), list(self.free["gel"])]
        for ht in range(HT):
            cba, cta = self.stream_w(w13[:, :, ht * 128:(ht + 1) * 128], 1024, key=("w1", l, ht))
            wsua = self.wsem_used
            wba = self.wbuf_used
            wcsa = self.cur_wslot
            banka = ("A0", "A1")[ht % 2]
            for d in range(8):
                mma = self.T(lambda e, bank=banka, d=d, cb=cba, wb=wba: e.matmul(
                    ps[bank][:], wb[:, cb + d * 128: cb + (d + 1) * 128],
                    sb["xnb"][:, d * CH:(d + 1) * CH],
                    start=(d == 0), stop=(d == 7), skip_group_check=True),
                    waits=([(wsua, cta)] + xn_done + self.bankfree[banka] +
                           (self.free["regB"] if ht == 0 else [])) if d == 0 else (),
                    inc=(d == 7))
            self.cur_consumed = wcsa; self.wdone(cba, mma)
            cbg, ctg = self.stream_w(w13[:, :, HID + ht * 128:HID + (ht + 1) * 128], 1024, key=("w1g", l, ht))
            wsug = self.wsem_used
            wbg = self.wbuf_used
            wcsg = self.cur_wslot
            bankg = ("B0", "B1")[ht % 2]
            for d in range(8):
                mmg = self.T(lambda e, bank=bankg, d=d, cb=cbg, wb=wbg: e.matmul(
                    ps[bank][:], wb[:, cb + d * 128: cb + (d + 1) * 128],
                    sb["xnb"][:, d * CH:(d + 1) * CH],
                    start=(d == 0), stop=(d == 7), skip_group_check=True),
                    waits=([(wsug, ctg)] + self.bankfree[bankg]) if d == 0 else (),
                    inc=(d == 7))
            self.cur_consumed = wcsg; self.wdone(cbg, mmg)
            gl = self.S(lambda e, bank=bankg, ht=ht, l=l: e.activation(
                sb["gel"][:, (ht % 2) * CH:(ht % 2 + 1) * CH], ps[bank][:], AF.Gelu,
                bias=sb["b1s"][:, l * 2 * HT + HT + ht:l * 2 * HT + HT + ht + 1]),
                waits=[("t", mmg)] + gel_free[ht % 2], inc=True)
            self.bankfree[bankg] = [("s", gl)]
            gv = self.V(lambda e, bank=banka, ht=ht, l=l: e.scalar_tensor_tensor(
                gb[:, ht * CH:(ht + 1) * CH], ps[bank][:],
                sb["b1s"][:, l * 2 * HT + ht:l * 2 * HT + ht + 1],
                sb["gel"][:, (ht % 2) * CH:(ht % 2 + 1) * CH], op0=ALU.add, op1=ALU.mult),
                waits=[("t", mma), ("s", gl)], inc=True)
            gel_free[ht % 2] = [("v", gv)]
            self.bankfree[banka] = [("v", gv)]
        gated_done = [("v", self.tick("v"))]
        w23 = dr["w2"].rearrange("l (k p) m -> l p k m", p=128)[l]
        for dt in range(8):
            mmt = None
            for half in range(2):
                cb, ct = self.stream_w(w23[:, :, dt * 128:(dt + 1) * 128][:, half * 11:(half + 1) * 11],
                                       11 * 128, key=("w2", l, dt, half))
                wsu = self.wsem_used
                wb = self.wbuf_used
                wcs = self.cur_wslot
                for k in range(11):
                    kk = half * 11 + k
                    mmt = self.T(lambda e, dt=dt, k=k, kk=kk, cb=cb, wb=wb: e.matmul(
                        ps[("A0", "A1")[dt % 2]][:], wb[:, cb + k * 128: cb + (k + 1) * 128],
                        gb[:, kk * CH:(kk + 1) * CH],
                        start=(kk == 0), stop=(kk == 21), skip_group_check=True),
                        waits=([(wsu, ct)] + gated_done +
                               (self.bankfree[("A0", "A1")[dt % 2]] if kk == 0 else [])) if k == 0 else (),
                        inc=(k == 10))
                self.cur_consumed = wcs; self.wdone(cb, mmt)
            yv = self.V(lambda e, dt=dt: e.scalar_tensor_tensor(
                sb["y"][:, dt * CH:(dt + 1) * CH], ps[("A0", "A1")[dt % 2]][:],
                sb["b2s"][:, l * 8 + dt:l * 8 + dt + 1],
                sb["y"][:, dt * CH:(dt + 1) * CH], op0=ALU.add, op1=ALU.add),
                waits=[("t", mmt)], inc=True)
            self.bankfree[("A0", "A1")[dt % 2]] = [("v", yv)]
        self.y_state = [("v", self.tick("v"))]
        self.free["xs"] = [("t", self.tick("t"))]
        self.free["xnb"] = [("t", self.tick("t"))]
        self.free["regB"] = [("t", self.tick("t"))]
        self.free["afm"] = [("t", self.tick("t"))]
        self.free["gel"] = [("t", self.tick("t"))]

    def chunk_end(self, s, c):

        sb, dr = self.sb, self.dr
        gt = self.G(lambda e, s=s, c=c: e.dma_start(
            out=dr["ysp"].rearrange("(k d p) m -> k p d m", d=8, p=128)[s * NCH + c],
            in_=sb["y"][:].rearrange("p (d m) -> p d m", d=8)),
            waits=self.y_state, sem="dspy")
        self.free["y"] = [("dspy", gt)]

    def output(self, s):
        OSTAGE = 9
        sb, dr, ps = self.sb, self.dr, self.ps
        k1 = None
        if False:
            pass
        else:
          for h in range(H):
            par, p = h % 2, h // 2
            k1 = self.S(lambda e, par=par, p=p, h=h: e.activation(
                sb["kvb2"][par * 64:(par + 1) * 64, p * 64:(p + 1) * 64],
                sb["kv"][:, h * 64:(h + 1) * 64], AF.Copy, scale=0.125),
                waits=[("v", self.kv_tick)] if h == 0 else (), inc=(h == H - 1))
          kvb_done = [("s", k1)]
        woc = dr["woT"].rearrange("(t p) m -> p t m", p=128)
        wqc = dr["wqT"].rearrange("(t p) m -> p t m", p=128)
        for c in range(self.cfg.get("nchunks", NCH)):
            if True:
                gx = self.G(lambda e, s=s, c=c: e.dma_start(
                    out=sb["xnb"][:].rearrange("p (d m) -> p d m", d=8),
                    in_=dr["xnsp"].rearrange("(k d p) m -> k p d m", d=8, p=128)[s * NCH + c]),
                    waits=self.free["xnb"] + [("dspx", self.tick("dspx"))], sem="dgx")
            else:
                gx = 0
            if True:
                gy = self.G(lambda e, s=s, c=c: e.dma_start(
                    out=sb["y"][:].rearrange("p (d m) -> p d m", d=8),
                    in_=dr["ysp"].rearrange("(k d p) m -> k p d m", d=8, p=128)[s * NCH + c]),
                    waits=self.free["y"] + [("dspy", self.tick("dspy"))], sem="dgy")
            else:
                gy = 0
            if OSTAGE < 2:
                continue
            for dt in range(8):
                cb, ct = self.stream_w(wqc[:, :, dt * 128:(dt + 1) * 128], 1024, key=("wq", dt))
                wsu = self.wsem_used
                wb = self.wbuf_used
                wcs = self.cur_wslot
                bank = ("A0", "A1")[dt % 2]
                for d in range(8):
                    mmt = self.T(lambda e, bank=bank, d=d, cb=cb, wb=wb: e.matmul(
                        ps[bank][:], wb[:, cb + d * 128: cb + (d + 1) * 128],
                        sb["xnb"][:, d * CH:(d + 1) * CH],
                        start=(d == 0), stop=(d == 7), skip_group_check=True),
                        waits=([(wsu, ct), ("dgx", gx)] + self.bankfree[bank] +
                               (self.free["qp"] if dt == 0 else [])) if d == 0 else (),
                        inc=(d == 7))
                self.cur_consumed = wcs; self.wdone(cb, mmt)
                ev = self.V(lambda e, dt=dt, bank=bank: e.tensor_copy(
                    sb["qp"][:, dt * CH:(dt + 1) * CH], ps[bank][:]),
                    waits=[("t", mmt)], inc=True)
                self.bankfree[bank] = [("v", ev)]
            qo_done = [("v", self.tick("v"))]
            if OSTAGE < 3:
                continue
            for p in range(8):
                for par in (0, 1):
                    bank = ("B0", "B1")[par]
                    r0 = par * 64
                    mmt = self.T(lambda e, bank=bank, p=p, r0=r0: e.matmul(
                        ps[bank][0:64, :], sb["kvb2"][r0:r0 + 64, p * 64:(p + 1) * 64],
                        sb["qp"][r0:r0 + 64, p * CH:(p + 1) * CH], start=True, stop=True,
                        skip_group_check=True),
                        waits=kvb_done + qo_done + self.bankfree[bank] +
                              (self.free["xs"] if (p == 0 and par == 0) else []),
                        inc=True)
                    ev = self.V(lambda e, bank=bank, p=p, r0=r0: e.tensor_copy(
                        sb["xs"][r0:r0 + 64, p * CH:(p + 1) * CH], ps[bank][0:64, :]),
                        waits=[("t", mmt)], inc=True)
                    self.bankfree[bank] = [("v", ev)]
            sp_done = [("v", self.tick("v"))]
            if OSTAGE < 4:
                continue
            for t in range(4):
                for bi, bank in enumerate(("A0", "A1")):
                    tt = None
                    for d in range(4 * bi, 4 * bi + 4):
                        tt = self.T(lambda e, bank=bank, d=d, t=t: e.transpose(
                            ps[bank][:, (d % 4) * 128:((d % 4) + 1) * 128],
                            sb["y"][:, d * CH + t * 128: d * CH + (t + 1) * 128], sb["identf"][:]),
                            waits=([("dgy", gy)] + self.bankfree[bank]) if d % 4 == 0 else (),
                            inc=(d % 4 == 3))
                    gv = self.S(lambda e, bank=bank, bi=bi, t=t: e.activation(
                        sb["gtm"][:, t * 1024 + bi * 512: t * 1024 + (bi + 1) * 512],
                        ps[bank][:], AF.Silu),
                        waits=[("t", tt)] + (self.free["gtm"] + self.free["regB"]
                                             if (t == 0 and bi == 0) else []), inc=True)
                    self.bankfree[bank] = [("s", gv)]
            g_done = [("s", self.tick("s"))]
            base = s * S + c * CH
            st_dg = []
            for t in range(4):
                dx = self.Sy(lambda e, t=t, base=base: e.dma_start(
                    out=sb["xt"][:, (t % 2) * 1024:(t % 2 + 1) * 1024],
                    in_=dr["xin"][base + t * 128: base + (t + 1) * 128, :]),
                    waits=self.xt_free[t % 2], sem=("dxA", "dxB")[t % 2])
                for half in range(2):
                    bank = ("B0", "B1")[half]
                    for blk in range(4 * half, 4 * half + 4):
                        cb, ct = self.stream_w(woc[:, :, blk * 128:(blk + 1) * 128], 1024, key=("wo", blk))
                        wsu = self.wsem_used
                        wb = self.wbuf_used
                        wcs = self.cur_wslot
                        for p in range(8):
                            mmt = self.T(lambda e, bank=bank, p=p, t=t, blk=blk, cb=cb, wb=wb: e.matmul(
                                ps[bank][:, (blk % 4) * 128:(blk % 4 + 1) * 128],
                                sb["xs"][:, p * CH + t * 128: p * CH + (t + 1) * 128],
                                wb[:, cb + p * 128: cb + (p + 1) * 128],
                                start=(p == 0), stop=(p == 7), skip_group_check=True),
                                waits=([(wsu, ct)] + sp_done +
                                       (self.bankfree[bank] if blk % 4 == 0 else [])) if p == 0 else (),
                                inc=(p == 7))
                        self.cur_consumed = wcs; self.wdone(cb, mmt)
                    c0 = half * 512
                    t1 = self.V(lambda e, bank=bank, t=t, c0=c0: e.tensor_tensor(
                        sb["otm"][:, c0:c0 + 512], ps[bank][:],
                        sb["gtm"][:, t * 1024 + c0: t * 1024 + c0 + 512], op=ALU.mult),
                        waits=[("t", mmt)] + g_done + (self.free["otm"] if half == 0 else []),
                        inc=True)
                    t2 = self.V(lambda e, t=t, c0=c0: e.tensor_tensor(
                        sb["otm"][:, c0:c0 + 512], sb["otm"][:, c0:c0 + 512],
                        sb["xt"][:, (t % 2) * 1024 + c0: (t % 2) * 1024 + c0 + 512], op=ALU.add),
                        waits=[(("dxA", "dxB")[t % 2], dx), ("v", t1)], inc=True)
                    self.bankfree[bank] = [("v", t2)]
                gs = self.G(lambda e, t=t, base=base: e.dma_start(
                    out=dr["yout"][base + t * 128: base + (t + 1) * 128, :], in_=sb["otm"][:, 0:1024]),
                    waits=[("v", self.tick("v"))], sem="dgo")
                st_dg.append(gs)
                self.xt_free[t % 2] = [("v", self.tick("v"))]
                self.free["otm"] = [("dgo", gs)]
            self.free["xs"] = [("t", self.tick("t"))]
            self.free["gtm"] = [("v", self.tick("v"))]
            self.free["regB"] = [("v", self.tick("v"))]
            self.free["xnb"] = [("t", self.tick("t"))]
            self.free["y"] = [("t", self.tick("t"))]
            self.free["qp"] = [("t", self.tick("t"))]
        self.out_done = [("dgo", self.tick("dgo"))]

    TAPS = {
        "y": ("y", F32), "xnb": ("xnb", BF16), "qp": ("qp", BF16), "kp": ("kp", BF16),
        "xs": ("xs", BF16), "vaug": ("vaug", BF16), "afm": ("afm", BF16),
        "kv": ("kv", F32), "ptsb": ("ptsb", BF16), "gtm": ("gtm", F32),
        "stats": ("stats", F32), "xt": ("xt", F32), "gated": ("afm", BF16),
        "small": ("small", F32), "otm": ("otm", F32), "kvb2": ("kvb2", BF16),
    }

    def emit_taps(self):
        names = self.cfg.get("taps", ())
        if not names:
            return
        sb = self.sb
        fw = []
        vq = self.V(lambda e: e.tensor_copy(sb["stats"][:, 0:1], sb["stats"][:, 1:2]), inc=True)
        sq = self.S(lambda e: e.copy(sb["stats"][:, 2:3], sb["stats"][:, 3:4]), inc=True)
        fw = [("v", vq), ("s", sq), ("t", self.tick("t"))]
        caps = {"afm": 8 * CH, "gtm": 4096, "gel": 1024, "gated": HT * CH}
        for n in names:
            src, dt = self.TAPS[n]
            hh = self.sb[src]
            shape = list(hh.shape)
            if n in caps:
                shape = [shape[0], caps[n]]
            tap = self.nc.dram_tensor("tap_" + n, shape, dt, kind="ExternalOutput").ap()
            self.G(lambda e, tap=tap, hh=hh, shape=shape: e.dma_start(
                out=tap[:, :], in_=hh[0:shape[0], 0:shape[1]]),
                   waits=fw, sem="dgo")

    def emit_phases(self):
        self.y_state = []
        self.kv_tick = 0
        nl = self.cfg.get("nlayers", NL)
        for s in range(self.cfg.get("nseq", NSEQ)):
            self.seq_init(s)
            for c in range(self.cfg.get("nchunks", NCH)):
                self.phase1(s, c)
                self.y_state = list(self.y_ready)
                for l in range(nl):
                    qd, kd, vd = self.attn(s, c, l)
                    self.attn_core(s, c, l, qd, kd, vd)
                    if not self.cfg.get("attn_only", False):
                        self.ffn(s, c, l)
                if self.cfg.get("output", True):
                    self.chunk_end(s, c)
            if self.cfg.get("output", True):
                self.output(s)

    def build(self):
        self.declare_dram()
        with self.es:
            self.alloc()
            self.init_stream()
            self.prologue()
            saved_q = {k: list(v) for k, v in self.P.q.items()}
            saved_semc = dict(self.P.semc)
            saved_pending = {k: list(v) for k, v in getattr(self.P, "pending", {e: [] for e in self.P.ENGINES}).items()}
            saved_free = {k: list(v) for k, v in self.free.items()}
            saved_bank = {k: list(v) for k, v in self.bankfree.items()}
            self.collect = True
            self.emit_phases()
            self.collect = False
            self.P.q = saved_q
            self.P.semc = saved_semc
            self.P.pending = saved_pending
            self.free = saved_free
            self.bankfree = saved_bank
            self.xt_free = [[], []]
            self.xnt_free = [[], []]
            self.wslot = 0
            self.wfree = [[] for _ in range(6)]
            self.wconv_waited = [False] * 6
            self.emit_weight_cache()
            self.emit_phases()
            self.emit_taps()
            final = self.tick("dgo")
            if final:
                self.P.op("sync", lambda e: None, waits=[("dgo", final)])
            block = self.es.enter_context(self.nc.Block())
            self.P.finalize(block)
        return self.nc


def build_core(cfg):
    core = Core(cfg)
    nc = core.build()
    return nc


_NCORES = 8


def kernel(**inputs):
    from concourse.bass_utils import run_bass_kernel_spmd
    x = np.ascontiguousarray(np.asarray(inputs["L0_attn_x"], dtype=np.float32))
    B = x.shape[0]
    consts = make_consts()
    w = make_weights({k: np.asarray(v) for k, v in inputs.items()})
    base = {}
    for k in ("wqT", "wkT", "wvT", "woT", "wqkv", "wo_i", "w1", "w2",
              "b1", "b2", "lng", "lnb", "qs", "ks"):
        base[k] = w[k]
    for k in ("cs8", "ss8", "cik", "sik"):
        base[k] = consts[k]
    base["masks"] = np.ascontiguousarray(
        np.concatenate([consts["mask_n0"], consts["mask_n1"], consts["mask_f0"], consts["mask_f1"]], 1))
    base["identb"] = consts["ident_b"]
    base["identf"] = consts["ident_f"]
    base["rmat"] = consts["rmat"]
    base["l2ones"] = consts["l2ones"]
    base["l2pat"] = consts["l2pat"]
    base["onesb"] = consts["ones"]
    cfg = {"nseq": NSEQ, "nchunks": NCH, "nlayers": NL, "output": True, "taps": ()}
    nc = build_core(cfg)
    in_maps = []
    per = B // _NCORES
    for ci in range(_NCORES):
        m = dict(base)
        m["xin"] = np.ascontiguousarray(x[ci * per:(ci + 1) * per].reshape(per * S, D))
        in_maps.append(m)
    res = run_bass_kernel_spmd(nc, in_maps, core_ids=list(range(_NCORES)))
    out = np.empty((B, S, D), dtype=np.float32)
    for ci in range(_NCORES):
        out[ci * per:(ci + 1) * per] = np.asarray(
            res.results[ci]["yout"], dtype=np.float32).reshape(per, S, D)
    return out


# version 06 (final)
# speedup vs baseline: 1.5417x; 1.1555x over previous
import numpy as np
import ml_dtypes
import contextlib
import concourse.bass as bass
import concourse.mybir as mybir

F32 = mybir.dt.float32
BF16 = mybir.dt.bfloat16
AF = mybir.ActivationFunctionType
ALU = mybir.AluOpType
AX = mybir.AxisListType.X

D = 1024
CH = 512
S = 2048
NCH = S // CH
NSEQ = 2
H = 16
DH = 64
W = 256
NL = 6
HID = 2816
HT = HID // 128
NEG = -1.0e30

bf = ml_dtypes.bfloat16


def make_consts():
    c = {}
    c["ident_b"] = np.eye(128, dtype=bf)
    c["ident_f"] = np.eye(128, dtype=np.float32)
    R = np.zeros((128, 128), dtype=np.float64)
    for h0 in (0, 64):
        for m in range(32):
            R[h0 + m + 32, h0 + m] = -1.0
            R[h0 + m, h0 + m + 32] = 1.0
    c["rmat"] = R.astype(bf)
    lo = np.zeros((128, 8, 16), dtype=np.float64)
    for f in range(8):
        lo[0:64, f, 2 * f] = 1.0
        lo[64:128, f, 2 * f + 1] = 1.0
    c["l2ones"] = lo.reshape(128, 128).astype(bf)
    pat = np.zeros((16, 8, 128), dtype=np.float64)
    for f in range(8):
        pat[2 * f, f, 0:64] = 1.0
        pat[2 * f + 1, f, 64:128] = 1.0
    c["l2pat"] = pat.reshape(16, 1024).astype(bf)
    c["ones"] = np.ones((128, 128), dtype=bf)
    half = np.arange(0, DH, 2, dtype=np.float64)
    inv_freq = 1.0 / 10000.0 ** (half / DH)
    t = np.arange(S, dtype=np.float64)
    freqs = t[:, None] * inv_freq[None, :]
    freqs = np.concatenate([freqs, freqs], -1)
    base_scale = (half + 0.4 * DH) / (1.4 * DH)
    power = (t - S // 2) / (W // 2)
    scale = base_scale[None, :] ** power[:, None]
    scale = np.concatenate([scale, scale], -1)
    cos, sin = np.cos(freqs), np.sin(freqs)
    qk_scale = 8.0
    cs8 = (cos * scale * qk_scale).T
    ss8 = (sin * scale * qk_scale).T
    cik = (cos / scale).T
    sik = (sin / scale).T
    for name, tab in (("cs8", cs8), ("ss8", ss8), ("cik", cik), ("sik", sik)):
        c[name] = np.concatenate([tab, tab], 0).astype(np.float32)
    def mask_tile(i_base, j_base, first):
        m = np.zeros((128, 384), dtype=np.float64)
        for ii in range(128):
            i = i_base + ii
            for jj in range(384):
                j = j_base + jj
                valid = (i <= j <= i + W) and not (first and j < 256)
                if not valid:
                    m[ii, jj] = NEG
        return m.astype(bf)
    c["mask_n0"] = mask_tile(0, 0, False)
    c["mask_n1"] = mask_tile(128, 128, False)
    c["mask_f0"] = mask_tile(0, 0, True)
    c["mask_f1"] = mask_tile(128, 128, True)
    return c


def make_weights(iw):
    w = {}
    w["wqT"] = np.ascontiguousarray(iw["L0_attn_W_q"].T)
    w["wkT"] = np.ascontiguousarray(iw["L0_attn_W_k"].T)
    w["wvT"] = np.ascontiguousarray(iw["L0_attn_W_v"].T)
    w["woT"] = np.ascontiguousarray(iw["L0_attn_W_o"].T)
    w["wqkv"] = np.ascontiguousarray(iw["L0_ffn_Wqkv"])
    w["wo_i"] = np.ascontiguousarray(iw["L0_ffn_Wo"])
    W1 = iw["L0_ffn_W1"]
    w1p = np.zeros((NL, D, 2 * HID), dtype=np.float32)
    w1p[:, :, 0:2730] = W1[:, :, 0:2730]
    w1p[:, :, HID:HID + 2730] = W1[:, :, 2730:5460]
    w["w1"] = w1p
    b1 = iw["L0_ffn_b1"]
    b1p = np.zeros((NL, 2 * HID), dtype=np.float32)
    b1p[:, 0:2730] = b1[:, 0:2730]
    b1p[:, HID:HID + 2730] = b1[:, 2730:5460]
    w["b1"] = np.ascontiguousarray(b1p.reshape(NL, 2 * HT, 128).transpose(2, 0, 1).reshape(128, NL * 2 * HT))
    W2 = iw["L0_ffn_W2"]
    w2p = np.zeros((NL, HID, D), dtype=np.float32)
    w2p[:, 0:2730, :] = W2
    w["w2"] = w2p
    def pp(v):
        return np.ascontiguousarray(v.reshape(NL, 8, 128).transpose(2, 0, 1).reshape(128, NL * 8))
    w["b2"] = pp(iw["L0_ffn_b2"])
    w["lng"] = pp(iw["L0_ffn_ln_g"])
    w["lnb"] = pp(iw["L0_ffn_ln_b"])
    w["qs"] = np.ascontiguousarray(np.tile(iw["L0_ffn_q_scale"], (1, 2)).T)
    w["ks"] = np.ascontiguousarray(np.tile(iw["L0_ffn_k_scale"], (1, 2)).T)
    return {k: np.asarray(v, dtype=(bf if v.dtype == bf else np.float32)) for k, v in w.items()}


class Prog:
    ENGINES = ("sync", "tensor", "vector", "scalar", "gpsimd")

    def __init__(self, nc, es):
        self.nc = nc
        self.es = es
        self.q = {e: [] for e in self.ENGINES}
        self.semc = {}
        self.sems = {}

    def sem(self, name):
        h = self.es.enter_context(self.nc.semaphore(name))
        self.semc[name] = 0
        self.sems[name] = h
        return name

    def tick(self, name):
        return self.semc[name]

    DEFER = ("vector", "scalar")

    def set_spacer(self, eng, fn):
        self.spacer_fn = getattr(self, "spacer_fn", {})
        self.spacer_fn[eng] = fn

    def _emit_spacer(self, eng, carry):
        fn = self.spacer_fn[eng]

        def cb(e, fn=fn, carry=carry):
            ins = fn(e)
            for h, amt in carry:
                ins = ins.then_inc(h, amt)
        self.q[eng].append(cb)

    def op(self, eng, fn, waits=(), inc=None, amt=1):
        for s, v in waits:
            assert v <= self.semc[s], f"wait-ahead-of-inc on {s}: {v} > {self.semc[s]}"
        self.pending = getattr(self, "pending", {e: [] for e in self.ENGINES})
        waits_h = [(self.sems[s], v) for s, v in waits if v > 0]
        defer = eng in self.DEFER
        if defer and self.pending[eng] and waits_h:
            self._emit_spacer(eng, self.pending[eng])
            self.pending[eng] = []
        carry = list(self.pending[eng]) if defer else []
        if defer:
            self.pending[eng] = []
        cap_inc = None
        if inc is not None:
            self.semc[inc] += amt
            if not defer:
                cap_inc = (self.sems[inc], amt)

        def cb(e, fn=fn, waits_h=waits_h, cap_inc=cap_inc, carry=carry):
            for h, v in waits_h:
                e.wait_ge(h, v)
            ins = fn(e)
            for h, amt2 in carry:
                assert ins is not None, "carry needs a real instruction"
                ins = ins.then_inc(h, amt2)
            if cap_inc is not None:
                assert ins is not None, "inc requested but fn returned None"
                ins.then_inc(*cap_inc)
        self.q[eng].append(cb)
        if inc is not None and defer:
            self.pending[eng].append((self.sems[inc], amt))
        return self.semc[inc] if inc is not None else None

    def finalize(self, block):
        for eng in self.DEFER:
            if getattr(self, "pending", {}).get(eng):
                self._emit_spacer(eng, self.pending[eng])
                self.pending[eng] = []
        for name in self.ENGINES:
            cbs = self.q[name]
            if not cbs:
                continue
            def section(e, cbs=cbs):
                for cb in cbs:
                    cb(e)
            getattr(block, name)(section)


class Core:

    def __init__(self, cfg):
        self.cfg = cfg
        self.nc = bass.Bass("TRN2", target_bir_lowering=False)
        self.es = contextlib.ExitStack()
        self.taps = {}

    def declare_dram(self):
        nc = self.nc
        dr = {}
        dr["xin"] = nc.dram_tensor("xin", [NSEQ * S, D], F32, kind="ExternalInput").ap()
        dr["yout"] = nc.dram_tensor("yout", [NSEQ * S, D], F32, kind="ExternalOutput").ap()
        for n, shp in (("wqT", [D, D]), ("wkT", [D, D]), ("wvT", [D, D]), ("woT", [D, D])):
            dr[n] = nc.dram_tensor(n, shp, F32, kind="ExternalInput").ap()
        dr["wqkv"] = nc.dram_tensor("wqkv", [NL, D, 3072], F32, kind="ExternalInput").ap()
        dr["wo_i"] = nc.dram_tensor("wo_i", [NL, D, D], F32, kind="ExternalInput").ap()
        dr["w1"] = nc.dram_tensor("w1", [NL, D, 2 * HID], F32, kind="ExternalInput").ap()
        dr["w2"] = nc.dram_tensor("w2", [NL, HID, D], F32, kind="ExternalInput").ap()
        for n, cc in (("b1", NL * 2 * HT), ("b2", NL * 8), ("lng", NL * 8), ("lnb", NL * 8),
                      ("qs", NL), ("ks", NL)):
            dr[n] = nc.dram_tensor(n, [128, cc], F32, kind="ExternalInput").ap()
        for n in ("cs8", "ss8", "cik", "sik"):
            dr[n] = nc.dram_tensor(n, [128, S], F32, kind="ExternalInput").ap()
        dr["masks"] = nc.dram_tensor("masks", [128, 4 * 384], BF16, kind="ExternalInput").ap()
        dr["identb"] = nc.dram_tensor("identb", [128, 128], BF16, kind="ExternalInput").ap()
        dr["identf"] = nc.dram_tensor("identf", [128, 128], F32, kind="ExternalInput").ap()
        dr["rmat"] = nc.dram_tensor("rmat", [128, 128], BF16, kind="ExternalInput").ap()
        dr["l2ones"] = nc.dram_tensor("l2ones", [128, 128], BF16, kind="ExternalInput").ap()
        dr["l2pat"] = nc.dram_tensor("l2pat", [16, 1024], BF16, kind="ExternalInput").ap()
        dr["onesb"] = nc.dram_tensor("onesb", [128, 128], BF16, kind="ExternalInput").ap()
        dr["wbfd"] = nc.dram_tensor("wbfd", [128, self.cfg.get("wcols", 700000)], BF16, kind="Internal").ap()
        dr["xnsp"] = nc.dram_tensor("xnsp", [NSEQ * NCH * 8 * 128, CH], BF16, kind="Internal").ap()
        dr["ysp"] = nc.dram_tensor("ysp", [NSEQ * NCH * 8 * 128, CH], F32, kind="Internal").ap()
        self.dr = dr

    def alloc(self):
        nc, es = self.nc, self.es
        sb = {}

        def t(name, shape, dt):
            sb[name] = es.enter_context(nc.sbuf_tensor("s_" + name, shape, dt))
            return sb[name]

        t("y", [128, 8 * CH], F32)
        t("kcar", [128, NL * 8 * 256], BF16)
        t("vcar", [128, NL * 2 * 1040], BF16)
        t("acar", [128, NL * 4], F32)
        t("fcar", [128, NL * 4], F32)
        for n in ("cs8", "ss8", "cik", "sik"):
            t(n + "b", [128, S], BF16)
        t("masks", [128, 4 * 384], BF16)
        t("identb", [128, 128], BF16)
        t("identf", [128, 128], F32)
        t("rmat", [128, 128], BF16)
        t("l2ones", [128, 128], BF16)
        t("l2pat", [16, 1024], BF16)
        t("onesb", [128, 128], BF16)
        t("consts", [128, 8], F32)
        t("b1s", [128, NL * 2 * HT], F32)
        t("b2s", [128, NL * 8], F32)
        t("lngs", [128, NL * 8], F32)
        t("lnbs", [128, NL * 8], F32)
        t("qss", [128, NL], F32)
        t("kss", [128, NL], F32)
        t("kv", [64, 1024], F32)
        t("kvb2", [128, 512], BF16)
        t("xt", [128, 2 * 1024], F32)
        t("regB", [128, 22 * CH // 2], F32)
        t("xs", [128, 8 * CH], BF16)
        t("xnb", [128, 8 * CH], BF16)
        t("qp", [128, 8 * CH], BF16)
        t("kp", [128, 8 * CH], BF16)
        t("sqbf", [128, 2 * CH], BF16)
        t("rot", [128, 4 * CH], BF16)
        t("vaug", [128, 4 * 1040], BF16)
        t("pbuf", [128, 8 * CH], BF16)
        t("ptsb", [128, 2 * 1024], BF16)
        t("wsg", [128, 2 * 1408], F32)
        t("wbf", [128, 2 * 1408], BF16)
        t("stats", [128, 32], F32)
        t("nmax", [128, 8], F32)
        t("small", [64, 2048], F32)
        sb["invt"] = sb["small"][0:16, 0:1024]
        sb["lnt"] = sb["small"][0:1, 0:2048]
        sb["rs"] = sb["small"][0:1, 0:512]
        sb["rb"] = sb["small"][0:64, 512:1024]
        sb["invtb"] = sb["small"].bitcast(BF16)[0:16, 2048:2560]
        sb["otm"] = sb["xs"].bitcast(F32)[0:128, 0:1024]
        sb["gtm"] = sb["regB"]
        sb["gel"] = sb["pbuf"][:, 0:1024]
        sb["afm"] = sb["regB"].bitcast(BF16)
        sb["vfs"] = sb["xt"].bitcast(BF16)
        self.sb = sb

        ps = {}
        for n in ("A0", "A1", "B0", "B1", "C0", "C1"):
            ps[n] = es.enter_context(nc.psum_tensor(n, [128, 512], F32))
        ps["PT0"] = es.enter_context(nc.psum_tensor("PT0", [128, 1024], BF16))
        ps["PT1"] = es.enter_context(nc.psum_tensor("PT1", [128, 1024], BF16))
        self.ps = ps

        t("spc", [1, 8], F32)
        t("onesf", [1, 128], F32)
        t("rqs", [128, 128], BF16)
        self.P = Prog(nc, es)
        self.P.set_spacer("vector", lambda e: e.tensor_copy(sb["spc"][0:1, 0:1], sb["spc"][0:1, 1:2]))
        self.P.set_spacer("scalar", lambda e: e.copy(sb["spc"][0:1, 2:3], sb["spc"][0:1, 3:4]))
        for s in ("dwA", "dwB", "dwC", "dwD", "dwE", "dwF", "dxA", "dxB", "dco",
                  "dspy", "dspx", "dgx", "dgy", "dgo", "dgw", "t", "v", "s"):
            self.P.sem(s)
        self.bankfree = {n: [] for n in ps}
        self.free = {n: [] for n in ("xnb", "y", "qp", "kp", "vaug", "afm", "xs", "regB",
                                     "qbf", "sqbf", "rot", "pbuf", "ptsb", "gel", "vfs",
                                     "xt", "gtm", "otm")}

    def T(self, fn, waits=(), inc=False):
        return self.P.op("tensor", fn, waits, "t" if inc else None)

    def V(self, fn, waits=(), inc=False):
        return self.P.op("vector", fn, waits, "v" if inc else None)

    def S(self, fn, waits=(), inc=False):
        return self.P.op("scalar", fn, waits, "s" if inc else None)

    def Sy(self, fn, waits=(), sem="dwA"):
        return self.P.op("sync", fn, waits, sem, amt=16)

    def G(self, fn, waits=(), sem="dgo"):
        return self.P.op("gpsimd", fn, waits, sem, amt=16)

    def tick(self, s):
        return self.P.tick(s)

    def Vsp(self):
        self.P.op("vector", lambda e: e.tensor_copy(
            self.sb["spc"][0:1, 4:5], self.sb["spc"][0:1, 5:6]))

    def Ssp(self):
        self.P.op("scalar", lambda e: e.copy(
            self.sb["spc"][0:1, 6:7], self.sb["spc"][0:1, 7:8]))

    def init_stream(self):
        self.wslot = 0
        self.wfree = [[] for _ in range(6)]
        self.collect = False
        self.wreg = {}
        self.wreg_order = []
        self.wreg_cols = 0
        self.wconv_waited = [False] * 6

    def stream_w(self, dram_ap3, ncols, key=None):
        assert key is not None
        if self.collect:
            if key not in self.wreg:
                self.wreg[key] = (self.wreg_cols, ncols)
                self.wreg_order.append((key, dram_ap3, ncols))
                self.wreg_cols += ncols
            self.wsem_used = "dwA"
            self.wbuf_used = self.sb["wbf"]
            self.cur_wslot = 0
            return 0, 0
        off, nc2 = self.wreg[key][0], self.wreg[key][1]
        assert nc2 == ncols
        sl = self.wslot
        self.wslot = (self.wslot + 1) % 6
        sb = self.sb
        wsem = ("dwA", "dwB", "dwC", "dwD", "dwE", "dwF")[sl]
        extra = [] if self.wconv_waited[sl] else [("dgw", self.tick("dgw"))]
        self.wconv_waited[sl] = True
        buf = sb["wbf"] if sl < 2 else sb["wsg"].bitcast(BF16)
        base = (sl if sl < 2 else sl - 2) * 1408
        dw = self.Sy(lambda e, off=off, buf=buf, base=base, ncols=ncols: e.dma_start(
            out=buf[:, base: base + ncols],
            in_=self.dr["wbfd"][:, off:off + ncols]),
            waits=self.wfree[sl] + extra, sem=wsem)
        self.cur_wslot = sl
        self.wsem_used = wsem
        self.wbuf_used = buf
        return base, dw

    def wdone(self, sl_base, t_tick):
        self.wfree[self.cur_consumed] = [("t", t_tick)]

    def emit_weight_cache(self):
        sb = self.sb
        wsg_free = [[], []]
        wbf_free = [[], []]
        for i, (key, ap3, ncols) in enumerate(self.wreg_order):
            sl = i % 2
            kt = ncols // 128
            wsem = ("dxA", "dxB")[sl]
            dw = self.Sy(lambda e, a=ap3, sl=sl, kt=kt: e.dma_start(
                out=sb["wsg"][:, sl * 1408: sl * 1408 + kt * 128].rearrange("p (k m) -> p k m", k=kt),
                in_=a), waits=wsg_free[sl], sem=wsem)
            ct = self.S(lambda e, sl=sl, n=ncols: e.copy(
                sb["wbf"][:, sl * 1408: sl * 1408 + n], sb["wsg"][:, sl * 1408: sl * 1408 + n]),
                waits=[(wsem, dw)] + wbf_free[sl], inc=True)
            wsg_free[sl] = [("s", ct)]
            off = self.wreg[key][0]
            gw = self.G(lambda e, sl=sl, off=off, n=ncols: e.dma_start(
                out=self.dr["wbfd"][:, off:off + n],
                in_=sb["wbf"][:, sl * 1408: sl * 1408 + n]),
                waits=[("s", ct)], sem="dgw")
            wbf_free[sl] = [("dgw", gw)]
        self.wfree = [[] for _ in range(6)]

    def prologue(self):
        sb, dr = self.sb, self.dr
        self.V(lambda e: e.memset(sb["onesf"][0:1, :], 1.0))
        self.V(lambda e: e.memset(sb["spc"][0:1, 0:2], 0.0))
        self.V(lambda e: e.memset(sb["spc"][0:1, 4:6], 0.0))
        self.S(lambda e: e.memzero(sb["spc"][0:1, 2:4]))
        self.S(lambda e: e.memzero(sb["spc"][0:1, 6:8]))
        for n, d in (("masks", "masks"), ("identb", "identb"), ("identf", "identf"),
                     ("rmat", "rmat"), ("l2ones", "l2ones"), ("l2pat", "l2pat"),
                     ("onesb", "onesb"), ("b1s", "b1"), ("b2s", "b2"), ("lngs", "lng"),
                     ("lnbs", "lnb"), ("qss", "qs"), ("kss", "ks")):
            self.Sy(lambda e, n=n, d=d: e.dma_start(out=sb[n][:], in_=dr[d][:, :]), sem="dco")
        for rnd, (na, nb) in enumerate((("cs8", "ss8"), ("cik", "sik"))):
            wv = [("v", self.tick("v"))] if rnd else []
            ta = self.Sy(lambda e, na=na, r=rnd: e.dma_start(out=sb["gtm"][:, 0:2048], in_=dr[na][:, :]),
                         waits=wv, sem="dxA")
            tb = self.Sy(lambda e, nb=nb: e.dma_start(out=sb["gtm"][:, 2048:4096], in_=dr[nb][:, :]),
                         waits=wv, sem="dxB")
            self.V(lambda e, na=na: e.tensor_copy(sb[na + "b"][:], sb["gtm"][:, 0:2048]),
                   waits=[("dxA", ta)], inc=True)
            self.V(lambda e, nb=nb: e.tensor_copy(sb[nb + "b"][:], sb["gtm"][:, 2048:4096]),
                   waits=[("dxB", tb)], inc=True)
        cvals = [(-1.0, 0), (1e-5, 1), (1.0 / 1024.0, 2), (1e-24, 3), (0.125, 4), (1.0, 5)]
        for v, i in cvals:
            self.V(lambda e, v=v, i=i: e.memset(sb["consts"][:, i:i + 1], v))
            self.nc.const_aps.register(F32, v, sb["consts"][:, i:i + 1])
        self.cap = {v: sb["consts"][:, i:i + 1] for v, i in cvals}
        self.V(lambda e: e.memset(sb["small"][:], 0.0))
        self.V(lambda e: e.memset(sb["kcar"][:], 0.0))
        self.V(lambda e: e.memset(sb["vcar"][:], 0.0))
        self.V(lambda e: e.memset(
            sb["vaug"][:].rearrange("p (t h c) -> p t h c", t=4, c=65)[:, :, :, 64:65], 1.0), inc=True)
        self.prol_v = self.tick("v")
        self.free["regB"] = [("v", self.prol_v)]
        pw = [("dco", self.tick("dco"))]
        for eng in ("tensor", "vector", "scalar", "gpsimd"):
            self.P.op(eng, lambda e: None, waits=pw)

    def seq_init(self, s):
        sb = self.sb
        self.V(lambda e: e.memset(sb["acar"][:], 0.0))
        self.V(lambda e: e.memset(sb["fcar"][:], 0.0))
        self.V(lambda e: e.memset(sb["kv"][:], 0.0), inc=True)
        self.seq_v = self.tick("v")

    def phase1(self, s, c):
        sb, dr, ps = self.sb, self.dr, self.ps
        parts = 9
        ntiles = 4
        base = s * S + c * CH
        st = sb["stats"]
        xn_done_s = None
        self.xt_free = getattr(self, "xt_free", [[], []])
        self.xnt_free = getattr(self, "xnt_free", [[], []])
        for t in range(ntiles):
            sl = t % 2
            xsem = ("dxA", "dxB")[sl]
            dx = self.Sy(lambda e, t=t, sl=sl: e.dma_start(
                out=sb["xt"][:, sl * 1024:(sl + 1) * 1024],
                in_=dr["xin"][base + t * 128: base + (t + 1) * 128, :]),
                waits=self.xt_free[sl] + ([("v", self.prol_v)] if (s == 0 and c == 0 and t == 0) else []),
                sem=xsem)
            xsl = sb["xt"][:, sl * 1024:(sl + 1) * 1024]
            NOSTATS = False
            if NOSTATS:
                ss_t = self.S(lambda e, xsl=xsl, t=t: e.copy(
                    sb["regB"][:, 2048:2176], xsl[:, 0:128]),
                    waits=[(xsem, dx)] + (self.free["regB"] if t == 0 else []), inc=True)
            else:
                ss_t = self.S(lambda e, xsl=xsl, t=t: e.activation(
                    sb["regB"][:, 2048:3072], xsl, AF.Square, accum_out=st[:, t:t + 1]),
                    waits=[(xsem, dx)] + (self.free["regB"] if t == 0 else []), inc=True)
            sm_t = self.V(lambda e, xsl=xsl, t=t: e.reduce_sum(st[:, 4 + t:5 + t], xsl, axis=AX),
                          waits=[(xsem, dx)], inc=True)
            mu_t = self.V(lambda e, t=t: e.tensor_scalar_mul(st[:, 8 + t:9 + t], st[:, 4 + t:5 + t], 1.0 / 1024.0),
                          waits=[("v", sm_t)], inc=True)
            m2_t = self.V(lambda e, t=t: e.tensor_scalar_mul(st[:, 12 + t:13 + t], st[:, t:t + 1], 1.0 / 1024.0),
                          waits=[("s", ss_t)], inc=True)
            mu2_t = self.V(lambda e, t=t: e.tensor_mul(st[:, 16 + t:17 + t], st[:, 8 + t:9 + t], st[:, 8 + t:9 + t]),
                           waits=[("v", mu_t)], inc=True)
            var_t = self.V(lambda e, t=t: e.tensor_sub(st[:, 20 + t:21 + t], st[:, 12 + t:13 + t],
                                                       st[:, 16 + t:17 + t]),
                           waits=[("v", mu2_t), ("v", m2_t)], inc=True)
            if NOSTATS:
                ln_t = self.S(lambda e, t=t: e.copy(st[:, 12 + t:13 + t], st[:, 20 + t:21 + t]),
                              waits=[("v", var_t)], inc=True)
                rstd_t = self.S(lambda e, t=t: e.copy(st[:, 24 + t:25 + t], st[:, 12 + t:13 + t]),
                                waits=[("s", ln_t)], inc=True)
            else:
                ln_t = self.S(lambda e, t=t: e.activation(st[:, 12 + t:13 + t], st[:, 20 + t:21 + t], AF.Ln,
                                                          bias=self.cap[1e-5], scale=1.0),
                              waits=[("v", var_t)], inc=True)
                rstd_t = self.S(lambda e, t=t: e.activation(st[:, 24 + t:25 + t], st[:, 12 + t:13 + t],
                                                            AF.Exp, scale=-0.5),
                                waits=[("s", ln_t)], inc=True)
            nmr_t = self.V(lambda e, t=t: e.scalar_tensor_tensor(
                st[:, 28 + t:29 + t], st[:, 8 + t:9 + t], -1.0, st[:, 24 + t:25 + t],
                op0=ALU.mult, op1=ALU.mult), waits=[("s", rstd_t)], inc=True)
            xsl2 = sb["regB"][:, (sl * 1024):(sl + 1) * 1024]
            xn_t = self.V(lambda e, xsl=xsl, xsl2=xsl2, t=t: e.tensor_scalar(
                xsl2, xsl, st[:, 24 + t:25 + t], st[:, 28 + t:29 + t],
                op0=ALU.mult, op1=ALU.add),
                waits=[("v", nmr_t)] + self.xnt_free[sl], inc=True)
            self.xt_free[sl] = [("v", xn_t)]
            EV = "vs"
            if parts < 1:
                self.xnt_free[sl] = []
                last_evs = []
                continue
            evs = []
            for bi, bank in enumerate(("A0", "A1")):
                tt = None
                for d in range(4 * bi, 4 * bi + 4):
                    tt = self.T(lambda e, bank=bank, d=d, xsl2=xsl2: e.transpose(
                        ps[bank][:, (d % 4) * 128:((d % 4) + 1) * 128],
                        xsl2[:, d * 128:(d + 1) * 128], sb["identf"][:]),
                        waits=([("v", xn_t)] + self.bankfree[bank]) if d % 4 == 0 else (),
                        inc=(d % 4 == 3))
                tt = self.tick("t")
                import os as _os
                if EV == "none":
                    yv = self.V(lambda e: e.tensor_copy(sb["stats"][:, 5:6], sb["stats"][:, 6:7]),
                                waits=[("t", tt)], inc=True)
                    xv = self.S(lambda e: e.copy(sb["stats"][:, 6:7], sb["stats"][:, 7:8]),
                                waits=[("t", tt)], inc=True)
                elif EV == "v":
                    yv = self.V(lambda e, bank=bank, bi=bi, t=t: e.tensor_copy(
                        sb["y"][:].rearrange("p (d m) -> p d m", d=8)[:, 4 * bi:4 * bi + 4, t * 128:(t + 1) * 128],
                        ps[bank][:].rearrange("p (d m) -> p d m", d=4)),
                        waits=[("t", tt)] + (self.free["y"] if (t == 0 and bi == 0) else []), inc=True)
                    xv = self.S(lambda e: e.copy(sb["stats"][:, 6:7], sb["stats"][:, 7:8]),
                                waits=[("t", tt)], inc=True)
                else:
                    yv = self.V(lambda e, bank=bank, bi=bi, t=t: e.tensor_copy(
                        sb["y"][:].rearrange("p (d m) -> p d m", d=8)[:, 4 * bi:4 * bi + 4, t * 128:(t + 1) * 128],
                        ps[bank][:].rearrange("p (d m) -> p d m", d=4)),
                        waits=[("t", tt)] + (self.free["y"] if (t == 0 and bi == 0) else []), inc=True)
                    xv = None
                    for dd in range(4):
                        d2 = 4 * bi + dd
                        xv = self.S(lambda e, d2=d2, t=t: e.copy(
                            sb["xnb"][:, d2 * CH + t * 128: d2 * CH + (t + 1) * 128],
                            sb["y"][:, d2 * CH + t * 128: d2 * CH + (t + 1) * 128]),
                            waits=([("v", yv)] + (self.free["xnb"] if (t == 0 and bi == 0) else []))
                            if dd == 0 else (),
                            inc=(dd == 3))
                self.bankfree[bank] = [("v", yv), ("s", xv)]
                evs += [("v", yv), ("s", xv)]
            self.xnt_free[sl] = [("t", self.tick("t"))]
            last_evs = evs
        self.xn_ready = last_evs
        del last_evs
        if parts < 2:
            self.y_ready = self.xn_ready
            self.kv_tick = 0
            return
        self.y_ready = self.xn_ready
        if parts < 1:
            self.kv_tick = 0
            return
        gsp = self.G(lambda e, s=s, c=c: e.dma_start(
            out=dr["xnsp"].rearrange("(k d p) m -> k p d m", d=8, p=128)[s * NCH + c],
            in_=sb["xnb"][:].rearrange("p (d m) -> p d m", d=8)),
            waits=self.xn_ready, sem="dspx")
        self.xnb_spilled = gsp
        if parts < 3:
            self.kv_tick = 0
            return
        wk = dr["wkT"].rearrange("(t p) m -> p t m", p=128)
        wv = dr["wvT"].rearrange("(t p) m -> p t m", p=128)
        for wi, (wsrc, dst) in enumerate(((wk, "qp"), (wv, "kp"))):
            for blk in range(8):
                cb, ct = self.stream_w(wsrc[:, :, blk * 128:(blk + 1) * 128], 1024, key=("okv", wi, blk))
                wsu = self.wsem_used
                wb = self.wbuf_used
                wcs = self.cur_wslot
                bank = ("B0", "B1")[blk % 2]
                mmt = None
                for t in range(4):
                    for d in range(8):
                        mmt = self.T(lambda e, bank=bank, t=t, d=d, cb=cb, wb=wb: e.matmul(
                            ps[bank][:, t * 128:(t + 1) * 128],
                            sb["xnb"][:, d * 512 + t * 128: d * 512 + (t + 1) * 128],
                            wb[:, cb + d * 128: cb + (d + 1) * 128],
                            start=(d == 0), stop=(d == 7)),
                            waits=([(wsu, ct)] + self.xn_ready + self.bankfree[bank]) if (t == 0 and d == 0) else (),
                            inc=(t == 3 and d == 7))
                mmt = self.tick("t")
                self.cur_consumed = wcs; self.wdone(cb, mmt)
                ev = self.V(lambda e, dst=dst, blk=blk, bank=bank: e.tensor_copy(
                    sb[dst][:].rearrange("p (t m) -> p t m", t=4)[:, :, blk * 128:(blk + 1) * 128],
                    ps[bank][:].rearrange("p (t m) -> p t m", t=4)),
                    waits=[("t", mmt)] + (self.free["qp" if dst == "qp" else "kp"] if blk == 0 else []),
                    inc=True)
                self.bankfree[bank] = [("v", ev)]
        kvready = [("v", self.tick("v"))]
        self.free["xnb"] = [("t", self.tick("t"))]
        if parts < 4:
            self.kv_tick = 0
            return
        mmt = None
        for t in range(4):
            for h in range(H):
                bank = "C0" if h < 8 else "C1"
                mmt = self.T(lambda e, t=t, h=h, bank=bank: e.matmul(
                    ps[bank][0:64, (h % 8) * 64: (h % 8 + 1) * 64],
                    sb["qp"][:, t * 1024 + h * 64: t * 1024 + (h + 1) * 64],
                    sb["kp"][:, t * 1024 + h * 64: t * 1024 + (h + 1) * 64],
                    start=(t == 0 and h % 8 == 0), stop=(t == 3 and h % 8 == 7),
                    skip_group_check=True),
                    waits=(kvready + self.bankfree["C0"] + self.bankfree["C1"])
                          if (t == 0 and h == 0) else (),
                    inc=(t == 3 and h == H - 1))
        self.free["qp"] = [("t", mmt)]
        self.free["kp"] = [("t", mmt)]
        kvv0 = self.V(lambda e: e.tensor_tensor(sb["kv"][:, 0:512], sb["kv"][:, 0:512],
                                                ps["C0"][0:64, :], op=ALU.add),
                      waits=[("t", mmt), ("v", self.seq_v)], inc=True)
        kvv = self.V(lambda e: e.tensor_tensor(sb["kv"][:, 512:1024], sb["kv"][:, 512:1024],
                                               ps["C1"][0:64, :], op=ALU.add), inc=True)
        self.bankfree["C0"] = [("v", kvv0)]
        self.bankfree["C1"] = [("v", kvv)]
        self.kv_tick = kvv

    def attn(self, s, c, l):
        sb, dr, ps = self.sb, self.dr, self.ps
        P = self.P
        ccol = c * CH
        x0 = self.V(lambda e: e.tensor_copy(sb["xs"][:, 0:4 * CH], sb["y"][:, 0:4 * CH]),
                    waits=self.free["xs"] + self.y_state, inc=True)
        for d in range(4, 8):
            self.V(lambda e, d=d: e.tensor_copy(sb["xs"][:, d * CH + 1:(d + 1) * CH],
                                                sb["y"][:, d * CH:d * CH + CH - 1]))
        self.V(lambda e, l=l: e.tensor_copy(
            sb["xs"][:].rearrange("p (d m) -> p d m", d=8)[:, 4:8, 0:1],
            sb["acar"][:].rearrange("p (l d) -> p l d", l=NL)[:, l, :, None]))
        acv = self.V(lambda e, l=l: e.tensor_copy(
            sb["acar"][:].rearrange("p (l d) -> p l d", l=NL)[:, l, :],
            sb["y"][:].rearrange("p (d m) -> p d m", d=8)[:, 4:8, CH - 1]), inc=True)
        xs_done = [("v", acv)]
        self.free["xs"] = []

        wq3 = dr["wqkv"].rearrange("l (t p) m -> l p t m", p=128)[l]
        for which, (dst, ssbank, tabs, scl) in enumerate(
                (("qp", "C0", ("cs8b", "ss8b"), "qss"), ("kp", "C1", ("cik b", "sikb"), "kss"))):
            tabs = (tabs[0].replace(" ", ""), tabs[1])
            rqs_t = self.V(lambda e, scl=scl, l=l: e.tensor_scalar_mul(
                sb["rqs"][:], sb["rmat"][:], sb[scl][:, l:l + 1]),
                waits=[("t", self.tick("t"))], inc=True)
            asq_rd = [[], []]
            for dt in range(8):
                col0 = (which * 8 + dt) * 128
                cb, ct = self.stream_w(wq3[:, :, col0:col0 + 128], 1024, key=("qkv", l, col0))
                wsu = self.wsem_used
                wb = self.wbuf_used
                wcs = self.cur_wslot
                bank = ("A0", "A1")[dt % 2]
                for d in range(8):
                    mmt = self.T(lambda e, bank=bank, d=d, cb=cb, wb=wb: e.matmul(
                        ps[bank][:], wb[:, cb + d * 128: cb + (d + 1) * 128],
                        sb["xs"][:, d * CH:(d + 1) * CH],
                        start=(d == 0), stop=(d == 7), skip_group_check=True),
                        waits=([(wsu, ct)] + xs_done + self.bankfree[bank] +
                               (self.free[dst] if dt == 0 else [])) if d == 0 else (),
                        inc=(d == 7))
                self.cur_consumed = wcs; self.wdone(cb, mmt)
                ev = self.V(lambda e, dst=dst, dt=dt, bank=bank: e.tensor_copy(
                    sb[dst][:, dt * CH:(dt + 1) * CH], ps[bank][:]),
                    waits=[("t", mmt)], inc=True)
                self.bankfree[bank] = [("v", ev)]
                sq = self.S(lambda e, dst=dst, dt=dt: e.activation(
                    sb["sqbf"][:, (dt % 2) * CH:(dt % 2 + 1) * CH], sb[dst][:, dt * CH:(dt + 1) * CH],
                    AF.Square), waits=[("v", ev)] + asq_rd[dt % 2], inc=True)
                cst = self.T(lambda e, dt=dt, ssbank=ssbank: e.matmul(
                    ps[ssbank][0:16, :], sb["l2ones"][:, dt * 16:(dt + 1) * 16],
                    sb["sqbf"][:, (dt % 2) * CH:(dt % 2 + 1) * CH],
                    start=(dt == 0), stop=(dt == 7), skip_group_check=True),
                    waits=[("s", sq)] + (self.bankfree[ssbank] if dt == 0 else []), inc=True)
                asq_rd[dt % 2] = [("t", cst)]
            mx = self.V(lambda e, ssbank=ssbank: e.tensor_single_scalar(
                sb["invt"][:, 0:512], ps[ssbank][0:16, :], 1e-24, op=ALU.max),
                waits=[("t", cst)], inc=True)
            self.bankfree[ssbank] = [("v", mx)]
            lnt_ = self.S(lambda e: e.activation(sb["invt"][:, 512:1024], sb["invt"][:, 0:512], AF.Ln),
                          waits=[("v", mx)], inc=True)
            ivt = self.S(lambda e: e.activation(sb["invtb"][:, :], sb["invt"][:, 512:1024],
                                                AF.Exp, scale=-0.5), waits=[("s", lnt_)], inc=True)
            rq_rd = [list(asq_rd[0]), list(asq_rd[1])]
            for dt in range(8):
                rqb = ("B0", "B1")[dt % 2]
                ivb = ("C0", "C1")[dt % 2]
                rq = self.T(lambda e, rqb=rqb, dst=dst, dt=dt: e.matmul(
                    ps[rqb][:], sb["rqs"][:], sb[dst][:, dt * CH:(dt + 1) * CH],
                    start=True, stop=True),
                    waits=([("v", rqs_t)] if dt == 0 else []) + self.bankfree[rqb], inc=True)
                iv = self.T(lambda e, ivb=ivb, dt=dt: e.matmul(
                    ps[ivb][:], sb["l2pat"][:, dt * 128:(dt + 1) * 128], sb["invtb"][:, :],
                    start=True, stop=True),
                    waits=[("s", ivt)] + self.bankfree[ivb], inc=True)
                rqc = self.S(lambda e, rqb=rqb, dt=dt: e.copy(
                    sb["sqbf"][:, (dt % 2) * CH:(dt % 2 + 1) * CH], ps[rqb][:]),
                    waits=[("t", rq)] + rq_rd[dt % 2], inc=True)
                self.bankfree[rqb] = [("s", rqc)]
                self.V(lambda e, dst=dst, dt=dt, tabs=tabs, ccol=ccol: e.tensor_tensor(
                    sb["rot"][:, 0:CH], sb[dst][:, dt * CH:(dt + 1) * CH],
                    sb[tabs[0]][:, ccol:ccol + CH], op=ALU.mult))
                t1s = self.V(lambda e, scl=scl, l=l: e.tensor_scalar_mul(
                    sb["rot"][:, 3 * CH:4 * CH], sb["rot"][:, 0:CH], sb[scl][:, l:l + 1]),
                    inc=True)
                t2r = self.V(lambda e, dt=dt, tabs=tabs, ccol=ccol: e.tensor_tensor(
                    sb["rot"][:, CH:2 * CH], sb["sqbf"][:, (dt % 2) * CH:(dt % 2 + 1) * CH],
                    sb[tabs[1]][:, ccol:ccol + CH], op=ALU.mult),
                    waits=[("s", rqc)], inc=True)
                rq_rd[dt % 2] = [("v", t2r)]
                t3 = self.V(lambda e: e.tensor_add(sb["rot"][:, 2 * CH:3 * CH],
                                                   sb["rot"][:, 3 * CH:4 * CH],
                                                   sb["rot"][:, CH:2 * CH]),
                            waits=[("v", t1s)], inc=True)
                qf = self.V(lambda e, dst=dst, dt=dt, ivb=ivb: e.tensor_tensor(
                    sb[dst][:, dt * CH:(dt + 1) * CH], sb["rot"][:, 2 * CH:3 * CH],
                    ps[ivb][:], op=ALU.mult), waits=[("t", iv), ("v", t3)], inc=True)
                self.bankfree[ivb] = [("v", qf)]
            if which == 0:
                qdone = [("v", self.tick("v"))]
            else:
                kdone = [("v", self.tick("v"))]
        for blk in range(8):
            cb, ct = self.stream_w(wq3[:, :, 2048 + blk * 128:2048 + (blk + 1) * 128], 1024, key=("qkv", l, 2048 + blk * 128))
            wsu = self.wsem_used
            wb = self.wbuf_used
            wcs = self.cur_wslot
            bank = ("A0", "A1")[blk % 2]
            for d in range(8):
                mmt = self.T(lambda e, bank=bank, d=d, cb=cb, wb=wb: e.matmul(
                    ps[bank][:], wb[:, cb + d * 128: cb + (d + 1) * 128],
                    sb["xs"][:, d * CH:(d + 1) * CH],
                    start=(d == 0), stop=(d == 7), skip_group_check=True),
                    waits=([(wsu, ct)] + self.bankfree[bank] +
                           (self.free["vfs"] + self.xt_free[0] + self.xt_free[1]
                            if blk == 0 else [])) if d == 0 else (),
                    inc=(d == 7))
            self.cur_consumed = wcs; self.wdone(cb, mmt)
            ev = self.V(lambda e, blk=blk, bank=bank: e.tensor_copy(
                sb["vfs"][:, blk * CH:(blk + 1) * CH], ps[bank][:]),
                waits=[("t", mmt)], inc=True)
            self.bankfree[bank] = [("v", ev)]
        vfs_done = [("v", self.tick("v"))]
        for st in range(4):
            ptb = ("PT0", "PT1")[st % 2]
            for blk in range(8):
                tt = self.T(lambda e, ptb=ptb, st=st, blk=blk: e.transpose(
                    ps[ptb][:, blk * 128:(blk + 1) * 128],
                    sb["vfs"][:, blk * CH + st * 128: blk * CH + (st + 1) * 128],
                    sb["identb"][:]),
                    waits=(vfs_done + self.bankfree[ptb] + (self.free["vaug"] if (st == 0 and blk == 0) else [])) if blk == 0 else (),
                    inc=(blk == 7))
            ev = self.V(lambda e, ptb=ptb, st=st: e.tensor_copy(
                sb["vaug"][:, st * 1040:(st + 1) * 1040].rearrange("p (h q) -> p h q", q=65)[:, :, 0:64],
                ps[ptb][:].rearrange("p (h q) -> p h q", q=64)),
                waits=[("t", tt)], inc=True)
            self.bankfree[ptb] = [("v", ev)]
        vaug_done = [("v", self.tick("v"))]
        self.xt_free[0] = [("t", self.tick("t"))]
        self.xt_free[1] = [("t", self.tick("t"))]
        self.free["vfs"] = [("t", self.tick("t"))]
        return qdone, kdone, vaug_done

    def attn_core(self, s, c, l, qdone, kdone, vaug_done):
        sb, dr, ps = self.sb, self.dr, self.ps
        first = (c == 0)
        nmx_i = [0]
        pbuf_i = [0]
        pbuf_free = [[] for _ in range(8)]
        nmax_free = [[] for _ in range(8)]
        ptsb_free = [[], []]
        for w in (0, 1):
            states = {}

            def emit_sims(p):
                simbanks = ("A0", "A1", "B0", "B1")
                st_p = {}
                for par in (0, 1):
                    r0 = par * 64
                    for i in (0, 1):
                        bank = simbanks[par * 2 + i]
                        segs = []
                        if w == 0:
                            kc = sb["kcar"][:].rearrange("p (L q m) -> p L q m", L=NL, q=8)
                            if i == 0:
                                segs.append((kc[r0:r0 + 64, l, p, 0:256], 0, 256))
                                segs.append((sb["kp"][r0:r0 + 64, p * CH: p * CH + 128], 256, 128))
                            else:
                                segs.append((kc[r0:r0 + 64, l, p, 128:256], 0, 128))
                                segs.append((sb["kp"][r0:r0 + 64, p * CH: p * CH + 256], 128, 256))
                        else:
                            j0 = i * 128
                            segs.append((sb["kp"][r0:r0 + 64, p * CH + j0: p * CH + j0 + 384], 0, 384))
                        lq = sb["qp"][r0:r0 + 64, p * CH + w * 256 + i * 128: p * CH + w * 256 + (i + 1) * 128]
                        w0waits = qdone + kdone + list(self.bankfree[bank])
                        for si, (rhs, o, n) in enumerate(segs):
                            self.T(lambda e, bank=bank, lq=lq, rhs=rhs, o=o, n=n, si=si: e.matmul(
                                ps[bank][:, o:o + n], lq, rhs, start=(si == 0), stop=False,
                                skip_group_check=True),
                                waits=w0waits if si == 0 else (), inc=False)
                        mi = (2 if (first and w == 0) else 0) + i
                        mmt = self.T(lambda e, bank=bank, mi=mi: e.matmul(
                            ps[bank][:, 0:384], sb["identb"][:], sb["masks"][:, mi * 384:(mi + 1) * 384],
                            start=False, stop=True, skip_group_check=True), inc=True)
                        nc_ = nmx_i[0] % 8
                        nmx_i[0] += 1
                        rmx = self.V(lambda e, bank=bank, nc_=nc_: e.reduce_max(
                            sb["nmax"][:, nc_:nc_ + 1], ps[bank][:, 0:384], axis=AX, negate=True),
                            waits=[("t", mmt)] + nmax_free[nc_], inc=True)
                        pb = pbuf_i[0] % 8
                        pbuf_i[0] += 1
                        ex = self.S(lambda e, bank=bank, nc_=nc_, pb=pb: e.activation(
                            sb["pbuf"][:, pb * CH: pb * CH + 384], ps[bank][:, 0:384], AF.Exp,
                            bias=sb["nmax"][:, nc_:nc_ + 1],
                            accum_out=sb["stats"][:, nc_:nc_ + 1]),
                            waits=[("v", rmx)] + pbuf_free[pb], inc=True)
                        nmax_free[nc_] = [("s", ex)]
                        self.bankfree[bank] = [("s", ex)]
                        rcp = self.V(lambda e, nc_=nc_: e.reciprocal(
                            sb["stats"][:, 8 + nc_:9 + nc_], sb["stats"][:, nc_:nc_ + 1]),
                            waits=[("s", ex)], inc=True)
                        scl_t = self.V(lambda e, nc_=nc_, pb=pb: e.tensor_scalar_mul(
                            sb["pbuf"][:, pb * CH: pb * CH + 384],
                            sb["pbuf"][:, pb * CH: pb * CH + 384],
                            sb["stats"][:, 8 + nc_:9 + nc_]),
                            waits=[("v", rcp)], inc=True)
                        st_p[(par, i)] = (pb, scl_t)
                return st_p

            def emit_tail(p, st_p):
                ptev = [None, None]
                for par in (0, 1):
                    ptb = ("PT0", "PT1")[par]
                    tt = None
                    for i in (0, 1):
                        pb, scl_t = st_p[(par, i)]
                        for jj in range(3):
                            jt = jj + i
                            tt = self.T(lambda e, ptb=ptb, pb=pb, jj=jj, jt=jt, i=i: e.transpose(
                                ps[ptb][:, jt * 256 + i * 128: jt * 256 + i * 128 + 128],
                                sb["pbuf"][:, pb * CH + jj * 128: pb * CH + (jj + 1) * 128],
                                sb["identb"][:]),
                                waits=([("v", scl_t)] + (self.bankfree[ptb] if (i == 0 and jj == 0) else [])),
                                inc=(jj == 2))
                        pbuf_free[pb] = [("t", tt)]
                    pev = self.V(lambda e, ptb=ptb, par=par: e.tensor_copy(
                        sb["ptsb"][:, par * 1024:(par + 1) * 1024], ps[ptb][:]),
                        waits=[("t", tt)] + ptsb_free[par], inc=True)
                    self.bankfree[ptb] = [("v", pev)]
                    ptev[par] = pev
                for par in (0, 1):
                    h = 2 * p + par
                    pvb = ("C0", "C1")[par]
                    vsrcs = []
                    for jt in range(4):
                        if w == 0 and jt < 2:
                            vc = sb["vcar"][:].rearrange("p (L q m) -> p L q m", L=NL, q=2)
                            vsrcs.append(vc[:, l, jt, h * 65:h * 65 + 64])
                        else:
                            st = (jt - 2) if w == 0 else jt
                            vsrcs.append(sb["vaug"][:, st * 1040 + h * 65: st * 1040 + h * 65 + 64])
                    mmt = None
                    order = (1, 2, 0, 3)
                    for oi, jt in enumerate(order):
                        ilo = 128 if jt == 3 else 0
                        ihi = 128 if jt == 0 else 256
                        mmt = self.T(lambda e, pvb=pvb, jt=jt, par=par, vs=vsrcs[jt], ilo=ilo, ihi=ihi,
                                     oi=oi: e.matmul(
                            ps[pvb][0:64, ilo:ihi], vs,
                            sb["ptsb"][:, par * 1024 + jt * 256 + ilo: par * 1024 + jt * 256 + ihi],
                            start=(oi == 0), stop=(oi == 3), skip_group_check=True),
                            waits=([("v", ptev[par])] + vaug_done + self.bankfree[pvb]) if oi == 0 else (),
                            inc=(oi == 3))
                    ptsb_free[par] = [("t", mmt)]
                    at = self.V(lambda e, pvb=pvb, r0=par * 64, p=p, w=w: e.tensor_copy(
                        sb["afm"][r0:r0 + 64, p * CH + w * 256: p * CH + (w + 1) * 256],
                        ps[pvb][0:64, 0:256]),
                        waits=[("t", mmt)] + (self.free["afm"] if (w == 0 and p == 0 and par == 0) else []),
                        inc=True)
                    self.bankfree[pvb] = [("v", at)]

            for p in range(9):
                if p < 8:
                    states[p] = emit_sims(p)
                if p >= 1:
                    emit_tail(p - 1, states.pop(p - 1))
            if w == 0:
                tnow = self.tick("t")
                kcv = self.V(lambda e, l=l: e.tensor_copy(
                    sb["kcar"][:].rearrange("p (L q m) -> p L q m", L=NL, q=8)[:, l],
                    sb["kp"][:].rearrange("p (q m) -> p q m", q=8)[:, :, 256:512]),
                    waits=[("t", tnow)], inc=True)
                vcv = self.V(lambda e, l=l: e.tensor_copy(
                    sb["vcar"][:].rearrange("p (L q m) -> p L q m", L=NL, q=2)[:, l],
                    sb["vaug"][:].rearrange("p (q m) -> p q m", q=4)[:, 2:4]),
                    waits=[("t", tnow)], inc=True)
        afm_done = [("v", self.tick("v"))]
        self.free["qp"] = [("t", self.tick("t"))]
        self.free["kp"] = [("t", self.tick("t"))]
        self.free["vaug"] = [("t", self.tick("t"))]
        self.free["xs"] = [("t", self.tick("t"))]
        wo3 = dr["wo_i"].rearrange("l (t p) m -> l p t m", p=128)[l]
        for dt in range(8):
            cb, ct = self.stream_w(wo3[:, :, dt * 128:(dt + 1) * 128], 1024, key=("woi", l, dt))
            wsu = self.wsem_used
            wb = self.wbuf_used
            wcs = self.cur_wslot
            bank = ("A0", "A1")[dt % 2]
            for d in range(8):
                mmt = self.T(lambda e, bank=bank, d=d, cb=cb, wb=wb: e.matmul(
                    ps[bank][:], wb[:, cb + d * 128: cb + (d + 1) * 128],
                    sb["afm"][:, d * CH:(d + 1) * CH],
                    start=(d == 0), stop=(d == 7), skip_group_check=True),
                    waits=([(wsu, ct)] + afm_done + self.bankfree[bank]) if d == 0 else (),
                    inc=(d == 7))
            self.cur_consumed = wcs; self.wdone(cb, mmt)
            yv = self.V(lambda e, dt=dt, bank=bank: e.scalar_tensor_tensor(
                sb["y"][:, dt * CH:(dt + 1) * CH], ps[bank][:], 1.0,
                sb["y"][:, dt * CH:(dt + 1) * CH], op0=ALU.mult, op1=ALU.add),
                waits=[("t", mmt)], inc=True)
            self.bankfree[bank] = [("v", yv)]
        self.y_state = [("v", self.tick("v"))]
        self.free["afm"] = [("t", self.tick("t"))]
        self.free["regB"] = [("t", self.tick("t"))]

    def ffn(self, s, c, l):
        sb, dr, ps = self.sb, self.dr, self.ps
        x0 = self.V(lambda e: e.tensor_copy(sb["xs"][:, 0:4 * CH], sb["y"][:, 0:4 * CH]),
                    waits=self.free["xs"] + self.y_state, inc=True)
        for d in range(4, 8):
            self.V(lambda e, d=d: e.tensor_copy(sb["xs"][:, d * CH + 1:(d + 1) * CH],
                                                sb["y"][:, d * CH:d * CH + CH - 1]))
        xcp = self.V(lambda e, l=l: e.tensor_copy(
            sb["xs"][:].rearrange("p (d m) -> p d m", d=8)[:, 4:8, 0:1],
            sb["fcar"][:].rearrange("p (l d) -> p l d", l=NL)[:, l, :, None]), inc=True)
        xs_done = [("v", xcp)]
        cst = sst = None
        sq_rd = [[], []]
        SQV = False
        for d in range(8):
            if SQV:
                sq = self.V(lambda e, d=d: e.tensor_mul(
                    sb["sqbf"][:, (d % 2) * CH:(d % 2 + 1) * CH], sb["xs"][:, d * CH:(d + 1) * CH],
                    sb["xs"][:, d * CH:(d + 1) * CH]),
                    waits=(list(xs_done) if d == 0 else []) + sq_rd[d % 2], inc=True)
                sqw = [("v", sq)]
            else:
                sq = self.S(lambda e, d=d: e.activation(
                    sb["sqbf"][:, (d % 2) * CH:(d % 2 + 1) * CH], sb["xs"][:, d * CH:(d + 1) * CH],
                    AF.Square), waits=(list(xs_done) if d == 0 else []) + sq_rd[d % 2], inc=True)
                sqw = [("s", sq)]
            cst = self.T(lambda e, d=d: e.matmul(
                ps["C0"][0:1, :], sb["onesb"][:, 0:1], sb["xs"][:, d * CH:(d + 1) * CH],
                start=(d == 0), stop=(d == 7), skip_group_check=True),
                waits=(xs_done + self.bankfree["C0"]) if d == 0 else (), inc=True)
            sst = self.T(lambda e, d=d: e.matmul(
                ps["C1"][0:1, :], sb["onesb"][:, 0:1],
                sb["sqbf"][:, (d % 2) * CH:(d % 2 + 1) * CH],
                start=(d == 0), stop=(d == 7), skip_group_check=True),
                waits=sqw + (self.bankfree["C1"] if d == 0 else []), inc=True)
            sq_rd[d % 2] = [("t", sst)]
        mu = self.V(lambda e: e.tensor_scalar_mul(sb["lnt"][0:1, 0:512], ps["C0"][0:1, :], 1.0 / 1024.0),
                    waits=[("t", cst)], inc=True)
        m2 = self.V(lambda e: e.tensor_scalar_mul(sb["lnt"][0:1, 512:1024], ps["C1"][0:1, :], 1.0 / 1024.0),
                    waits=[("t", sst)], inc=True)
        mu2 = self.V(lambda e: e.tensor_mul(sb["lnt"][0:1, 1024:1536], sb["lnt"][0:1, 0:512],
                                            sb["lnt"][0:1, 0:512]), waits=[("v", mu)], inc=True)
        var = self.V(lambda e: e.tensor_sub(sb["lnt"][0:1, 1024:1536], sb["lnt"][0:1, 512:1024],
                                            sb["lnt"][0:1, 1024:1536]),
                     waits=[("v", mu2), ("v", m2)], inc=True)
        lnv = self.S(lambda e: e.activation(sb["lnt"][0:1, 512:1024], sb["lnt"][0:1, 1024:1536], AF.Ln,
                                            bias=self.cap[1e-5][0:1, 0:1]), waits=[("v", var)], inc=True)
        rstd = self.S(lambda e: e.activation(sb["lnt"][0:1, 1024:1536], sb["lnt"][0:1, 512:1024],
                                             AF.Exp, scale=-0.5), waits=[("s", lnv)], inc=True)
        nmr = self.V(lambda e: e.scalar_tensor_tensor(
            sb["lnt"][0:1, 1536:2048], sb["lnt"][0:1, 0:512], -1.0, sb["lnt"][0:1, 1024:1536],
            op0=ALU.mult, op1=ALU.mult), waits=[("s", rstd)], inc=True)
        rb_t = self.T(lambda e: e.matmul(ps["C0"][:], sb["onesf"][0:1, :], sb["lnt"][0:1, 1024:1536],
                                         start=True, stop=True, skip_group_check=True),
                      waits=[("v", nmr)], inc=True)
        nb_t = self.T(lambda e: e.matmul(ps["C1"][:], sb["onesf"][0:1, :], sb["lnt"][0:1, 1536:2048],
                                         start=True, stop=True, skip_group_check=True),
                      waits=[("v", nmr)], inc=True)
        for d in range(8):
            if d < 4:
                t1 = self.V(lambda e, d=d: e.tensor_tensor(
                    sb["otm"][:, 0:CH], sb["y"][:, d * CH:(d + 1) * CH], ps["C0"][:], op=ALU.mult),
                    waits=([("t", rb_t), ("t", nb_t)] + self.free["otm"]) if d == 0 else (), inc=True)
            else:
                t1c = self.V(lambda e, d=d, l=l: e.tensor_tensor(
                    sb["otm"][:, 0:1],
                    sb["fcar"][:].rearrange("p (l d) -> p l d", l=NL)[:, l, d - 4, None], ps["C0"][:, 0:1],
                    op=ALU.mult),
                    waits=(([("t", rb_t), ("t", nb_t)] + self.free["otm"]) if d == 4 else []), inc=True)
                t1 = self.V(lambda e, d=d: e.tensor_tensor(
                    sb["otm"][:, 1:CH], sb["y"][:, d * CH:d * CH + CH - 1], ps["C0"][:, 1:CH],
                    op=ALU.mult), inc=True)
                t2w = [("v", t1), ("v", t1c)]
            if d < 4:
                t2w = [("v", t1)]
            t2 = self.V(lambda e: e.tensor_tensor(sb["otm"][:, CH:2 * CH], sb["otm"][:, 0:CH],
                                                  ps["C1"][:], op=ALU.add), waits=t2w, inc=True)
            xv = self.V(lambda e, d=d, l=l: e.tensor_scalar(
                sb["xnb"][:, d * CH:(d + 1) * CH], sb["otm"][:, CH:2 * CH],
                sb["lngs"][:, l * 8 + d:l * 8 + d + 1], sb["lnbs"][:, l * 8 + d:l * 8 + d + 1],
                op0=ALU.mult, op1=ALU.add),
                waits=[("v", t2)] + (self.free["xnb"] if d == 0 else []), inc=True)
        self.V(lambda e, l=l: e.tensor_copy(
            sb["fcar"][:].rearrange("p (l d) -> p l d", l=NL)[:, l, :],
            sb["y"][:].rearrange("p (d m) -> p d m", d=8)[:, 4:8, CH - 1]), inc=True)
        xn_done = [("v", self.tick("v"))]
        self.bankfree["C0"] = [("v", self.tick("v")), ("s", self.tick("s"))]
        self.bankfree["C1"] = [("v", self.tick("v")), ("s", self.tick("s"))]
        self.free["otm"] = [("s", self.tick("s"))]
        w13 = dr["w1"].rearrange("l (t p) m -> l p t m", p=128)[l]
        gb = sb["regB"].bitcast(BF16)
        gel_free = [list(self.free["gel"]), list(self.free["gel"])]
        for ht in range(HT):
            cba, cta = self.stream_w(w13[:, :, ht * 128:(ht + 1) * 128], 1024, key=("w1", l, ht))
            wsua = self.wsem_used
            wba = self.wbuf_used
            wcsa = self.cur_wslot
            banka = ("A0", "A1")[ht % 2]
            for d in range(8):
                mma = self.T(lambda e, bank=banka, d=d, cb=cba, wb=wba: e.matmul(
                    ps[bank][:], wb[:, cb + d * 128: cb + (d + 1) * 128],
                    sb["xnb"][:, d * CH:(d + 1) * CH],
                    start=(d == 0), stop=(d == 7), skip_group_check=True),
                    waits=([(wsua, cta)] + xn_done + self.bankfree[banka] +
                           (self.free["regB"] if ht == 0 else [])) if d == 0 else (),
                    inc=(d == 7))
            self.cur_consumed = wcsa; self.wdone(cba, mma)
            cbg, ctg = self.stream_w(w13[:, :, HID + ht * 128:HID + (ht + 1) * 128], 1024, key=("w1g", l, ht))
            wsug = self.wsem_used
            wbg = self.wbuf_used
            wcsg = self.cur_wslot
            bankg = ("B0", "B1")[ht % 2]
            for d in range(8):
                mmg = self.T(lambda e, bank=bankg, d=d, cb=cbg, wb=wbg: e.matmul(
                    ps[bank][:], wb[:, cb + d * 128: cb + (d + 1) * 128],
                    sb["xnb"][:, d * CH:(d + 1) * CH],
                    start=(d == 0), stop=(d == 7), skip_group_check=True),
                    waits=([(wsug, ctg)] + self.bankfree[bankg]) if d == 0 else (),
                    inc=(d == 7))
            self.cur_consumed = wcsg; self.wdone(cbg, mmg)
            gl = self.S(lambda e, bank=bankg, ht=ht, l=l: e.activation(
                sb["gel"][:, (ht % 2) * CH:(ht % 2 + 1) * CH], ps[bank][:], AF.Gelu,
                bias=sb["b1s"][:, l * 2 * HT + HT + ht:l * 2 * HT + HT + ht + 1]),
                waits=[("t", mmg)] + gel_free[ht % 2], inc=True)
            self.bankfree[bankg] = [("s", gl)]
            gv = self.V(lambda e, bank=banka, ht=ht, l=l: e.scalar_tensor_tensor(
                gb[:, ht * CH:(ht + 1) * CH], ps[bank][:],
                sb["b1s"][:, l * 2 * HT + ht:l * 2 * HT + ht + 1],
                sb["gel"][:, (ht % 2) * CH:(ht % 2 + 1) * CH], op0=ALU.add, op1=ALU.mult),
                waits=[("t", mma), ("s", gl)], inc=True)
            gel_free[ht % 2] = [("v", gv)]
            self.bankfree[banka] = [("v", gv)]
        gated_done = [("v", self.tick("v"))]
        w23 = dr["w2"].rearrange("l (k p) m -> l p k m", p=128)[l]
        for dt in range(8):
            mmt = None
            for half in range(2):
                cb, ct = self.stream_w(w23[:, :, dt * 128:(dt + 1) * 128][:, half * 11:(half + 1) * 11],
                                       11 * 128, key=("w2", l, dt, half))
                wsu = self.wsem_used
                wb = self.wbuf_used
                wcs = self.cur_wslot
                for k in range(11):
                    kk = half * 11 + k
                    mmt = self.T(lambda e, dt=dt, k=k, kk=kk, cb=cb, wb=wb: e.matmul(
                        ps[("A0", "A1")[dt % 2]][:], wb[:, cb + k * 128: cb + (k + 1) * 128],
                        gb[:, kk * CH:(kk + 1) * CH],
                        start=(kk == 0), stop=(kk == 21), skip_group_check=True),
                        waits=([(wsu, ct)] + gated_done +
                               (self.bankfree[("A0", "A1")[dt % 2]] if kk == 0 else [])) if k == 0 else (),
                        inc=(k == 10))
                self.cur_consumed = wcs; self.wdone(cb, mmt)
            yv = self.V(lambda e, dt=dt: e.scalar_tensor_tensor(
                sb["y"][:, dt * CH:(dt + 1) * CH], ps[("A0", "A1")[dt % 2]][:],
                sb["b2s"][:, l * 8 + dt:l * 8 + dt + 1],
                sb["y"][:, dt * CH:(dt + 1) * CH], op0=ALU.add, op1=ALU.add),
                waits=[("t", mmt)], inc=True)
            self.bankfree[("A0", "A1")[dt % 2]] = [("v", yv)]
        self.y_state = [("v", self.tick("v"))]
        self.free["xs"] = [("t", self.tick("t"))]
        self.free["xnb"] = [("t", self.tick("t"))]
        self.free["regB"] = [("t", self.tick("t"))]
        self.free["afm"] = [("t", self.tick("t"))]
        self.free["gel"] = [("t", self.tick("t"))]

    def chunk_end(self, s, c):

        sb, dr = self.sb, self.dr
        gt = self.G(lambda e, s=s, c=c: e.dma_start(
            out=dr["ysp"].rearrange("(k d p) m -> k p d m", d=8, p=128)[s * NCH + c],
            in_=sb["y"][:].rearrange("p (d m) -> p d m", d=8)),
            waits=self.y_state, sem="dspy")
        self.free["y"] = [("dspy", gt)]

    def output(self, s):
        OSTAGE = 9
        sb, dr, ps = self.sb, self.dr, self.ps
        k1 = None
        if False:
            pass
        else:
          for h in range(H):
            par, p = h % 2, h // 2
            k1 = self.S(lambda e, par=par, p=p, h=h: e.activation(
                sb["kvb2"][par * 64:(par + 1) * 64, p * 64:(p + 1) * 64],
                sb["kv"][:, h * 64:(h + 1) * 64], AF.Copy, scale=0.125),
                waits=[("v", self.kv_tick)] if h == 0 else (), inc=(h == H - 1))
          kvb_done = [("s", k1)]
        woc = dr["woT"].rearrange("(t p) m -> p t m", p=128)
        wqc = dr["wqT"].rearrange("(t p) m -> p t m", p=128)
        for c in range(self.cfg.get("nchunks", NCH)):
            if True:
                gx = self.G(lambda e, s=s, c=c: e.dma_start(
                    out=sb["xnb"][:].rearrange("p (d m) -> p d m", d=8),
                    in_=dr["xnsp"].rearrange("(k d p) m -> k p d m", d=8, p=128)[s * NCH + c]),
                    waits=self.free["xnb"] + [("dspx", self.tick("dspx"))], sem="dgx")
            else:
                gx = 0
            if True:
                gy = self.G(lambda e, s=s, c=c: e.dma_start(
                    out=sb["y"][:].rearrange("p (d m) -> p d m", d=8),
                    in_=dr["ysp"].rearrange("(k d p) m -> k p d m", d=8, p=128)[s * NCH + c]),
                    waits=self.free["y"] + [("dspy", self.tick("dspy"))], sem="dgy")
            else:
                gy = 0
            if OSTAGE < 2:
                continue
            for dt in range(8):
                cb, ct = self.stream_w(wqc[:, :, dt * 128:(dt + 1) * 128], 1024, key=("wq", dt))
                wsu = self.wsem_used
                wb = self.wbuf_used
                wcs = self.cur_wslot
                bank = ("A0", "A1")[dt % 2]
                for d in range(8):
                    mmt = self.T(lambda e, bank=bank, d=d, cb=cb, wb=wb: e.matmul(
                        ps[bank][:], wb[:, cb + d * 128: cb + (d + 1) * 128],
                        sb["xnb"][:, d * CH:(d + 1) * CH],
                        start=(d == 0), stop=(d == 7), skip_group_check=True),
                        waits=([(wsu, ct), ("dgx", gx)] + self.bankfree[bank] +
                               (self.free["qp"] if dt == 0 else [])) if d == 0 else (),
                        inc=(d == 7))
                self.cur_consumed = wcs; self.wdone(cb, mmt)
                ev = self.V(lambda e, dt=dt, bank=bank: e.tensor_copy(
                    sb["qp"][:, dt * CH:(dt + 1) * CH], ps[bank][:]),
                    waits=[("t", mmt)], inc=True)
                self.bankfree[bank] = [("v", ev)]
            qo_done = [("v", self.tick("v"))]
            if OSTAGE < 3:
                continue
            for p in range(8):
                for par in (0, 1):
                    bank = ("B0", "B1")[par]
                    r0 = par * 64
                    mmt = self.T(lambda e, bank=bank, p=p, r0=r0: e.matmul(
                        ps[bank][0:64, :], sb["kvb2"][r0:r0 + 64, p * 64:(p + 1) * 64],
                        sb["qp"][r0:r0 + 64, p * CH:(p + 1) * CH], start=True, stop=True,
                        skip_group_check=True),
                        waits=kvb_done + qo_done + self.bankfree[bank] +
                              (self.free["xs"] if (p == 0 and par == 0) else []),
                        inc=True)
                    ev = self.V(lambda e, bank=bank, p=p, r0=r0: e.tensor_copy(
                        sb["xs"][r0:r0 + 64, p * CH:(p + 1) * CH], ps[bank][0:64, :]),
                        waits=[("t", mmt)], inc=True)
                    self.bankfree[bank] = [("v", ev)]
            sp_done = [("v", self.tick("v"))]
            if OSTAGE < 4:
                continue
            for t in range(4):
                for bi, bank in enumerate(("A0", "A1")):
                    tt = None
                    for d in range(4 * bi, 4 * bi + 4):
                        tt = self.T(lambda e, bank=bank, d=d, t=t: e.transpose(
                            ps[bank][:, (d % 4) * 128:((d % 4) + 1) * 128],
                            sb["y"][:, d * CH + t * 128: d * CH + (t + 1) * 128], sb["identf"][:]),
                            waits=([("dgy", gy)] + self.bankfree[bank]) if d % 4 == 0 else (),
                            inc=(d % 4 == 3))
                    gv = self.S(lambda e, bank=bank, bi=bi, t=t: e.activation(
                        sb["gtm"][:, t * 1024 + bi * 512: t * 1024 + (bi + 1) * 512],
                        ps[bank][:], AF.Silu),
                        waits=[("t", tt)] + (self.free["gtm"] + self.free["regB"]
                                             if (t == 0 and bi == 0) else []), inc=True)
                    self.bankfree[bank] = [("s", gv)]
            g_done = [("s", self.tick("s"))]
            base = s * S + c * CH
            st_dg = []
            for t in range(4):
                dx = self.Sy(lambda e, t=t, base=base: e.dma_start(
                    out=sb["xt"][:, (t % 2) * 1024:(t % 2 + 1) * 1024],
                    in_=dr["xin"][base + t * 128: base + (t + 1) * 128, :]),
                    waits=self.xt_free[t % 2], sem=("dxA", "dxB")[t % 2])
                for half in range(2):
                    bank = ("B0", "B1")[half]
                    for blk in range(4 * half, 4 * half + 4):
                        cb, ct = self.stream_w(woc[:, :, blk * 128:(blk + 1) * 128], 1024, key=("wo", blk))
                        wsu = self.wsem_used
                        wb = self.wbuf_used
                        wcs = self.cur_wslot
                        for p in range(8):
                            mmt = self.T(lambda e, bank=bank, p=p, t=t, blk=blk, cb=cb, wb=wb: e.matmul(
                                ps[bank][:, (blk % 4) * 128:(blk % 4 + 1) * 128],
                                sb["xs"][:, p * CH + t * 128: p * CH + (t + 1) * 128],
                                wb[:, cb + p * 128: cb + (p + 1) * 128],
                                start=(p == 0), stop=(p == 7), skip_group_check=True),
                                waits=([(wsu, ct)] + sp_done +
                                       (self.bankfree[bank] if blk % 4 == 0 else [])) if p == 0 else (),
                                inc=(p == 7))
                        self.cur_consumed = wcs; self.wdone(cb, mmt)
                    c0 = half * 512
                    t1 = self.V(lambda e, bank=bank, t=t, c0=c0: e.tensor_tensor(
                        sb["regB"][:, 4096 + c0:4096 + c0 + 512], ps[bank][:],
                        sb["gtm"][:, t * 1024 + c0: t * 1024 + c0 + 512], op=ALU.mult),
                        waits=[("t", mmt)] + g_done + (self.free["otm"] if half == 0 else []),
                        inc=True)
                    t2 = self.V(lambda e, t=t, c0=c0: e.tensor_tensor(
                        sb["regB"][:, 4096 + c0:4096 + c0 + 512], sb["regB"][:, 4096 + c0:4096 + c0 + 512],
                        sb["xt"][:, (t % 2) * 1024 + c0: (t % 2) * 1024 + c0 + 512], op=ALU.add),
                        waits=[(("dxA", "dxB")[t % 2], dx), ("v", t1)], inc=True)
                    self.bankfree[bank] = [("v", t2)]
                gs = self.G(lambda e, t=t, base=base: e.dma_start(
                    out=dr["yout"][base + t * 128: base + (t + 1) * 128, :],
                    in_=sb["regB"][:, 4096:5120]),
                    waits=[("v", self.tick("v"))], sem="dgo")
                st_dg.append(gs)
                self.xt_free[t % 2] = [("v", self.tick("v"))]
                self.free["otm"] = [("dgo", gs)]
            self.free["xs"] = [("t", self.tick("t"))]
            self.free["gtm"] = [("v", self.tick("v"))]
            self.free["regB"] = [("v", self.tick("v"))]
            self.free["xnb"] = [("t", self.tick("t"))]
            self.free["y"] = [("t", self.tick("t"))]
            self.free["qp"] = [("t", self.tick("t"))]
        self.out_done = [("dgo", self.tick("dgo"))]

    TAPS = {
        "y": ("y", F32), "xnb": ("xnb", BF16), "qp": ("qp", BF16), "kp": ("kp", BF16),
        "xs": ("xs", BF16), "vaug": ("vaug", BF16), "afm": ("afm", BF16),
        "kv": ("kv", F32), "ptsb": ("ptsb", BF16), "gtm": ("gtm", F32),
        "stats": ("stats", F32), "xt": ("xt", F32), "gated": ("afm", BF16),
        "small": ("small", F32), "otm": ("otm", F32), "kvb2": ("kvb2", BF16),
    }

    def emit_taps(self):
        names = self.cfg.get("taps", ())
        if not names:
            return
        sb = self.sb
        fw = []
        vq = self.V(lambda e: e.tensor_copy(sb["stats"][:, 0:1], sb["stats"][:, 1:2]), inc=True)
        sq = self.S(lambda e: e.copy(sb["stats"][:, 2:3], sb["stats"][:, 3:4]), inc=True)
        fw = [("v", vq), ("s", sq), ("t", self.tick("t"))]
        caps = {"afm": 8 * CH, "gtm": 4096, "gel": 1024, "gated": HT * CH}
        for n in names:
            src, dt = self.TAPS[n]
            hh = self.sb[src]
            shape = list(hh.shape)
            if n in caps:
                shape = [shape[0], caps[n]]
            tap = self.nc.dram_tensor("tap_" + n, shape, dt, kind="ExternalOutput").ap()
            self.G(lambda e, tap=tap, hh=hh, shape=shape: e.dma_start(
                out=tap[:, :], in_=hh[0:shape[0], 0:shape[1]]),
                   waits=fw, sem="dgo")

    def emit_phases(self):
        self.y_state = []
        self.kv_tick = 0
        nl = self.cfg.get("nlayers", NL)
        for s in range(self.cfg.get("nseq", NSEQ)):
            self.seq_init(s)
            for c in range(self.cfg.get("nchunks", NCH)):
                self.phase1(s, c)
                self.y_state = list(self.y_ready)
                for l in range(nl):
                    qd, kd, vd = self.attn(s, c, l)
                    self.attn_core(s, c, l, qd, kd, vd)
                    if not self.cfg.get("attn_only", False):
                        self.ffn(s, c, l)
                if self.cfg.get("output", True):
                    self.chunk_end(s, c)
            if self.cfg.get("output", True):
                self.output(s)

    def build(self):
        self.declare_dram()
        with self.es:
            self.alloc()
            self.init_stream()
            self.prologue()
            saved_q = {k: list(v) for k, v in self.P.q.items()}
            saved_semc = dict(self.P.semc)
            saved_pending = {k: list(v) for k, v in getattr(self.P, "pending", {e: [] for e in self.P.ENGINES}).items()}
            saved_free = {k: list(v) for k, v in self.free.items()}
            saved_bank = {k: list(v) for k, v in self.bankfree.items()}
            self.collect = True
            self.emit_phases()
            self.collect = False
            self.P.q = saved_q
            self.P.semc = saved_semc
            self.P.pending = saved_pending
            self.free = saved_free
            self.bankfree = saved_bank
            self.xt_free = [[], []]
            self.xnt_free = [[], []]
            self.wslot = 0
            self.wfree = [[] for _ in range(6)]
            self.wconv_waited = [False] * 6
            self.emit_weight_cache()
            self.emit_phases()
            self.emit_taps()
            final = self.tick("dgo")
            if final:
                self.P.op("sync", lambda e: None, waits=[("dgo", final)])
            block = self.es.enter_context(self.nc.Block())
            self.P.finalize(block)
        return self.nc


def build_core(cfg):
    core = Core(cfg)
    nc = core.build()
    return nc


_NCORES = 8


def kernel(**inputs):
    from concourse.bass_utils import run_bass_kernel_spmd
    x = np.ascontiguousarray(np.asarray(inputs["L0_attn_x"], dtype=np.float32))
    B = x.shape[0]
    consts = make_consts()
    w = make_weights({k: np.asarray(v) for k, v in inputs.items()})
    base = {}
    for k in ("wqT", "wkT", "wvT", "woT", "wqkv", "wo_i", "w1", "w2",
              "b1", "b2", "lng", "lnb", "qs", "ks"):
        base[k] = w[k]
    for k in ("cs8", "ss8", "cik", "sik"):
        base[k] = consts[k]
    base["masks"] = np.ascontiguousarray(
        np.concatenate([consts["mask_n0"], consts["mask_n1"], consts["mask_f0"], consts["mask_f1"]], 1))
    base["identb"] = consts["ident_b"]
    base["identf"] = consts["ident_f"]
    base["rmat"] = consts["rmat"]
    base["l2ones"] = consts["l2ones"]
    base["l2pat"] = consts["l2pat"]
    base["onesb"] = consts["ones"]
    cfg = {"nseq": NSEQ, "nchunks": NCH, "nlayers": NL, "output": True, "taps": ()}
    nc = build_core(cfg)
    in_maps = []
    per = B // _NCORES
    for ci in range(_NCORES):
        m = dict(base)
        m["xin"] = np.ascontiguousarray(x[ci * per:(ci + 1) * per].reshape(per * S, D))
        in_maps.append(m)
    res = run_bass_kernel_spmd(nc, in_maps, core_ids=list(range(_NCORES)))
    out = np.empty((B, S, D), dtype=np.float32)
    for ci in range(_NCORES):
        out[ci * per:(ci + 1) * per] = np.asarray(
            res.results[ci]["yout"], dtype=np.float32).reshape(per, S, D)
    return out
